# version 13
# baseline (speedup 1.0000x reference)
import contextlib
import numpy as np
import ml_dtypes
import concourse.bass as bass
import concourse.mybir as mybir
from concourse.bass_utils import run_bass_kernel_spmd

F32 = mybir.dt.float32
BF16 = mybir.dt.bfloat16
AF = mybir.ActivationFunctionType
ALU = mybir.AluOpType
AX = mybir.AxisListType
BF = ml_dtypes.bfloat16

D = 1024
T = 2048
NT = T // 128
NF = D // 128
NH = 16
HD = 64
GSCALE = HD ** -0.5
LH = 8
WIN = 512
NW = T // WIN
HID = 2730
HIDP = 2816
NHC = HIDP // 128
QK_SCALE = 8.0
MAXSTRIDE = 8

ENGINES = ("sync", "tensor", "vector", "scalar", "gpsimd")
SHUF_SWAP = [i ^ 1 for i in range(32)]


def _pair_perm():
    pp = np.empty(HD, np.int64)
    pp[0::2] = np.arange(32)
    pp[1::2] = np.arange(32) + 32
    return pp


def host_constants():
    c = {}
    c["ident_f32"] = np.eye(128, dtype=np.float32)
    c["ident_bf"] = np.eye(128, dtype=np.float32).astype(BF)
    j = np.arange(32, dtype=np.float64)
    inv = 1.0 / (10000.0 ** (j / 32.0))
    t = np.arange(T, dtype=np.float64)
    ang = t[None, :] * inv[:, None]
    cos = np.cos(ang); sin = np.sin(ang)
    cos64 = np.empty((64, T)); sin64 = np.empty((64, T))
    cos64[0::2] = cos; cos64[1::2] = cos
    sin64[0::2] = -sin; sin64[1::2] = sin
    c["g_cos"] = np.tile(cos64, (2, 1)).astype(np.float32)
    c["g_sin"] = np.tile(sin64, (2, 1)).astype(np.float32)

    half_idx = np.arange(0, HD, 2, dtype=np.float64)
    invf = 1.0 / (10000.0 ** (half_idx / HD))
    angl = t[None, :] * invf[:, None]
    cosl = np.cos(angl); sinl = np.sin(angl)
    base = (half_idx + 0.4 * HD) / (1.4 * HD)
    power = (t - T // 2) / float(WIN // 2)
    scale = base[:, None] ** power[None, :]

    def inter(even, odd):
        out = np.empty((64, T))
        out[0::2] = even; out[1::2] = odd
        return np.tile(out, (2, 1)).astype(np.float32)

    aq = cosl * scale * QK_SCALE; sq = sinl * scale * QK_SCALE
    ak = cosl / scale; sk = sinl / scale
    c["x_aq"] = inter(aq, aq)
    c["x_sq"] = inter(-sq, sq)
    c["x_ak"] = inter(ak, ak)
    c["x_sk"] = inter(-sk, sk)

    kk = np.arange(128)[:, None]; qq = np.arange(128)[None, :]
    c["tri_u"] = (qq >= kk).astype(np.float32).astype(BF)
    c["tri_l"] = (qq <= kk).astype(np.float32).astype(BF)
    return c


def prep_weights(inp):
    w = {}
    perm = _pair_perm()
    gperm = np.concatenate([perm + 64 * h for h in range(NH)])
    w["wq"] = np.ascontiguousarray(np.asarray(inp["L0_attn_wq_w"])[:, gperm])
    w["wk"] = np.ascontiguousarray(np.asarray(inp["L0_attn_wk_w"])[:, gperm])
    w["wv"] = np.ascontiguousarray(np.asarray(inp["L0_attn_wv_w"]))
    w["wo"] = np.ascontiguousarray(np.asarray(inp["L0_attn_wo_w"]))
    w["bq"] = np.ascontiguousarray(np.asarray(inp["L0_attn_wq_b"])[gperm].reshape(1, D))
    w["bk"] = np.ascontiguousarray(np.asarray(inp["L0_attn_wk_b"])[gperm].reshape(1, D))
    w["bv"] = np.ascontiguousarray(np.asarray(inp["L0_attn_wv_b"]).reshape(1, D))
    w["bo_c"] = np.ascontiguousarray(
        np.asarray(inp["L0_attn_wo_b"]).reshape(NF, 128).T)

    lperm = np.concatenate([perm + 64 * h for h in range(LH)])
    for l in range(2):
        wqkv = np.asarray(inp["L0_ffn_wqkv"][l])
        w[f"l{l}_wqk"] = np.ascontiguousarray(
            np.concatenate([wqkv[:, 0:512][:, lperm], wqkv[:, 512:1024][:, lperm]], axis=1))
        w[f"l{l}_wv"] = np.ascontiguousarray(wqkv[:, 1024:1536])
        w[f"l{l}_qs"] = np.ascontiguousarray(
            np.tile(np.asarray(inp["L0_ffn_q_scale"][l])[perm], 2).reshape(128, 1))
        w[f"l{l}_ks"] = np.ascontiguousarray(
            np.tile(np.asarray(inp["L0_ffn_k_scale"][l])[perm], 2).reshape(128, 1))
        w[f"l{l}_wo"] = np.ascontiguousarray(np.asarray(inp["L0_ffn_wo"][l]))
        w[f"l{l}_lng"] = np.ascontiguousarray(np.asarray(inp["L0_ffn_ln_g"][l]).reshape(NF, 128).T)
        w[f"l{l}_lnb"] = np.ascontiguousarray(np.asarray(inp["L0_ffn_ln_b"][l]).reshape(NF, 128).T)
        w1 = np.asarray(inp["L0_ffn_w1"][l]); b1 = np.asarray(inp["L0_ffn_b1"][l])
        a_w = np.zeros((D, HIDP), np.float32); a_w[:, :HID] = w1[:, :HID]
        g_w = np.zeros((D, HIDP), np.float32); g_w[:, :HID] = w1[:, HID:]
        a_b = np.zeros(HIDP, np.float32); a_b[:HID] = b1[:HID]
        g_b = np.zeros(HIDP, np.float32); g_b[:HID] = b1[HID:]
        w1p = np.empty((D, 2 * HIDP), np.float32)
        b1p = np.empty((1, 2 * HIDP), np.float32)
        for cc in range(NHC):
            s = slice(cc * 128, (cc + 1) * 128)
            w1p[:, (2 * cc) * 128:(2 * cc + 1) * 128] = a_w[:, s]
            w1p[:, (2 * cc + 1) * 128:(2 * cc + 2) * 128] = g_w[:, s]
            b1p[0, (2 * cc) * 128:(2 * cc + 1) * 128] = a_b[s]
            b1p[0, (2 * cc + 1) * 128:(2 * cc + 2) * 128] = g_b[s]
        w[f"l{l}_w1"] = w1p
        w[f"l{l}_b1c"] = np.ascontiguousarray(b1p.reshape(2 * NHC, 128).T)
        w2p = np.zeros((HIDP, D), np.float32)
        w2p[:HID] = np.asarray(inp["L0_ffn_w2"][l])
        w[f"l{l}_w2"] = w2p
        w[f"l{l}_b2c"] = np.ascontiguousarray(
            np.asarray(inp["L0_ffn_b2"][l]).reshape(NF, 128).T)
    return w


def _flat(waits):
    out = []
    for wv in waits:
        if wv is None:
            continue
        if isinstance(wv, (list, tuple)) and (len(wv) == 0 or not isinstance(wv[0], str)):
            out.extend(x for x in _flat(wv))
        else:
            out.append(wv)
    return out


class Prog:
    def __init__(self, nc, es):
        self.nc = nc
        self.es = es
        self.ops = {e: [] for e in ENGINES}
        self.sems = {}
        self.counts = {}
        self.last_wait = {e: {} for e in ENGINES}
        self.prev_op = {}
        for name in ("T", "V", "S", "G"):
            self.sems[name] = es.enter_context(nc.semaphore(name))
            self.counts[name] = 0

    def stream(self, name):
        if name not in self.sems:
            self.sems[name] = self.es.enter_context(self.nc.semaphore(name))
            self.counts[name] = 0
        return name

    def emit(self, eng, fn, waits=(), inc=None, n=1):
        wlist = []
        lw = self.last_wait[eng]
        allw = list(_flat(waits))
        if fn is not None and eng in ("vector", "scalar"):
            prev = self.prev_op.get(eng)
            if prev is not None:
                allw.append(prev)
        for (s, v) in allw:
            if lw.get(s, 0) >= v:
                continue
            lw[s] = v
            wlist.append((s, v))
        tok = None
        if inc is not None:
            self.counts[inc] += n
            tok = (inc, self.counts[inc])
            if fn is not None:
                self.prev_op[eng] = tok
        self.ops[eng].append((fn, tuple(wlist), (inc, n) if inc else None))
        return tok

    def barrier(self):
        toks = [(s, v) for s, v in self.counts.items() if v > 0]
        for e in ENGINES:
            self.emit(e, None, waits=toks)

    def now(self, sem):
        return (sem, self.counts[sem])

    def install(self, block):
        for ename in ENGINES:
            ops = self.ops[ename]
            sems = self.sems

            def body(eng, ops=ops):
                for fn, waits, inc in ops:
                    for (s, v) in waits:
                        eng.wait_ge(sems[s], v)
                    inst = fn(eng) if fn is not None else None
                    if inc is not None:
                        s, n = inc
                        assert inst is not None
                        inst.then_inc(sems[s], n)
            getattr(block, ename)(body)


class Arena:
    def __init__(self, handle, total_bytes):
        self.base = handle.ap()
        self.free_list = [(0, total_bytes)]
        self.live = {}

    def alloc(self, name, nbytes, align=128):
        assert name not in self.live
        for i, (off, sz) in enumerate(self.free_list):
            aoff = (off + align - 1) // align * align
            pad = aoff - off
            if sz >= nbytes + pad:
                pieces = []
                if pad:
                    pieces.append((off, pad))
                rest = sz - nbytes - pad
                if rest:
                    pieces.append((aoff + nbytes, rest))
                self.free_list[i:i + 1] = pieces
                self.live[name] = (aoff, nbytes)
                return
        raise MemoryError(f"arena OOM: {name} needs {nbytes}; free={self.free_list}")

    def free(self, name):
        off, sz = self.live.pop(name)
        fl = sorted(self.free_list + [(off, sz)])
        merged = []
        for o, s in fl:
            if merged and merged[-1][0] + merged[-1][1] == o:
                merged[-1] = (merged[-1][0], merged[-1][1] + s)
            else:
                merged.append((o, s))
        self.free_list = merged

    def view(self, name, part0, nparts, elem0, nelems, dtype):
        off, sz = self.live[name]
        esz = 4 if dtype == F32 else 2
        assert (elem0 + nelems) * esz <= sz, (name, elem0, nelems, sz)
        bb = self.base.bitcast(dtype)
        st = off // esz
        return bb[part0:part0 + nparts, st + elem0: st + elem0 + nelems]


def build_nc(upto="all", debug=()):
    nc = bass.Bass("TRN2", target_bir_lowering=False)
    LEVELS = {"p0": 0, "p1": 1, "attn": 2, "ln1": 3, "ffn0": 4, "ffn1": 5, "all": 6}
    LVL = LEVELS[upto]
    dbgset = dict(debug)

    X = nc.dram_tensor("x", [T, D], F32, kind="ExternalInput").ap()
    OUT = nc.dram_tensor("out", [T, D], F32, kind="ExternalOutput").ap()
    dins = {}

    def din(name, shape, dt=F32):
        dins[name] = nc.dram_tensor(name, list(shape), dt, kind="ExternalInput").ap()
    for nm, shp in [("wq", (D, D)), ("wk", (D, D)), ("wv", (D, D)), ("wo", (D, D)),
                    ("bq", (1, D)), ("bk", (1, D)), ("bv", (1, D)),
                    ("bo_c", (128, NF)),
                    ("ident_f32", (128, 128)), ("g_cos", (128, T)), ("g_sin", (128, T)),
                    ("x_aq", (128, T)), ("x_sq", (128, T)), ("x_ak", (128, T)),
                    ("x_sk", (128, T))]:
        din(nm, shp)
    din("ident_bf", (128, 128), BF16)
    din("tri_u", (128, 128), BF16)
    din("tri_l", (128, 128), BF16)
    for l in range(2):
        din(f"l{l}_wqk", (D, D)); din(f"l{l}_wv", (D, 512))
        din(f"l{l}_qs", (128, 1)); din(f"l{l}_ks", (128, 1))
        din(f"l{l}_wo", (512, D))
        din(f"l{l}_lng", (128, NF)); din(f"l{l}_lnb", (128, NF))
        din(f"l{l}_w1", (D, 2 * HIDP)); din(f"l{l}_b1c", (128, 2 * NHC))
        din(f"l{l}_w2", (HIDP, D)); din(f"l{l}_b2c", (128, NF))

    spill_x = nc.dram_tensor("spill_x", [NF, 128, T], F32, kind="Internal").ap()
    spill_x2 = nc.dram_tensor("spill_x2", [NF, 128, T], F32, kind="Internal").ap()
    spill_xf = nc.dram_tensor("spill_xf", [NF, 128, T], F32, kind="Internal").ap()
    dbg = {}
    for name, spec in dbgset.items():
        shape, ddt = (spec, F32) if not (isinstance(spec[-1], str)) else (spec[0], BF16)
        if isinstance(spec, tuple) and len(spec) == 2 and isinstance(spec[1], str):
            shape, ddt = spec[0], (BF16 if spec[1] == "bf16" else F32)
        dbg[name] = nc.dram_tensor(name, list(shape), ddt, kind="ExternalOutput").ap()

    es = contextlib.ExitStack()
    with es:
        ARENA_BYTES = 180 * 1024
        arena_h = es.enter_context(nc.sbuf_tensor("ARENA", [128, ARENA_BYTES // 4], F32))
        ar = Arena(arena_h, ARENA_BYTES)

        def sb_(name, shape, dt=F32):
            return es.enter_context(nc.sbuf_tensor(name, shape, dt))
        ident_f = sb_("ident_f", [128, 128])
        ident_b = sb_("ident_b", [128, 128], BF16)
        tri_u = sb_("tri_u_s", [128, 128], BF16)
        tri_l = sb_("tri_l_s", [128, 128], BF16)
        ones_r = sb_("ones_r", [65, 512])
        ones_cb = sb_("ones_cb", [128, 1], BF16)
        eps_ln = sb_("eps_ln", [1, 1])
        brow = sb_("brow", [1, D], BF16)
        brow2 = sb_("brow2", [1, D], BF16)
        ones_rb = sb_("ones_rb", [1, 512], BF16)
        b1c_s = sb_("b1c_s", [128, 2 * NHC])
        b2c_s = sb_("b2c_s", [128, NF])
        bo_c_s = sb_("bo_c_s", [128, NF])
        qs_c = sb_("qs_c", [128, 1])
        ks_c = sb_("ks_c", [128, 1])
        lng_c = sb_("lng_c", [128, NF])
        lnb_c = sb_("lnb_c", [128, NF])
        m_f = sb_("m_f", [128, 2])
        m_bf = sb_("m_bf", [128, 2], BF16)
        rowbuf = sb_("rowbuf", [128, T])
        rowb2 = sb_("rowb2", [1, T])

        PGA = es.enter_context(nc.psum_tensor("PGA", [128, T], F32))
        PGB = es.enter_context(nc.psum_tensor("PGB", [128, T], F32))

        p = Prog(nc, es)
        p.sems["STX"] = es.enter_context(nc.semaphore("STORE"))
        p.counts["STX"] = 0
        taps = {}

        def dma(eng, out, in_, waits=(), stream=None):
            assert stream is not None
            p.stream(stream)
            return p.emit(eng, lambda e, o=out, i=in_: e.dma_start(out=o, in_=i),
                          waits=waits, inc=stream, n=16)

        def mm(out, lhsT, rhs, start, stop, waits=(), inc="T", tpos=None):
            return p.emit("tensor",
                          lambda e, o=out, lt=lhsT, r=rhs, s1=start, s2=stop, tp=tpos:
                          e.matmul(o, lt, r, start=s1, stop=s2, tile_position=tp),
                          waits=waits, inc=inc)

        def vop(fn, waits=(), inc="V"):
            return p.emit("vector", fn, waits=waits, inc=inc)

        def sop(fn, waits=(), inc="S"):
            return p.emit("scalar", fn, waits=waits, inc=inc)

        def gop(fn, waits=(), inc="G"):
            return p.emit("gpsimd", fn, waits=waits, inc=inc)

        def tap(name, ap_list, waits):
            if name not in dbgset:
                return
            dr = dbg[name]
            if ap_list == "dram_xf":
                for i in range(NF):
                    p.emit("sync", lambda e, o=dr[i], i2=spill_xf[i]:
                           e.dma_start(out=o, in_=i2), waits=waits, inc="STX", n=16)
            else:
                for i, ap in enumerate(ap_list):
                    p.emit("sync", lambda e, o=dr[i], i2=ap: e.dma_start(out=o, in_=i2),
                           waits=waits, inc="STX", n=16)
            p.emit("sync", None, waits=[("STX", p.counts["STX"])])
            p.barrier()


        tok_c = {}
        for nm, dst in [("ident_f32", ident_f), ("ident_bf", ident_b),
                        ("tri_u", tri_u), ("tri_l", tri_l)]:
            tok_c[nm] = dma("sync", dst[:], dins[nm], stream="cst")
        vop(lambda e: e.memset(eps_ln[:], 1e-5))
        tok_ones = vop(lambda e: e.memset(ones_r[:], 1.0))
        tok_onesb = vop(lambda e: e.memset(ones_rb[:], 1.0))
        tok_ones_cb = vop(lambda e: e.memset(ones_cb[:], 1.0))

        ar.alloc("XR", 64 * 1024)
        ar.alloc("XBF", 32 * 1024)
        ar.alloc("CC", 8 * 1024)
        ar.alloc("CS", 8 * 1024)
        ar.alloc("G0", 8 * 1024)
        ar.alloc("G1", 8 * 1024)
        ar.alloc("XTS", 8 * 1024)
        ar.alloc("QKS", 8 * 1024)

        XR = [ar.view("XR", 0, 128, c * T, T, F32) for c in range(NF)]
        XBF = [ar.view("XBF", 0, 128, c * T, T, BF16) for c in range(NF)]
        CCOS = ar.view("CC", 0, 128, 0, T, F32)
        CSIN = ar.view("CS", 0, 128, 0, T, F32)
        G0 = ar.view("G0", 0, 128, 0, T, F32)
        G1 = ar.view("G1", 0, 128, 0, T, F32)
        XT = [ar.view("XTS", 0, 128, i * D, D, F32) for i in range(2)]

        tok_c["cos"] = dma("sync", CCOS, dins["g_cos"], stream="cst")
        tok_c["sin"] = dma("sync", CSIN, dins["g_sin"], stream="cst")
        for _k in list(tok_c):
            tok_c[_k] = ("cst", p.counts["cst"])

        def wslice(gw, fc, o0, no):
            return gw[:, fc * D + o0: fc * D + o0 + no]

        ev_tok = [None] * NT
        bankfree = [None] * 4
        tv = ts = None
        for i in range(NT):
            b = i % 2
            tl = dma("sync", XT[b], X[i * 128:(i + 1) * 128, :],
                     waits=[ev_tok[i - 2] if i >= 2 else None], stream=f"xt{b}")
            for fc in range(NF):
                bank = (i * NF + fc) % 4
                ps = PGA[:, bank * 512: bank * 512 + 128]
                tmm = mm(ps, XT[b][:, fc * 128:(fc + 1) * 128], ident_f[:], True, True,
                         waits=[tl, tok_c["ident_f32"], bankfree[bank]])
                tv = vop(lambda e, ps=ps, d=XR[fc], c0=i * 128:
                         e.tensor_copy(d[:, c0:c0 + 128], ps), waits=[tmm])
                ts = sop(lambda e, ps=ps, d=XBF[fc], c0=i * 128:
                         e.copy(d[:, c0:c0 + 128], ps), waits=[tmm, tv])
                bankfree[bank] = [tv, ts]
            ev_tok[i] = [tv, ts]
        t_xdone = [tv, ts]

        for fc in range(NF):
            dma("sync", spill_x[fc], XR[fc], waits=[t_xdone], stream="spx")
        tok_xspill = ("spx", p.counts["spx"])

        if LVL == 0:
            tap("XR", XR, [t_xdone])
            tap("XBF", XBF, [t_xdone])

        p.barrier()

        if LVL >= 1:
            ar.free("XR")
            ar.free("XTS")
            ar.alloc("QT", 32 * 1024)
            ar.alloc("KT", 32 * 1024)
            QT = [ar.view("QT", 0, 128, c * T, T, BF16) for c in range(NF)]
            KT = [ar.view("KT", 0, 128, c * T, T, BF16) for c in range(NF)]
            QKS = [ar.view("QKS", 0, 128, i * D, D, BF16) for i in range(4)]

            def load_colblock(dram, col0, slot, waits=()):
                src = dram[:, col0:col0 + 128].rearrange("(c p) m -> p c m", p=128)
                dst = QKS[slot].rearrange("p (c m) -> p c m", c=NF)
                return dma("gpsimd", dst, src, waits=waits, stream=f"qks{slot}")

            slot_state = {"i": 0, "cons": [None, None, None, None]}

            def qk_proj(dst, wdram, browx, tok_b, grp_free):
                outs = []
                t2 = None
                for oc in range(NF):
                    sl = slot_state["i"] % 4
                    slot_state["i"] += 1
                    tok_w = load_colblock(wdram, oc * 128, sl,
                                          waits=[slot_state["cons"][sl]])
                    g = oc % 2
                    PG = PGA if g == 0 else PGB
                    tmm = None
                    for tc in range(4):
                        mm(PG[:, tc * 512:(tc + 1) * 512], browx[:, oc * 128:(oc + 1) * 128],
                           ones_rb[:], True, False,
                           waits=[tok_b, tok_w, grp_free[g], tok_onesb])
                    for fc in range(NF):
                        for tc in range(4):
                            tmm = mm(PG[:, tc * 512:(tc + 1) * 512],
                                     QKS[sl][:, fc * 128:(fc + 1) * 128],
                                     XBF[fc][:, tc * 512:(tc + 1) * 512],
                                     False, fc == NF - 1)
                    slot_state["cons"][sl] = tmm
                    t1 = vop(lambda e, PG=PG: e.tensor_mul(G0[:], PG[:], CCOS),
                             waits=[tmm, tok_c["cos"]])
                    t2 = vop(lambda e, PG=PG: e.stream_shuffle(G1[:], PG[:], SHUF_SWAP))
                    vop(lambda e: e.tensor_mul(G1[:], G1[:], CSIN), waits=[tok_c["sin"]])
                    t4 = vop(lambda e, d=dst[oc]: e.tensor_add(d[:], G0[:], G1[:]))
                    grp_free[g] = t2
                    outs.append(t4)
                return outs, grp_free

            tok_bq2 = dma("gpsimd", brow2[:], dins["bq"], stream="brow2")
            tok_qt, gfree = qk_proj(QT, dins["wq"], brow2, tok_bq2, [None, None])
            tok_bk2 = dma("gpsimd", brow2[:], dins["bk"], waits=[p.now("T")], stream="brow2")
            tok_kt, gfree = qk_proj(KT, dins["wk"], brow2, tok_bk2, gfree)
            tok_rope_done = p.now("V")

            ar.free("CC")
            ar.free("CS")
            ar.free("QKS")
            ar.alloc("GW", 16 * 1024)
            for i in range(NT):
                ar.alloc(f"V{i}", 1040 * 2)
            VT = [ar.view(f"V{i}", 0, 128, 0, 1040, BF16) for i in range(NT)]
            GWv = ar.view("GW", 0, 128, 0, NF * D, BF16)

            tok_wv = dma("gpsimd", GWv.rearrange("p (c m) -> p c m", c=NF),
                         dins["wv"].rearrange("(c p) m -> p c m", p=128),
                         waits=[tok_rope_done, p.now("T")], stream="gwv")
            tok_bv2 = dma("gpsimd", brow[:], dins["bv"], stream="brow")
            tok_vones = []
            for i in range(NT):
                tok_vones.append(vop(lambda e, v=VT[i]: e.memset(v[:, 64:1040:65], 1.0),
                                     waits=[tok_rope_done]))

            vproj_done = []
            half_free = [gfree[0], gfree[1]]
            tmm = None
            for i in range(NT):
                hh = i % 2
                ps = PGA[:, hh * 1024:(hh + 1) * 1024] if hh == 0 else PGB[:, 0:1024]
                mm(ps[:, 0:512], ones_rb[0:1, 0:128], brow[:, 0:512], True, False,
                   waits=[tok_bv2, half_free[hh], tok_wv, tok_ones])
                mm(ps[:, 512:1024], ones_rb[0:1, 0:128], brow[:, 512:1024], True, False)
                for fc in range(NF):
                    mm(ps[:, 0:512], XBF[fc][:, i * 128:(i + 1) * 128],
                       wslice(GWv, fc, 0, 512), False, fc == NF - 1)
                    tmm = mm(ps[:, 512:1024], XBF[fc][:, i * 128:(i + 1) * 128],
                             wslice(GWv, fc, 512, 512), False, fc == NF - 1)
                tev = vop(lambda e, ps=ps, v=VT[i]:
                          e.tensor_copy(v[:, 0:1040].rearrange("p (h c) -> p h c", h=16)[:, :, 0:64],
                                        ps.rearrange("p (h c) -> p h c", h=16)),
                          waits=[tmm, tok_vones[i]])
                half_free[hh] = tev
                vproj_done.append(tev)
            ar.free("GW")

            tap("QT", QT, tok_qt)
            tap("KT", KT, tok_kt)
            tap("VT", VT, vproj_done)

            if LVL >= 2:
                p.barrier()
                ar.free("XBF")

                ar.alloc("PT", 8 * 1024)
                ar.alloc("EXT", 8 * 1024)
                ar.alloc("EXT2", 8 * 1024)
                ar.alloc("SQG", 8 * 1024)
                for c in range(NF):
                    ar.alloc(f"OT{c}", 4 * 1024)
                PT = [ar.view("PT", 0, 128, i * T, T, BF16) for i in range(2)]
                QX = ar.view("EXT", 0, 65, 0, T, BF16)
                KX = ar.view("EXT", 0, 65, T, T, BF16)
                OT = [ar.view(f"OT{c}", 0, 128, 0, T, BF16) for c in range(NF)]
                ROW_L = [rowbuf[0:1, :], rowb2[0:1, :]]
                GB64 = [ar.view("G0", 0, 64, 0, T, F32), ar.view("G1", 0, 64, 0, T, F32)]
                QX2 = [QX, ar.view("EXT2", 0, 65, 0, T, BF16)]
                KX2 = [KX, ar.view("EXT2", 0, 65, T, T, BF16)]
                PT2 = [[PT[0][:, 0:1024], PT[0][:, 1024:2048]],
                       [PT[1][:, 0:1024], PT[1][:, 1024:2048]]]
                SQG = [ar.view("SQG", 0, 128, i * T, T, BF16) for i in range(2)]
                tok_kxones = [vop(lambda e: e.memset(KX2[0][64:65, :], 1.0)),
                              vop(lambda e: e.memset(KX2[1][64:65, :], 1.0))]

                ot_done = []
                lane_state = [
                    dict(prev_norm=None, prev_bcopy=None, prev_exp=None,
                         pt_av=[None, None], sq_cons=None),
                    dict(prev_norm=None, prev_bcopy=None, prev_exp=None,
                         pt_av=[None, None], sq_cons=None),
                ]

                def gen_head(h, lane):
                    st = lane_state[lane]
                    PG = PGA if lane == 0 else PGB
                    ch, sub = h // 2, (h % 2) * 64
                    QXl, KXl = QX2[lane], KX2[lane]
                    tq = vop(lambda e: e.tensor_copy(QXl[0:64, :], QT[ch][sub:sub + 64, :]),
                             waits=[tok_qt[ch], st["prev_exp"]])
                    tk = vop(lambda e: e.tensor_copy(KXl[0:64, :], KT[ch][sub:sub + 64, :]),
                             waits=[tok_kt[ch], tok_kxones[lane], st["prev_exp"]])
                    sqg = SQG[lane]
                    tsq = sop(lambda e: e.activation(sqg[sub:sub + 64, :],
                                                     QT[ch][sub:sub + 64, :], AF.Square),
                              waits=[tok_qt[ch], st["sq_cons"]])
                    ssq = []
                    for tc in range(4):
                        ssq.append(mm(PG[0:1, tc * 512:(tc + 1) * 512],
                                      ones_cb[sub:sub + 64, :],
                                      sqg[sub:sub + 64, tc * 512:(tc + 1) * 512],
                                      True, True,
                                      waits=[tsq, tok_ones_cb,
                                             st["prev_bcopy"], st["prev_norm"]],
                                      tpos=(sub, 0)))
                    yield
                    tsk = sop(lambda e: e.activation(sqg[sub:sub + 64, :],
                                                     KT[ch][sub:sub + 64, :], AF.Square),
                              waits=[tok_kt[ch]] + ssq)
                    ssk = []
                    for tc in range(4):
                        ssk.append(mm(PG[32:33, tc * 512:(tc + 1) * 512],
                                      ones_cb[sub:sub + 64, :],
                                      sqg[sub:sub + 64, tc * 512:(tc + 1) * 512],
                                      True, True, waits=[tsk], tpos=(sub, 32)))
                    st["sq_cons"] = ssk[-1]
                    yield
                    mk2 = m_f[0:1, lane:lane + 1]
                    tmx = vop(lambda e, PG=PG, mk2=mk2:
                              e.reduce_max(mk2, PG[32:33, :], axis=AX.X), waits=ssk)
                    tsx = sop(lambda e, mk2=mk2: e.activation(mk2, mk2, AF.Sqrt),
                              waits=[tmx])
                    tnegmk = vop(lambda e, mk2=mk2: e.tensor_scalar_mul(mk2, mk2, -1.0),
                                 waits=[tsx])
                    rslot = ROW_L[lane]
                    tr1 = sop(lambda e, PG=PG, rslot=rslot:
                              e.activation(rslot, PG[0:1, :], AF.Sqrt), waits=ssq)
                    srow = sop(lambda e, rslot=rslot, mk2=mk2, QXl=QXl:
                               e.activation(QXl[64:65, :], rslot, AF.Copy, scale=mk2),
                               waits=[tr1, tnegmk, tq])
                    scal_mrow = srow
                    prev_max = [ssq[-1], ssk[-1], tr1, tmx]
                    yield
                    norm_toks = []
                    for qh in range(2):
                        q0 = qh * 1024
                        rot_free = prev_max if qh == 0 else norm_toks[-1]
                        avtoks = []
                        tav = None
                        for kt_i in range(NT):
                            ptb = PT2[kt_i % 2][lane]
                            mms = []
                            for qc in range(2):
                                tmm = mm(PG[:, qc * 512:(qc + 1) * 512],
                                         KXl[:, kt_i * 128:(kt_i + 1) * 128],
                                         QXl[:, q0 + qc * 512:q0 + (qc + 1) * 512],
                                         True, True,
                                         waits=[tk, scal_mrow, rot_free])
                                mms.append(tmm)
                            texp = sop(lambda e, ptb=ptb, PG=PG:
                                       e.activation(ptb[:], PG[:, 0:1024], AF.Exp,
                                                    scale=GSCALE),
                                       waits=mms + [st["pt_av"][kt_i % 2]])
                            rot_free = texp
                            yield
                            for qc in range(2):
                                tav = mm(PG[0:65, 1024 + qc * 512:1024 + (qc + 1) * 512],
                                         VT[kt_i][:, h * 65:(h + 1) * 65],
                                         ptb[:, qc * 512:(qc + 1) * 512],
                                         kt_i == 0, kt_i == NT - 1,
                                         waits=[texp, vproj_done[kt_i],
                                                (prev_max if qh == 0 else norm_toks[-1])
                                                if kt_i == 0 else None])
                                avtoks.append(tav)
                            st["pt_av"][kt_i % 2] = tav
                            yield
                        sop(lambda e, PG=PG, q0=q0, lane=lane:
                            e.activation(ROW_L[lane][0:1, q0:q0 + 1024],
                                         PG[64:65, 1024:2048], AF.Ln),
                            waits=avtoks[-2:])
                        trc = sop(lambda e, q0=q0, lane=lane:
                                  e.activation(ROW_L[lane][0:1, q0:q0 + 1024],
                                               ROW_L[lane][0:1, q0:q0 + 1024],
                                               AF.Exp, scale=-1.0))
                        yield
                        yield
                        tb = []
                        for qc in range(2):
                            tb.append(mm(PG[0:64, qc * 512:(qc + 1) * 512], ones_r[0:1, 0:64],
                                         ROW_L[lane][0:1, q0 + qc * 512:q0 + (qc + 1) * 512],
                                         True, True, waits=[trc, rot_free]))
                        tbc = sop(lambda e, PG=PG, lane=lane:
                                  e.copy(GB64[lane][:, 0:1024], PG[0:64, 0:1024]), waits=tb)
                        tnm = vop(lambda e, PG=PG, lane=lane, q0=q0:
                                  e.tensor_mul(OT[ch][sub:sub + 64, q0:q0 + 1024],
                                               PG[0:64, 1024:2048], GB64[lane][:, 0:1024]),
                                  waits=[tbc] + avtoks[-2:])
                        norm_toks.append([tbc, tnm])
                        yield
                    st["prev_bcopy"] = norm_toks[-1][0]
                    st["prev_norm"] = norm_toks[-1][1]
                    st["prev_exp"] = rot_free
                    ot_done.append(norm_toks[-1][1])

                active = {0: gen_head(0, 0), 1: gen_head(1, 1)}
                nxt = [2, 3]
                while active:
                    for lane in (0, 1):
                        g = active.get(lane)
                        if g is None:
                            continue
                        try:
                            next(g)
                        except StopIteration:
                            if nxt[lane] < NH:
                                active[lane] = gen_head(nxt[lane], lane)
                                nxt[lane] += 2
                            else:
                                del active[lane]

                tap("OT", OT, ot_done)

                if LVL >= 3:
                    p.barrier()
                    for i in range(NT):
                        ar.free(f"V{i}")
                    ar.free("PT")
                    ar.free("EXT")
                    ar.free("EXT2")
                    ar.free("SQG")
                    ar.free("QT")
                    ar.free("KT")
                    for c in range(NF):
                        ar.alloc(f"XRb{c}", 8 * 1024)
                    ar.alloc("WOS", 4 * 1024)
                    XR2 = [ar.view(f"XRb{c}", 0, 128, 0, T, F32) for c in range(NF)]
                    WOS = [ar.view("WOS", 0, 128, i * D, D, BF16) for i in range(2)]
                    wos_cons = [None, None]

                    tok_bo = dma("gpsimd", bo_c_s[:], dins["bo_c"], stream="boc")
                    for c in range(NF):
                        dma("sync", XR2[c], spill_x[c], waits=[tok_xspill], stream="xrl")
                    tok_xrl = [("xrl", p.counts["xrl"])] * NF

                    grp = [None, None]
                    resid_done = []
                    for oc in range(NF):
                        g = oc % 2
                        PG = PGA if g == 0 else PGB
                        sl = oc % 2
                        tok_wo = dma("gpsimd",
                                     WOS[sl].rearrange("p (c m) -> p c m", c=NF),
                                     dins["wo"][:, oc * 128:(oc + 1) * 128]
                                     .rearrange("(c p) m -> p c m", p=128),
                                     waits=[wos_cons[sl]], stream=f"wos{sl}")
                        tmm = None
                        for fc in range(NF):
                            for tc in range(4):
                                tmm = mm(PG[:, tc * 512:(tc + 1) * 512],
                                         WOS[sl][:, fc * 128:(fc + 1) * 128],
                                         OT[fc][:, tc * 512:(tc + 1) * 512],
                                         fc == 0, fc == NF - 1,
                                         waits=[ot_done[-1], tok_wo] +
                                               ([grp[g]] if fc == 0 else []))
                        wos_cons[sl] = tmm
                        tadd = vop(lambda e, c=oc, PG=PG, oc=oc:
                                   e.scalar_tensor_tensor(
                                       XR2[c][:], PG[:], bo_c_s[:, oc:oc + 1],
                                       XR2[c][:], ALU.add, ALU.add),
                                   waits=[tmm, tok_xrl[oc], tok_bo])
                        grp[g] = tadd
                        resid_done.append(tadd)
                    for c in range(NF):
                        ar.alloc(f"ZBc{c}", 4 * 1024)
                    ar.alloc("SQ", 4 * 1024)
                    ar.free("WOS")
                    for c in range(NF):
                        ar.free(f"OT{c}")
                    ZB = [ar.view(f"ZBc{c}", 0, 128, 0, T, BF16) for c in range(NF)]
                    SQ = ar.view("SQ", 0, 128, 0, T, BF16)

                    def ln_stats(xr_list, zb_list, waits_per_chunk, pg_free):
                        tok_sq_mm = None
                        zb_toks = []
                        for fc in range(NF):
                            if fc % 2 == 0:
                                tz = sop(lambda e, zf=zb_list[fc], xf=xr_list[fc]:
                                         e.copy(zf[:], xf[:]),
                                         waits=[waits_per_chunk[fc]])
                            else:
                                tz = gop(lambda e, zf=zb_list[fc], xf=xr_list[fc]:
                                         e.tensor_copy(zf[:], xf[:]),
                                         waits=[waits_per_chunk[fc]])
                            zb_toks.append(tz)
                            tsq = sop(lambda e, zf=zb_list[fc]: e.activation(SQ, zf[:], AF.Square),
                                      waits=[tz, tok_sq_mm])
                            for tc in range(4):
                                mm(PGA[0:1, tc * 512:(tc + 1) * 512], ones_cb[:],
                                   zb_list[fc][:, tc * 512:(tc + 1) * 512],
                                   fc == 0, fc == NF - 1,
                                   waits=[tz, tok_ones_cb, pg_free if fc == 0 else None])
                                tok_sq_mm = mm(PGA[32:33, tc * 512:(tc + 1) * 512], ones_cb[:],
                                               SQ[:, tc * 512:(tc + 1) * 512],
                                               fc == 0, fc == NF - 1, waits=[tsq],
                                               tpos=(0, 32))
                        return tok_sq_mm, zb_toks

                    tok_stat, _ = ln_stats(XR2, ZB, resid_done, grp[0])

                    s_mu = sop(lambda e: e.activation(rowbuf[0:1, :], PGA[0:1, :], AF.Copy,
                                                      scale=1.0 / D), waits=[tok_stat])
                    s_e2 = sop(lambda e: e.activation(G1[0:1, :], PGA[32:33, :], AF.Copy,
                                                      scale=1.0 / D))
                    s_m2 = sop(lambda e: e.activation(G0[0:1, :], rowbuf[0:1, :], AF.Square))
                    v_var = vop(lambda e: e.tensor_sub(G1[0:1, :], G1[0:1, :],
                                                       G0[0:1, :]), waits=[s_m2, s_e2])
                    s_ln = sop(lambda e: e.activation(G0[0:1, :], G1[0:1, :], AF.Ln,
                                                      bias=eps_ln[:]), waits=[v_var])
                    s_rstd = sop(lambda e: e.activation(G1[0:1, :], G0[0:1, :], AF.Exp,
                                                        scale=-0.5))

                    tmb = []
                    for tc in range(4):
                        tmb.append(mm(PGA[:, tc * 512:(tc + 1) * 512], ones_r[0:1, 0:128],
                                      rowbuf[0:1, tc * 512:(tc + 1) * 512], True, True,
                                      waits=[s_e2, s_mu]))
                    trb = []
                    for tc in range(4):
                        trb.append(mm(PGB[:, tc * 512:(tc + 1) * 512], ones_r[0:1, 0:128],
                                      G1[0:1, tc * 512:(tc + 1) * 512], True, True,
                                      waits=[s_rstd, grp[1]]))
                    x2_done = []
                    tok_x2sp = []
                    for fc in range(NF):
                        t1 = vop(lambda e, c=fc: e.tensor_sub(G0[:], XR2[c][:], PGA[:]),
                                 waits=tmb + [tok_stat, s_rstd])
                        t2 = vop(lambda e, c=fc: e.tensor_mul(XR2[c][:], G0[:], PGB[:]),
                                 waits=trb)
                        x2_done.append(t2)
                        tok_x2sp.append(dma("sync", spill_x2[fc], XR2[fc], waits=[t2],
                                            stream=f"sp2{fc}"))
                    tap("X2", XR2, x2_done)

                    if LVL >= 4:
                        for c in range(NF):
                            ar.free(f"ZBc{c}")
                        ar.free("SQ")

                        def make_xs(XS, XF, xf_ready, xg=()):
                            xs_tok = []
                            for c in range(NF):
                                if c < 4:
                                    tt = sop(lambda e, c=c: e.copy(XS[c][:], XF[c][:]),
                                             waits=[xf_ready[c]] + list(xg))
                                else:
                                    tv = vop(lambda e, c=c: e.memset(XS[c][:, 0:1], 0.0),
                                             waits=[xf_ready[c]] + list(xg))
                                    tt = sop(lambda e, c=c:
                                             e.copy(XS[c][:, 1:T], XF[c][:, 0:T - 1]),
                                             waits=[xf_ready[c]] + list(xg))
                                    tt = [tt, tv]
                                xs_tok.append(tt)
                            return xs_tok

                        def emit_ffn_layer(l, XF, xf_ready, pg_init, region_free=(),
                                           xf_src=None, final=False, x2_src=None):
                            LWd = dins
                            ar.alloc("SQL", 4 * 1024)
                            ar.alloc("SQL2", 4 * 1024)
                            ar.alloc("QKS2", 4 * 1024)
                            ar.alloc("GWL", 8 * 1024)
                            pre_guard = [p.now("T"), p.now("S")]
                            t_qs = dma("sync", qs_c[:], LWd[f"l{l}_qs"],
                                       waits=[p.now("V")], stream="qs")
                            t_ks = dma("sync", ks_c[:], LWd[f"l{l}_ks"],
                                       waits=[p.now("V")], stream="ks")
                            t_lng = dma("sync", lng_c[:], LWd[f"l{l}_lng"],
                                        waits=[p.now("V")], stream="lng")
                            t_lnb = dma("sync", lnb_c[:], LWd[f"l{l}_lnb"],
                                        waits=[p.now("V")], stream="lnb")

                            for c in range(NF):
                                ar.alloc(f"XS{c}", 4 * 1024)
                            XS = [ar.view(f"XS{c}", 0, 128, 0, T, BF16) for c in range(NF)]
                            xs_tok = make_xs(XS, XF, xf_ready,
                                             xg=[p.now("T")] if l > 0 else []) \
                                + list(region_free)
                            for c in range(NF):
                                ar.free(f"XFl{c}" if l > 0 else f"XRb{c}")

                            for c in range(NF):
                                ar.alloc(f"QKT{c}", 4 * 1024)
                            for i in range(NT):
                                ar.alloc(f"vL{i}", 520 * 2)
                            for i in range(4):
                                ar.alloc(f"NRM{i}", 8 * 1024)
                            ar.alloc("ROWS0", 8 * 1024)
                            ar.alloc("ROWS1", 8 * 1024)
                            QKT = [ar.view(f"QKT{c}", 0, 128, 0, T, BF16) for c in range(NF)]
                            VL = [ar.view(f"vL{i}", 0, 128, 0, 520, BF16) for i in range(NT)]
                            SQLv = [ar.view("SQL", 0, 128, 0, T, BF16),
                                    ar.view("SQL2", 0, 128, 0, T, BF16)]
                            QKS2 = [ar.view("QKS2", 0, 128, i * D, D, BF16) for i in range(2)]
                            GWL = ar.view("GWL", 0, 128, 0, NF * 512, BF16)
                            NRM = [ar.view(f"NRM{i}", 0, 128, 0, T, F32) for i in range(4)]
                            ROWS = [ar.view("ROWS0", 0, 128, 0, T, F32),
                                    ar.view("ROWS1", 0, 128, 0, T, F32)]
                            tok_wvl = dma("gpsimd",
                                          GWL.rearrange("p (c m) -> p c m", c=NF),
                                          LWd[f"l{l}_wv"].rearrange("(c p) m -> p c m", p=128),
                                          waits=pre_guard, stream="gwv")

                            s1state = [dict(pg_free=pg_init[0], cons=None),
                                       dict(pg_free=pg_init[1], cons=None)]
                            raw_done = [None] * NF
                            rows_done = [None] * NF

                            def s1gen(oc, lane):
                                st = s1state[lane]
                                PG = PGA if lane == 0 else PGB
                                SQl = SQLv[lane]
                                nb = 64 * (oc % 2)
                                rA = NRM[oc // 2][nb:nb + 1, :]
                                rB = NRM[oc // 2][nb + 32:nb + 33, :]
                                tok_w = dma("gpsimd",
                                            QKS2[lane].rearrange("p (c m) -> p c m", c=NF),
                                            LWd[f"l{l}_wqk"][:, oc * 128:(oc + 1) * 128]
                                            .rearrange("(c p) m -> p c m", p=128),
                                            waits=[st["cons"]] + pre_guard,
                                            stream=f"qks2{lane}")
                                tmm = None
                                for fc in range(NF):
                                    for tc in range(4):
                                        tmm = mm(PG[:, tc * 512:(tc + 1) * 512],
                                                 QKS2[lane][:, fc * 128:(fc + 1) * 128],
                                                 XS[fc][:, tc * 512:(tc + 1) * 512],
                                                 fc == 0, fc == NF - 1,
                                                 waits=[tok_w, st["pg_free"], xs_tok[fc]] +
                                                       (list(region_free)
                                                        if oc < 2 and fc == 0 else []))
                                st["cons"] = tmm
                                yield
                                tcpy = sop(lambda e, d=QKT[oc], PG=PG: e.copy(d[:], PG[:]),
                                           waits=[tmm])
                                tsq = sop(lambda e, PG=PG, SQl=SQl:
                                          e.activation(SQl[:], PG[:], AF.Square))
                                raw_done[oc] = tcpy
                                yield
                                ssA, ssB = [], []
                                for tc in range(4):
                                    ssA.append(mm(PG[0:1, tc * 512:(tc + 1) * 512],
                                                  ones_cb[0:64, :],
                                                  SQl[0:64, tc * 512:(tc + 1) * 512],
                                                  True, True, waits=[tsq, tok_ones_cb]))
                                    ssB.append(mm(PG[32:33, tc * 512:(tc + 1) * 512],
                                                  ones_cb[64:128, :],
                                                  SQl[64:128, tc * 512:(tc + 1) * 512],
                                                  True, True, tpos=(64, 32)))
                                yield
                                vA = vop(lambda e, rA=rA, PG=PG: e.tensor_scalar_max(
                                    rA, PG[0:1, :], 1e-24), waits=ssA)
                                vB = vop(lambda e, rB=rB, PG=PG: e.tensor_scalar_max(
                                    rB, PG[32:33, :], 1e-24), waits=ssB)
                                sop(lambda e, rA=rA: e.activation(rA, rA, AF.Ln), waits=[vA])
                                sA2 = sop(lambda e, rA=rA: e.activation(rA, rA, AF.Exp,
                                                                        scale=-0.5))
                                sop(lambda e, rB=rB: e.activation(rB, rB, AF.Ln), waits=[vB])
                                sB2 = sop(lambda e, rB=rB: e.activation(rB, rB, AF.Exp,
                                                                        scale=-0.5))
                                st["pg_free"] = [vA, vB]
                                rows_done[oc] = [sA2, sB2]

                            s1act = {0: s1gen(0, 0), 1: s1gen(1, 1)}
                            s1nxt = [2, 3]
                            while s1act:
                                for lane in (0, 1):
                                    g = s1act.get(lane)
                                    if g is None:
                                        continue
                                    try:
                                        next(g)
                                    except StopIteration:
                                        if s1nxt[lane] < NF:
                                            s1act[lane] = s1gen(s1nxt[lane], lane)
                                            s1nxt[lane] += 2
                                        else:
                                            del s1act[lane]

                            vl_ones = [vop(lambda e, v=VL[i]: e.memset(v[:, 64:520:65], 1.0),
                                           waits=xs_tok) for i in range(NT)]
                            vl_done = []
                            vhf = [s1state[0]["pg_free"], s1state[1]["pg_free"]]
                            for i in range(NT):
                                hh = i % 2
                                ps = PGA[:, 0:512] if hh == 0 else PGB[:, 0:512]
                                tmm = None
                                for fc in range(NF):
                                    tmm = mm(ps, XS[fc][:, i * 128:(i + 1) * 128],
                                             GWL[:, fc * 512:(fc + 1) * 512],
                                             fc == 0, fc == NF - 1,
                                             waits=[tok_wvl, vhf[hh]])
                                tev = vop(lambda e, ps=ps, v=VL[i]:
                                          e.tensor_copy(
                                              v[:, 0:520].rearrange("p (h c) -> p h c", h=8)[:, :, 0:64],
                                              ps.rearrange("p (h c) -> p h c", h=8)),
                                          waits=[tmm, vl_ones[i]])
                                vhf[hh] = tev
                                vl_done.append(tev)

                            for c in range(NF):
                                ar.free(f"XS{c}")
                            ar.free("GWL")
                            for nmx in ("XCaq", "XCsq", "XCak", "XCsk"):
                                ar.alloc(nmx, 8 * 1024)
                            XAQ = ar.view("XCaq", 0, 128, 0, T, F32)
                            XSQ = ar.view("XCsq", 0, 128, 0, T, F32)
                            XAK = ar.view("XCak", 0, 128, 0, T, F32)
                            XSK = ar.view("XCsk", 0, 128, 0, T, F32)
                            xcg = [p.now("T"), p.now("V")]
                            dma("sync", XAQ, dins["x_aq"], waits=xcg, stream="xc")
                            dma("sync", XSQ, dins["x_sq"], stream="xc")
                            dma("sync", XAK, dins["x_ak"], stream="xc")
                            dma("sync", XSK, dins["x_sk"], stream="xc")
                            tok_xc = ("xc", p.counts["xc"])

                            qk_outs = []
                            s2_pg = [vhf[0], vhf[1]]
                            rows_cons = [None, None]
                            for oc in range(NF):
                                lane = oc % 2
                                is_q = oc < 4
                                PG = PGA if lane == 0 else PGB
                                RW = ROWS[lane]
                                nb = 64 * (oc % 2)
                                rA = NRM[oc // 2][nb:nb + 1, :]
                                rB = NRM[oc // 2][nb + 32:nb + 33, :]
                                scpA = sop(lambda e, rA=rA, RW=RW: e.copy(RW[0:1, :], rA),
                                           waits=[rows_done[oc][0], rows_cons[lane]])
                                scpB = sop(lambda e, rB=rB, RW=RW: e.copy(RW[64:65, :], rB),
                                           waits=[rows_done[oc][1]])
                                btoks = []
                                for tc in range(4):
                                    btoks.append(mm(PG[0:64, tc * 512:(tc + 1) * 512],
                                                    ones_r[0:1, 0:64],
                                                    RW[0:1, tc * 512:(tc + 1) * 512],
                                                    True, True,
                                                    waits=[scpA, s2_pg[lane], tok_ones]))
                                    btoks.append(mm(PG[64:128, tc * 512:(tc + 1) * 512],
                                                    ones_r[64:65, 0:64],
                                                    RW[64:65, tc * 512:(tc + 1) * 512],
                                                    True, True, tpos=(64, 64),
                                                    waits=[scpB]))
                                rows_cons[lane] = btoks
                                scol = qs_c if is_q else ks_c
                                tqn = vop(lambda e, sc=scol, s=QKT[oc], PG=PG:
                                          e.scalar_tensor_tensor(G0[:], s[:], sc[:], PG[:],
                                                                 ALU.mult, ALU.mult),
                                          waits=btoks + [t_qs, t_ks, raw_done[oc]])
                                s2_pg[lane] = tqn
                                CA = XAQ if is_q else XAK
                                CS2 = XSQ if is_q else XSK
                                tm1 = gop(lambda e, CA=CA: e.tensor_mul(G1[:], G0[:], CA),
                                          waits=[tqn, tok_xc])
                                tsh = vop(lambda e: e.stream_shuffle(rowbuf[:], G0[:],
                                                                     SHUF_SWAP))
                                tm2 = gop(lambda e, CS2=CS2:
                                          e.tensor_mul(rowbuf[:], rowbuf[:], CS2),
                                          waits=[tsh, tok_xc])
                                t4 = vop(lambda e, d=QKT[oc]: e.tensor_add(d[:], G1[:],
                                                                           rowbuf[:]),
                                         waits=[tm1, tm2])
                                qk_outs.append(t4)
                            grpf = [s2_pg[0], s2_pg[1]]

                            tap(f"QKT_l{l}", QKT, qk_outs)
                            tap(f"VL_l{l}", VL, vl_done)
                            for nmx in ("XCaq", "XCsq", "XCak", "XCsk"):
                                ar.free(nmx)
                            ar.free("SQL"); ar.free("SQL2"); ar.free("QKS2")
                            for i in range(4):
                                ar.free(f"NRM{i}")
                            ar.free("ROWS0"); ar.free("ROWS1")
                            for c in range(4):
                                ar.alloc(f"OTL{c}", 4 * 1024)
                            ar.alloc("EXTL", 8 * 1024)
                            ar.alloc("EXTL2", 8 * 1024)
                            ar.alloc("PTL", 4 * 1024)
                            ar.alloc("LWO", 8 * 1024)
                            for c in range(NF):
                                ar.alloc(f"XFl{c}", 8 * 1024)
                            XFl = [ar.view(f"XFl{c}", 0, 128, 0, T, F32)
                                   for c in range(NF)]
                            xsrc_buf, xsrc_tok = xf_src
                            xg = [p.now("T"), p.now("V"), p.now("S")]
                            tok_xlr = [dma("sync", XFl[c], xsrc_buf[c],
                                           waits=[xsrc_tok[c]] + xg,
                                           stream=f"xlr{l}{c}")
                                       for c in range(NF)]
                            OTL = [ar.view(f"OTL{c}", 0, 128, 0, T, BF16) for c in range(4)]
                            QXL = [ar.view("EXTL", 0, 65, 0, T, BF16),
                                   ar.view("EXTL2", 0, 65, 0, T, BF16)]
                            KXL = [ar.view("EXTL", 0, 65, T, T, BF16),
                                   ar.view("EXTL2", 0, 65, T, T, BF16)]
                            PTL = [[ar.view("PTL", 0, 128, i * 512, 512, BF16)
                                    for i in range(2)],
                                   [ar.view("PTL", 0, 128, (2 + i) * 512, 512, BF16)
                                    for i in range(2)]]
                            LWO = ar.view("LWO", 0, 128, 0, 4 * D, BF16)
                            tok_wol = dma("gpsimd",
                                          LWO.rearrange("p (c m) -> p c m", c=4),
                                          LWd[f"l{l}_wo"].rearrange("(c p) m -> p c m", p=128),
                                          waits=[p.now("T"), p.now("V"), p.now("S")],
                                          stream="lwo")
                            tok_kxo = [vop(lambda e: e.memset(KXL[0][64:65, :], 1.0),
                                           waits=[p.now("S")]),
                                       vop(lambda e: e.memset(KXL[1][64:65, :], 1.0),
                                           waits=[p.now("S")])]

                            otl_done = []
                            lstate = [dict(prev_norm=None, prev_bcopy=None, prev_exp=None,
                                           pt_av=[None, None]),
                                      dict(prev_norm=None, prev_bcopy=None, prev_exp=None,
                                           pt_av=[None, None])]

                            def lgen_head(h, lane):
                                st = lstate[lane]
                                PG = PGA if lane == 0 else PGB
                                ch, sub = h // 2, (h % 2) * 64
                                QXl, KXl = QXL[lane], KXL[lane]
                                mf, mb = m_f[:, lane:lane + 1], m_bf[:, lane:lane + 1]
                                tqx = vop(lambda e: e.tensor_copy(QXl[0:64, :],
                                                                  QKT[ch][sub:sub + 64, :]),
                                          waits=[qk_outs[ch], st["prev_norm"],
                                                 st["prev_exp"]])
                                tkx = vop(lambda e: e.tensor_copy(KXl[0:64, :],
                                                                  QKT[4 + ch][sub:sub + 64, :]),
                                          waits=[qk_outs[4 + ch], tok_kxo[lane]])
                                for w in range(NW):
                                    kcol0 = (w - 1) * 512 if w > 0 else 0
                                    kw = 1024 if w > 0 else 512
                                    nkt = kw // 128
                                    scal_mrow = st.get("scal_mrow")
                                    prev_max = st.get("win_free")
                                    for qt_i in range(4):
                                        q0 = w * 512 + qt_i * 128
                                        ktoks = []
                                        for kc in range(kw // 512):
                                            tmm = mm(PG[:, kc * 512:(kc + 1) * 512],
                                                     QKT[ch][sub:sub + 64, q0:q0 + 128],
                                                     QKT[4 + ch][sub:sub + 64,
                                                                 kcol0 + kc * 512:kcol0 + (kc + 1) * 512],
                                                     True, True,
                                                     waits=[tqx, tkx, prev_max])
                                            ktoks.append(tmm)
                                        tmax = vop(lambda e, kw=kw, PG=PG, mf=mf:
                                                   e.reduce_max(mf, PG[:, 0:kw:MAXSTRIDE],
                                                                axis=AX.X), waits=ktoks)
                                        tneg = vop(lambda e, mf=mf, mb=mb:
                                                   e.tensor_scalar_mul(mb, mf, -1.0))
                                        yield
                                        tmr = mm(PG[0:1, 1536 + qt_i * 128:1536 + (qt_i + 1) * 128],
                                                 mb, ident_b[:], True, True,
                                                 waits=[tneg, tok_c["ident_bf"], scal_mrow,
                                                        st["pt_av"][0] if (w == 0 and qt_i == 0)
                                                        else None,
                                                        st["pt_av"][1] if (w == 0 and qt_i == 0)
                                                        else None])
                                        scal_mrow = sop(lambda e, q0=q0, qt_i=qt_i, PG=PG,
                                                        QXl=QXl:
                                                        e.copy(QXl[64:65, q0:q0 + 128],
                                                               PG[0:1, 1536 + qt_i * 128:1536 + (qt_i + 1) * 128]),
                                                        waits=[tmr, tqx])
                                        prev_max = [tmax, scal_mrow]
                                        yield
                                    st["scal_mrow"] = scal_mrow
                                    avtoks = []
                                    rot_free = [prev_max, prev_max]
                                    own0 = (kw - 512) // 128
                                    order = [(own0 + i2, i2 * 128, 512, "own") for i2 in range(4)]
                                    order += [(i2, 0, (i2 + 1) * 128, "prev")
                                              for i2 in range(nkt - 4)]
                                    for idx, (kt, qlo, qhi, kind) in enumerate(order):
                                        width = qhi - qlo
                                        rb = idx % 2
                                        ptb = PTL[lane][rb]
                                        tmm = mm(PG[:, rb * 512:rb * 512 + width],
                                                 KXl[:, kcol0 + kt * 128:kcol0 + (kt + 1) * 128],
                                                 QXl[:, w * 512 + qlo:w * 512 + qhi],
                                                 True, True,
                                                 waits=[tkx, scal_mrow, rot_free[rb]])
                                        texp = sop(lambda e, rb=rb, ptb=ptb, qlo=qlo,
                                                   width=width, PG=PG:
                                                   e.activation(ptb[:, qlo:qlo + width],
                                                                PG[:, rb * 512:rb * 512 + width],
                                                                AF.Exp),
                                                   waits=[tmm, st["pt_av"][rb]])
                                        rot_free[rb] = texp
                                        if kind == "own":
                                            tmask = vop(lambda e, ptb=ptb, qlo=qlo:
                                                        e.tensor_mul(ptb[:, qlo:qlo + 128],
                                                                     ptb[:, qlo:qlo + 128],
                                                                     tri_u[:]),
                                                        waits=[texp, tok_c["tri_u"]])
                                        else:
                                            tmask = vop(lambda e, ptb=ptb, qhi=qhi:
                                                        e.tensor_mul(ptb[:, qhi - 128:qhi],
                                                                     ptb[:, qhi - 128:qhi],
                                                                     tri_l[:]),
                                                        waits=[texp, tok_c["tri_l"]])
                                        yield
                                        gkt = kcol0 // 128 + kt
                                        tav = mm(PG[0:65, 1024 + qlo:1024 + qhi],
                                                 VL[gkt][:, h * 65:(h + 1) * 65],
                                                 ptb[:, qlo:qhi],
                                                 idx == 0, idx == len(order) - 1,
                                                 waits=[tmask, vl_done[gkt],
                                                        st["prev_norm"] if idx == 0 else None])
                                        st["pt_av"][rb] = tav
                                        avtoks.append(tav)
                                        yield
                                    sop(lambda e, PG=PG, lane=lane:
                                        e.activation(ROW_L[lane][0:1, 0:512],
                                                     PG[64:65, 1024:1536], AF.Ln),
                                        waits=avtoks)
                                    trc = sop(lambda e, lane=lane:
                                              e.activation(ROW_L[lane][0:1, 0:512],
                                                           ROW_L[lane][0:1, 0:512],
                                                           AF.Exp, scale=-1.0))
                                    yield
                                    yield
                                    tb = mm(PG[0:64, 0:512], ones_r[0:1, 0:64],
                                            ROW_L[lane][0:1, 0:512], True, True,
                                            waits=[trc, rot_free[0]])
                                    tbc = sop(lambda e, PG=PG, lane=lane:
                                              e.copy(GB64[lane][:, 0:512], PG[0:64, 0:512]),
                                              waits=[tb])
                                    tnm = vop(lambda e, PG=PG, lane=lane, w=w:
                                              e.tensor_mul(OTL[ch][sub:sub + 64,
                                                               w * 512:(w + 1) * 512],
                                                           PG[0:64, 1024:1536],
                                                           GB64[lane][:, 0:512]),
                                              waits=[tbc] + avtoks)
                                    st["prev_bcopy"] = tbc
                                    st["prev_norm"] = tnm
                                    st["prev_exp"] = rot_free[1]
                                    st["win_free"] = [rot_free[0], rot_free[1], tbc, tnm]
                                    otl_done.append(tnm)
                                    yield

                            lact = {0: lgen_head(0, 0), 1: lgen_head(1, 1)}
                            lnxt = [2, 3]
                            while lact:
                                for lane in (0, 1):
                                    g = lact.get(lane)
                                    if g is None:
                                        continue
                                    try:
                                        next(g)
                                    except StopIteration:
                                        if lnxt[lane] < LH:
                                            lact[lane] = lgen_head(lnxt[lane], lane)
                                            lnxt[lane] += 2
                                        else:
                                            del lact[lane]
                            prev_norm = [lstate[0]["prev_norm"], lstate[1]["prev_norm"]]
                            prev_bcopy = [lstate[0]["prev_bcopy"], lstate[1]["prev_bcopy"]]

                            tap(f"OTL_l{l}", OTL, otl_done)
                            attn_res = []
                            grpw = [lstate[0]["win_free"], lstate[1]["win_free"]]
                            for oc in range(NF):
                                g = oc % 2
                                PG = PGA if g == 0 else PGB
                                tmm = None
                                for fc in range(4):
                                    for tc in range(4):
                                        tmm = mm(PG[:, tc * 512:(tc + 1) * 512],
                                                 LWO[:, fc * D + oc * 128:fc * D + (oc + 1) * 128],
                                                 OTL[fc][:, tc * 512:(tc + 1) * 512],
                                                 fc == 0, fc == 3,
                                                 waits=[tok_wol, grpw[g], otl_done[-1],
                                                        prev_norm])
                                tadd = vop(lambda e, PG=PG, c=oc:
                                           e.tensor_add(XFl[c][:], XFl[c][:], PG[:]),
                                           waits=[tmm, tok_xlr[oc]])
                                grpw[g] = tadd
                                attn_res.append(tadd)

                            for c in range(NF):
                                ar.free(f"QKT{c}")
                            for i in range(NT):
                                ar.free(f"vL{i}")
                            ar.free("EXTL"); ar.free("EXTL2")
                            ar.free("PTL")
                            for c in range(NF):
                                ar.alloc(f"HB{c}", 4 * 1024)
                            for c in range(NF):
                                ar.alloc(f"ZB2c{c}", 4 * 1024)
                            ar.alloc("SQ2", 4 * 1024)
                            for c in range(4):
                                ar.free(f"OTL{c}")
                            ar.free("LWO")
                            HB = [ar.view(f"HB{c}", 0, 128, 0, T, BF16) for c in range(NF)]
                            ZB2 = [ar.view(f"ZB2c{c}", 0, 128, 0, T, BF16) for c in range(NF)]
                            SQ2 = ar.view("SQ2", 0, 128, 0, T, BF16)

                            tok_sq2 = None
                            tok_s_l = None
                            for fc in range(NF):
                                if fc % 2 == 0:
                                    tz = sop(lambda e, c=fc: e.copy(ZB2[c][:], XFl[c][:]),
                                             waits=[attn_res[fc]])
                                else:
                                    tz = gop(lambda e, c=fc:
                                             e.tensor_copy(ZB2[c][:], XFl[c][:]),
                                             waits=[attn_res[fc]])
                                tsq = sop(lambda e, c=fc: e.activation(SQ2, ZB2[c][:], AF.Square),
                                          waits=[tz, tok_sq2])
                                half = 0 if fc < 4 else 1
                                srow = 0 if half == 0 else 32
                                for tc in range(4):
                                    tok_s_l = mm(PGA[srow:srow + 1, tc * 512:(tc + 1) * 512],
                                                 ones_cb[:],
                                                 ZB2[fc][:, tc * 512:(tc + 1) * 512],
                                                 fc % 4 == 0, fc % 4 == 3,
                                                 waits=[tz] + ([grpw[0]] if fc == 0 else []),
                                                 tpos=None if half == 0 else (0, 32))
                                    tok_sq2 = mm(PGB[srow:srow + 1, tc * 512:(tc + 1) * 512],
                                                 ones_cb[:],
                                                 SQ2[:, tc * 512:(tc + 1) * 512],
                                                 fc % 4 == 0, fc % 4 == 3,
                                                 waits=[tsq] + ([grpw[1]] if fc == 0 else []),
                                                 tpos=None if half == 0 else (0, 32))
                            tok_stat2 = [tok_sq2, tok_s_l]
                            r_s2 = sop(lambda e: e.copy(rowbuf[0:1, :], PGA[32:33, :]),
                                       waits=[tok_stat2])
                            v_mu = vop(lambda e: e.tensor_add(G0[0:1, 1:T], PGA[0:1, 1:T],
                                                              rowbuf[0:1, 0:T - 1]),
                                       waits=[r_s2, tok_stat2])
                            v_mu0 = vop(lambda e: e.tensor_copy(G0[0:1, 0:1], PGA[0:1, 0:1]))
                            s_mu = sop(lambda e: e.activation(G1[0:1, :], G0[0:1, :], AF.Copy,
                                                              scale=1.0 / D),
                                       waits=[v_mu, v_mu0])
                            r_q2 = sop(lambda e: e.copy(rowbuf[0:1, :], PGB[32:33, :]))
                            v_e2 = vop(lambda e: e.tensor_add(G0[0:1, 1:T], PGB[0:1, 1:T],
                                                              rowbuf[0:1, 0:T - 1]),
                                       waits=[r_q2, s_mu])
                            v_e20 = vop(lambda e: e.tensor_copy(G0[0:1, 0:1], PGB[0:1, 0:1]))
                            s_e2 = sop(lambda e: e.activation(rowbuf[0:1, :], G0[0:1, :], AF.Copy,
                                                              scale=1.0 / D),
                                       waits=[v_e2, v_e20])
                            s_m2 = sop(lambda e: e.activation(G0[0:1, :], G1[0:1, :], AF.Square))
                            v_var = vop(lambda e: e.tensor_sub(rowbuf[0:1, :], rowbuf[0:1, :],
                                                               G0[0:1, :]), waits=[s_m2, s_e2])
                            s_ln2 = sop(lambda e: e.activation(G0[0:1, :], rowbuf[0:1, :], AF.Ln,
                                                               bias=eps_ln[:]), waits=[v_var])
                            s_rstd2 = sop(lambda e: e.activation(rowbuf[0:1, :], G0[0:1, :],
                                                                 AF.Exp, scale=-0.5))
                            tmb2, trb2 = [], []
                            tmb2, trb2 = [], []
                            for tc in range(4):
                                tmb2.append(mm(PGA[:, tc * 512:(tc + 1) * 512], ones_r[0:1, 0:128],
                                               G1[0:1, tc * 512:(tc + 1) * 512], True, True,
                                               waits=[s_mu, s_e2, s_m2, r_s2, s_rstd2]))
                                trb2.append(mm(PGB[:, tc * 512:(tc + 1) * 512], ones_r[0:1, 0:128],
                                               rowbuf[0:1, tc * 512:(tc + 1) * 512], True, True,
                                               waits=[s_rstd2, r_q2]))
                            hb_done = []
                            for fc in range(NF):
                                if fc < 4:
                                    t1 = vop(lambda e, c=fc: e.tensor_sub(G0[:], XFl[c][:], PGA[:]),
                                             waits=tmb2 + [s_rstd2, s_ln2, attn_res[fc]])
                                else:
                                    vop(lambda e: e.tensor_scalar_mul(G0[:, 0:1], PGA[:, 0:1], -1.0),
                                        waits=tmb2 + [s_rstd2, s_ln2, attn_res[fc]])
                                    t1 = vop(lambda e, c=fc: e.tensor_sub(G0[:, 1:T],
                                                                          XFl[c][:, 0:T - 1],
                                                                          PGA[:, 1:T]))
                                t2 = vop(lambda e: e.tensor_mul(G1[:], G0[:], PGB[:]),
                                         waits=trb2)
                                t3 = vop(lambda e, c=fc: e.tensor_scalar(
                                    HB[c][:], G1[:], lng_c[:, c:c + 1], lnb_c[:, c:c + 1],
                                    ALU.mult, ALU.add),
                                    waits=[t2, t_lng, t_lnb])
                                hb_done.append([t2, t3])
                            tap(f"HB_l{l}", HB, hb_done)
                            for c in range(NF):
                                ar.free(f"ZB2c{c}")
                            ar.free("SQ2")

                            ar.alloc("W1S", 4 * 1024)
                            ar.alloc("W2Sa", 6 * 1024)
                            ar.alloc("W2Sb", 6 * 1024)
                            W1S = [ar.view("W1S", 0, 128, i * D, D, BF16) for i in range(2)]
                            NB0 = NHC // 2
                            wstate = {"cons": [[t3, t3], [t3, t3]]}
                            tok_b1c = dma("gpsimd", b1c_s[:], LWd[f"l{l}_b1c"],
                                          waits=[p.now("V"), p.now("S")], stream="b1c")
                            tok_b2c = dma("gpsimd", b2c_s[:], LWd[f"l{l}_b2c"],
                                          waits=[p.now("V")], stream="b2c")
                            w2state = [[p.now("T"), p.now("S")]] * 2
                            mlp_res = []
                            mlp_add = []
                            pgl = [hb_done[-1], hb_done[-1]]
                            for blk in range(2):
                                ccs = list(range(0, NB0)) if blk == 0 else \
                                      list(range(NB0, NHC))
                                nblk = len(ccs)
                                for cc in ccs:
                                    ar.alloc(f"H2c{cc}", 4 * 1024)
                                H2 = {cc: ar.view(f"H2c{cc}", 0, 128, 0, T, BF16)
                                      for cc in ccs}
                                h2_done = {}
                                for cc in ccs:
                                    toks_pair = {}
                                    for ag in range(2):
                                        mcc = 2 * cc + ag
                                        sl = mcc % 2
                                        tok_w1 = dma("gpsimd",
                                                     W1S[sl].rearrange("p (c m) -> p c m", c=NF),
                                                     LWd[f"l{l}_w1"][:, mcc * 128:(mcc + 1) * 128]
                                                     .rearrange("(c p) m -> p c m", p=128),
                                                     waits=[wstate["cons"][sl][0]] +
                                                           (hb_done[-1] if mcc < 2 else []),
                                                     stream=f"w1s{sl}")
                                        PG = PGA if ag == 0 else PGB
                                        tmm = None
                                        for fc in range(NF):
                                            for tc in range(4):
                                                tmm = mm(PG[:, tc * 512:(tc + 1) * 512],
                                                         W1S[sl][:, fc * 128:(fc + 1) * 128],
                                                         HB[fc][:, tc * 512:(tc + 1) * 512],
                                                         fc == 0, fc == NF - 1,
                                                         waits=[hb_done[fc], tok_w1] +
                                                               ([pgl[ag]] if fc == 0
                                                                else []))
                                        wstate["cons"][sl] = [tmm, tmm]
                                        toks_pair[ag] = tmm
                                    tgel = sop(lambda e, cc=cc:
                                               e.activation(G0[:], PGB[:], AF.Gelu,
                                                            bias=b1c_s[:, 2 * cc + 1:
                                                                       2 * cc + 2]),
                                               waits=[toks_pair[1], tok_b1c])
                                    tmul = vop(lambda e, d=H2[cc], cc=cc:
                                               e.scalar_tensor_tensor(
                                                   d[:], PGA[:],
                                                   b1c_s[:, 2 * cc:2 * cc + 1], G0[:],
                                                   ALU.add, ALU.mult),
                                               waits=[tgel, toks_pair[0], tok_b1c])
                                    pgl = [tmul, tgel]
                                    h2_done[cc] = tmul
                                if blk == 1:
                                    for c in range(NF):
                                        ar.free(f"HB{c}")
                                    ar.free("W1S")
                                    if final:
                                        ar.alloc("X2Ra", 8 * 1024)
                                        ar.alloc("X2Rb", 8 * 1024)
                                        X2R = [ar.view("X2Ra", 0, 128, 0, T, F32),
                                               ar.view("X2Rb", 0, 128, 0, T, F32)]
                                        x2_buf, x2_tok = x2_src
                                        x2g = [p.now("T"), p.now("V"), p.now("S")]
                                        x2prev = [None, None]
                                for oq in range(4):
                                    sl = oq % 2
                                    tok_w2 = dma("gpsimd",
                                                 ar.view(f"W2S{'ab'[sl]}", 0, 128, 0,
                                                         nblk * 256, BF16)
                                                 .rearrange("p (c m) -> p c m", c=nblk),
                                                 LWd[f"l{l}_w2"][ccs[0] * 128:
                                                                 (ccs[0] + nblk) * 128,
                                                                 oq * 256:(oq + 1) * 256]
                                                 .rearrange("(c p) m -> p c m", p=128),
                                                 waits=[w2state[sl]], stream=f"w2s{sl}")
                                    W2Sv = ar.view(f"W2S{'ab'[sl]}", 0, 128, 0,
                                                   nblk * 256, BF16)
                                    tmm = {}
                                    for i_f, cc in enumerate(ccs):
                                        for och in range(2):
                                            PG = PGA if och == 0 else PGB
                                            for tc in range(4):
                                                tmm[och] = mm(
                                                    PG[:, tc * 512:(tc + 1) * 512],
                                                    W2Sv[:, i_f * 256 + och * 128:
                                                         i_f * 256 + (och + 1) * 128],
                                                    H2[cc][:, tc * 512:(tc + 1) * 512],
                                                    i_f == 0, i_f == nblk - 1,
                                                    waits=[h2_done[cc], tok_w2] +
                                                          ([pgl[och]] if i_f == 0
                                                           else []))
                                    w2state[sl] = [tmm[0], tmm[1]]
                                    for och in range(2):
                                        oc = oq * 2 + och
                                        PG = PGA if och == 0 else PGB
                                        if blk == 0:
                                            tadd = vop(lambda e, PG=PG, c=oc, oc=oc:
                                                       e.scalar_tensor_tensor(
                                                           XFl[c][:], PG[:],
                                                           b2c_s[:, oc:oc + 1],
                                                           XFl[c][:], ALU.add, ALU.add),
                                                       waits=[tmm[och], tok_b2c])
                                        else:
                                            tadd = vop(lambda e, PG=PG, c=oc:
                                                       e.tensor_add(XFl[c][:],
                                                                    XFl[c][:], PG[:]),
                                                       waits=[tmm[och]])
                                        pgl[och] = tadd
                                        if blk == 1:
                                            mlp_add.append(tadd)
                                            if final:
                                                tld = dma("sync", X2R[oc % 2], x2_buf[oc],
                                                          waits=[x2_tok[oc],
                                                                 x2prev[oc % 2]] + x2g,
                                                          stream=f"x2l{oc % 2}")
                                                tz2 = vop(lambda e, c=oc:
                                                          e.tensor_add(XFl[c][:], XFl[c][:],
                                                                       X2R[c % 2][:]),
                                                          waits=[tld])
                                                x2prev[oc % 2] = tz2
                                                pgl[och] = tadd
                                                mlp_res.append(tz2)
                                            else:
                                                mlp_res.append(
                                                    dma("sync", spill_xf[oc], XFl[oc][:],
                                                        waits=[tadd], stream=f"sxf{oc}"))
                                for cc in ccs:
                                    ar.free(f"H2c{cc}")
                            ar.free("W2Sa")
                            ar.free("W2Sb")
                            if final:
                                ar.free("X2Ra"); ar.free("X2Rb")
                            return mlp_res, [pgl[0], pgl[1]], XFl, mlp_add

                        mlp_res0, pgf, XFl0, mlp_add0 = emit_ffn_layer(
                            0, XR2, x2_done, [x2_done[-1], x2_done[-1]],
                            region_free=list(tok_x2sp),
                            xf_src=(spill_x2, tok_x2sp))
                        tap("XF0", "dram_xf", mlp_res0)

                        if LVL >= 5:
                            mlp_res1, pgf, XFl1, mlp_add1 = emit_ffn_layer(
                                1, XFl0, mlp_add0, pgf,
                                region_free=list(mlp_res0),
                                xf_src=(spill_xf, mlp_res0),
                                final=(LVL >= 6),
                                x2_src=(spill_x2, tok_x2sp))
                            tap("XF1", XFl1, mlp_res1)

                        if LVL >= 6:
                            for c in range(NF):
                                ar.alloc(f"ZB3c{c}", 4 * 1024)
                            ar.alloc("SQ3", 4 * 1024)
                            XZ = XFl1
                            ZB3 = [ar.view(f"ZB3c{c}", 0, 128, 0, T, BF16) for c in range(NF)]
                            SQ3 = ar.view("SQ3", 0, 128, 0, T, BF16)
                            z_done = mlp_res1
                            tguard = [p.now("T"), p.now("V")]

                            tok_sq3 = None
                            for fc in range(NF):
                                if fc % 2 == 0:
                                    tz = sop(lambda e, c=fc: e.copy(ZB3[c][:], XZ[c][:]),
                                             waits=[z_done[fc],
                                                    tguard if fc == 0 else None])
                                else:
                                    tz = gop(lambda e, c=fc:
                                             e.tensor_copy(ZB3[c][:], XZ[c][:]),
                                             waits=[z_done[fc],
                                                    tguard if fc == 1 else None])
                                tsq = sop(lambda e, c=fc: e.activation(SQ3, ZB3[c][:], AF.Square),
                                          waits=[tz, tok_sq3])
                                for tc in range(4):
                                    mm(PGA[0:1, tc * 512:(tc + 1) * 512], ones_cb[:],
                                       ZB3[fc][:, tc * 512:(tc + 1) * 512],
                                       fc == 0, fc == NF - 1,
                                       waits=[tz] + ([pgf[0], p.now("S")]
                                                     if fc == 0 else []))
                                    tok_sq3 = mm(PGA[32:33, tc * 512:(tc + 1) * 512], ones_cb[:],
                                                 SQ3[:, tc * 512:(tc + 1) * 512],
                                                 fc == 0, fc == NF - 1, waits=[tsq],
                                                 tpos=(0, 32))
                            s_mu = sop(lambda e: e.activation(rowbuf[0:1, :], PGA[0:1, :], AF.Copy,
                                                              scale=1.0 / D), waits=[tok_sq3])
                            s_e2 = sop(lambda e: e.activation(G1[0:1, :], PGA[32:33, :], AF.Copy,
                                                              scale=1.0 / D))
                            s_m2 = sop(lambda e: e.activation(G0[0:1, :], rowbuf[0:1, :], AF.Square))
                            v_var = vop(lambda e: e.tensor_sub(G1[0:1, :], G1[0:1, :],
                                                               G0[0:1, :]), waits=[s_m2, s_e2])
                            s_ln = sop(lambda e: e.activation(G0[0:1, :], G1[0:1, :], AF.Ln,
                                                              bias=eps_ln[:]), waits=[v_var])
                            s_rstd = sop(lambda e: e.activation(G1[0:1, :], G0[0:1, :], AF.Exp,
                                                                scale=-0.5))
                            tmb = []
                            for tc in range(4):
                                tmb.append(mm(PGA[:, tc * 512:(tc + 1) * 512], ones_r[0:1, 0:128],
                                              rowbuf[0:1, tc * 512:(tc + 1) * 512], True, True,
                                              waits=[s_e2, s_mu]))
                            trb = []
                            for tc in range(4):
                                trb.append(mm(PGB[:, tc * 512:(tc + 1) * 512], ones_r[0:1, 0:128],
                                              G1[0:1, tc * 512:(tc + 1) * 512], True, True,
                                              waits=[s_rstd, pgf[1]]))
                            out_done = []
                            for fc in range(NF):
                                t1 = vop(lambda e, c=fc: e.tensor_sub(G0[:], XZ[c][:], PGA[:]),
                                         waits=tmb + [tok_sq3, s_rstd])
                                t2 = vop(lambda e, c=fc: e.tensor_mul(XZ[c][:], G0[:], PGB[:]),
                                         waits=trb)
                                out_done.append(t2)

                            otile = [G0, G1]
                            prev_ev = [out_done[-1], out_done[-1]]
                            st_tok = [None, None]
                            for i in range(NT):
                                hh = i % 2
                                ps = PGA[:, 0:1024] if hh == 0 else PGB[:, 0:1024]
                                tmm = None
                                for fc in range(NF):
                                    tmm = mm(ps[:, fc * 128:(fc + 1) * 128],
                                             XZ[fc][:, i * 128:(i + 1) * 128], ident_f[:],
                                             True, True,
                                             waits=[out_done[fc], tok_c["ident_f32"],
                                                    prev_ev[hh]])
                                tev = vop(lambda e, ps=ps, gb=otile[hh]:
                                          e.tensor_copy(gb[:, 0:1024], ps),
                                          waits=[tmm, st_tok[hh]])
                                p.stream(f"sto{hh}")
                                st_tok[hh] = p.emit("sync",
                                       lambda e, i=i, gb=otile[hh]:
                                       e.dma_start(out=OUT[i * 128:(i + 1) * 128, :],
                                                   in_=gb[:, 0:1024]),
                                       waits=[tev], inc=f"sto{hh}", n=16)
                                prev_ev[hh] = tev

                        p.barrier()


        p.emit("sync", None,
               waits=[("STX", p.counts["STX"])] +
                     [(s, p.counts[s]) for s in ("sto0", "sto1") if s in p.counts])
        if "out_dummy" not in dbgset and LVL < 6:
            zs = rowbuf[0:128, 0:D]
            tz = vop(lambda e: e.memset(zs, 0.0))
            for i in range(NT):
                p.emit("sync", lambda e, o=OUT[i * 128:(i + 1) * 128, :], i2=zs:
                       e.dma_start(out=o, in_=i2), waits=[tz], inc="STX", n=16)
            p.emit("sync", None, waits=[("STX", p.counts["STX"])])

        block = es.enter_context(nc.Block())
        p.install(block)

    return nc


def kernel(**inputs):
    consts = host_constants()
    w = prep_weights(inputs)
    nc = build_nc(upto="all")
    x = np.asarray(inputs["L0_attn_x"])
    in_maps = []
    for core in range(8):
        im = {"x": np.ascontiguousarray(x[core])}
        im.update({k: np.asarray(v) for k, v in w.items()})
        im.update(consts)
        in_maps.append(im)
    res = run_bass_kernel_spmd(nc, in_maps, core_ids=list(range(8)))
    out = np.stack([r["out"] for r in res.results], axis=0)
    return out.astype(np.float32)


# version 14
# speedup vs baseline: 1.0023x; 1.0023x over previous
import contextlib
import numpy as np
import ml_dtypes
import concourse.bass as bass
import concourse.mybir as mybir
from concourse.bass_utils import run_bass_kernel_spmd

F32 = mybir.dt.float32
BF16 = mybir.dt.bfloat16
AF = mybir.ActivationFunctionType
ALU = mybir.AluOpType
AX = mybir.AxisListType
BF = ml_dtypes.bfloat16

D = 1024
T = 2048
NT = T // 128
NF = D // 128
NH = 16
HD = 64
GSCALE = HD ** -0.5
LH = 8
WIN = 512
NW = T // WIN
HID = 2730
HIDP = 2816
NHC = HIDP // 128
QK_SCALE = 8.0
MAXSTRIDE = 8

ENGINES = ("sync", "tensor", "vector", "scalar", "gpsimd")
SHUF_SWAP = [i ^ 1 for i in range(32)]


def _pair_perm():
    pp = np.empty(HD, np.int64)
    pp[0::2] = np.arange(32)
    pp[1::2] = np.arange(32) + 32
    return pp


def host_constants():
    c = {}
    c["ident_f32"] = np.eye(128, dtype=np.float32)
    c["ident_bf"] = np.eye(128, dtype=np.float32).astype(BF)
    j = np.arange(32, dtype=np.float64)
    inv = 1.0 / (10000.0 ** (j / 32.0))
    t = np.arange(T, dtype=np.float64)
    ang = t[None, :] * inv[:, None]
    cos = np.cos(ang); sin = np.sin(ang)
    cos64 = np.empty((64, T)); sin64 = np.empty((64, T))
    cos64[0::2] = cos; cos64[1::2] = cos
    sin64[0::2] = -sin; sin64[1::2] = sin
    c["g_cos"] = np.tile(cos64, (2, 1)).astype(np.float32)
    c["g_sin"] = np.tile(sin64, (2, 1)).astype(np.float32)

    half_idx = np.arange(0, HD, 2, dtype=np.float64)
    invf = 1.0 / (10000.0 ** (half_idx / HD))
    angl = t[None, :] * invf[:, None]
    cosl = np.cos(angl); sinl = np.sin(angl)
    base = (half_idx + 0.4 * HD) / (1.4 * HD)
    power = (t - T // 2) / float(WIN // 2)
    scale = base[:, None] ** power[None, :]

    def inter(even, odd):
        out = np.empty((64, T))
        out[0::2] = even; out[1::2] = odd
        return np.tile(out, (2, 1)).astype(np.float32)

    aq = cosl * scale * QK_SCALE; sq = sinl * scale * QK_SCALE
    ak = cosl / scale; sk = sinl / scale
    c["x_aq"] = inter(aq, aq)
    c["x_sq"] = inter(-sq, sq)
    c["x_ak"] = inter(ak, ak)
    c["x_sk"] = inter(-sk, sk)

    kk = np.arange(128)[:, None]; qq = np.arange(128)[None, :]
    c["tri_u"] = (qq >= kk).astype(np.float32).astype(BF)
    c["tri_l"] = (qq <= kk).astype(np.float32).astype(BF)
    return c


def prep_weights(inp):
    w = {}
    perm = _pair_perm()
    gperm = np.concatenate([perm + 64 * h for h in range(NH)])
    w["wq"] = np.ascontiguousarray(np.asarray(inp["L0_attn_wq_w"])[:, gperm])
    w["wk"] = np.ascontiguousarray(np.asarray(inp["L0_attn_wk_w"])[:, gperm])
    w["wv"] = np.ascontiguousarray(np.asarray(inp["L0_attn_wv_w"]))
    w["wo"] = np.ascontiguousarray(np.asarray(inp["L0_attn_wo_w"]))
    w["bq"] = np.ascontiguousarray(np.asarray(inp["L0_attn_wq_b"])[gperm].reshape(1, D))
    w["bk"] = np.ascontiguousarray(np.asarray(inp["L0_attn_wk_b"])[gperm].reshape(1, D))
    w["bv"] = np.ascontiguousarray(np.asarray(inp["L0_attn_wv_b"]).reshape(1, D))
    w["bo_c"] = np.ascontiguousarray(
        np.asarray(inp["L0_attn_wo_b"]).reshape(NF, 128).T)

    lperm = np.concatenate([perm + 64 * h for h in range(LH)])
    for l in range(2):
        wqkv = np.asarray(inp["L0_ffn_wqkv"][l])
        w[f"l{l}_wqk"] = np.ascontiguousarray(
            np.concatenate([wqkv[:, 0:512][:, lperm], wqkv[:, 512:1024][:, lperm]], axis=1))
        w[f"l{l}_wv"] = np.ascontiguousarray(wqkv[:, 1024:1536])
        w[f"l{l}_qs"] = np.ascontiguousarray(
            np.tile(np.asarray(inp["L0_ffn_q_scale"][l])[perm], 2).reshape(128, 1))
        w[f"l{l}_ks"] = np.ascontiguousarray(
            np.tile(np.asarray(inp["L0_ffn_k_scale"][l])[perm], 2).reshape(128, 1))
        w[f"l{l}_wo"] = np.ascontiguousarray(np.asarray(inp["L0_ffn_wo"][l]))
        w[f"l{l}_lng"] = np.ascontiguousarray(np.asarray(inp["L0_ffn_ln_g"][l]).reshape(NF, 128).T)
        w[f"l{l}_lnb"] = np.ascontiguousarray(np.asarray(inp["L0_ffn_ln_b"][l]).reshape(NF, 128).T)
        w1 = np.asarray(inp["L0_ffn_w1"][l]); b1 = np.asarray(inp["L0_ffn_b1"][l])
        a_w = np.zeros((D, HIDP), np.float32); a_w[:, :HID] = w1[:, :HID]
        g_w = np.zeros((D, HIDP), np.float32); g_w[:, :HID] = w1[:, HID:]
        a_b = np.zeros(HIDP, np.float32); a_b[:HID] = b1[:HID]
        g_b = np.zeros(HIDP, np.float32); g_b[:HID] = b1[HID:]
        w1p = np.empty((D, 2 * HIDP), np.float32)
        b1p = np.empty((1, 2 * HIDP), np.float32)
        for cc in range(NHC):
            s = slice(cc * 128, (cc + 1) * 128)
            w1p[:, (2 * cc) * 128:(2 * cc + 1) * 128] = a_w[:, s]
            w1p[:, (2 * cc + 1) * 128:(2 * cc + 2) * 128] = g_w[:, s]
            b1p[0, (2 * cc) * 128:(2 * cc + 1) * 128] = a_b[s]
            b1p[0, (2 * cc + 1) * 128:(2 * cc + 2) * 128] = g_b[s]
        w[f"l{l}_w1"] = w1p
        w[f"l{l}_b1c"] = np.ascontiguousarray(b1p.reshape(2 * NHC, 128).T)
        w2p = np.zeros((HIDP, D), np.float32)
        w2p[:HID] = np.asarray(inp["L0_ffn_w2"][l])
        w[f"l{l}_w2"] = w2p
        w[f"l{l}_b2c"] = np.ascontiguousarray(
            np.asarray(inp["L0_ffn_b2"][l]).reshape(NF, 128).T)
    return w


def _flat(waits):
    out = []
    for wv in waits:
        if wv is None:
            continue
        if isinstance(wv, (list, tuple)) and (len(wv) == 0 or not isinstance(wv[0], str)):
            out.extend(x for x in _flat(wv))
        else:
            out.append(wv)
    return out


class Prog:
    def __init__(self, nc, es):
        self.nc = nc
        self.es = es
        self.ops = {e: [] for e in ENGINES}
        self.sems = {}
        self.counts = {}
        self.last_wait = {e: {} for e in ENGINES}
        self.prev_op = {}
        for name in ("T", "V", "S", "G"):
            self.sems[name] = es.enter_context(nc.semaphore(name))
            self.counts[name] = 0

    def stream(self, name):
        if name not in self.sems:
            self.sems[name] = self.es.enter_context(self.nc.semaphore(name))
            self.counts[name] = 0
        return name

    def emit(self, eng, fn, waits=(), inc=None, n=1):
        wlist = []
        lw = self.last_wait[eng]
        allw = list(_flat(waits))
        if fn is not None and eng in ("vector", "scalar"):
            prev = self.prev_op.get(eng)
            if prev is not None:
                allw.append(prev)
        for (s, v) in allw:
            if lw.get(s, 0) >= v:
                continue
            lw[s] = v
            wlist.append((s, v))
        tok = None
        if inc is not None:
            self.counts[inc] += n
            tok = (inc, self.counts[inc])
            if fn is not None:
                self.prev_op[eng] = tok
        self.ops[eng].append((fn, tuple(wlist), (inc, n) if inc else None))
        return tok

    def barrier(self):
        toks = [(s, v) for s, v in self.counts.items() if v > 0]
        for e in ENGINES:
            self.emit(e, None, waits=toks)

    def now(self, sem):
        return (sem, self.counts[sem])

    def install(self, block):
        for ename in ENGINES:
            ops = self.ops[ename]
            sems = self.sems

            def body(eng, ops=ops):
                for fn, waits, inc in ops:
                    for (s, v) in waits:
                        eng.wait_ge(sems[s], v)
                    inst = fn(eng) if fn is not None else None
                    if inc is not None:
                        s, n = inc
                        assert inst is not None
                        inst.then_inc(sems[s], n)
            getattr(block, ename)(body)


class Arena:
    def __init__(self, handle, total_bytes):
        self.base = handle.ap()
        self.free_list = [(0, total_bytes)]
        self.live = {}

    def alloc(self, name, nbytes, align=128):
        assert name not in self.live
        for i, (off, sz) in enumerate(self.free_list):
            aoff = (off + align - 1) // align * align
            pad = aoff - off
            if sz >= nbytes + pad:
                pieces = []
                if pad:
                    pieces.append((off, pad))
                rest = sz - nbytes - pad
                if rest:
                    pieces.append((aoff + nbytes, rest))
                self.free_list[i:i + 1] = pieces
                self.live[name] = (aoff, nbytes)
                return
        raise MemoryError(f"arena OOM: {name} needs {nbytes}; free={self.free_list}")

    def free(self, name):
        off, sz = self.live.pop(name)
        fl = sorted(self.free_list + [(off, sz)])
        merged = []
        for o, s in fl:
            if merged and merged[-1][0] + merged[-1][1] == o:
                merged[-1] = (merged[-1][0], merged[-1][1] + s)
            else:
                merged.append((o, s))
        self.free_list = merged

    def view(self, name, part0, nparts, elem0, nelems, dtype):
        off, sz = self.live[name]
        esz = 4 if dtype == F32 else 2
        assert (elem0 + nelems) * esz <= sz, (name, elem0, nelems, sz)
        bb = self.base.bitcast(dtype)
        st = off // esz
        return bb[part0:part0 + nparts, st + elem0: st + elem0 + nelems]


def build_nc(upto="all", debug=()):
    nc = bass.Bass("TRN2", target_bir_lowering=False)
    LEVELS = {"p0": 0, "p1": 1, "attn": 2, "ln1": 3, "ffn0": 4, "ffn1": 5, "all": 6}
    LVL = LEVELS[upto]
    dbgset = dict(debug)

    X = nc.dram_tensor("x", [T, D], F32, kind="ExternalInput").ap()
    OUT = nc.dram_tensor("out", [T, D], F32, kind="ExternalOutput").ap()
    dins = {}

    def din(name, shape, dt=F32):
        dins[name] = nc.dram_tensor(name, list(shape), dt, kind="ExternalInput").ap()
    for nm, shp in [("wq", (D, D)), ("wk", (D, D)), ("wv", (D, D)), ("wo", (D, D)),
                    ("bq", (1, D)), ("bk", (1, D)), ("bv", (1, D)),
                    ("bo_c", (128, NF)),
                    ("ident_f32", (128, 128)), ("g_cos", (128, T)), ("g_sin", (128, T)),
                    ("x_aq", (128, T)), ("x_sq", (128, T)), ("x_ak", (128, T)),
                    ("x_sk", (128, T))]:
        din(nm, shp)
    din("ident_bf", (128, 128), BF16)
    din("tri_u", (128, 128), BF16)
    din("tri_l", (128, 128), BF16)
    for l in range(2):
        din(f"l{l}_wqk", (D, D)); din(f"l{l}_wv", (D, 512))
        din(f"l{l}_qs", (128, 1)); din(f"l{l}_ks", (128, 1))
        din(f"l{l}_wo", (512, D))
        din(f"l{l}_lng", (128, NF)); din(f"l{l}_lnb", (128, NF))
        din(f"l{l}_w1", (D, 2 * HIDP)); din(f"l{l}_b1c", (128, 2 * NHC))
        din(f"l{l}_w2", (HIDP, D)); din(f"l{l}_b2c", (128, NF))

    spill_x = nc.dram_tensor("spill_x", [NF, 128, T], F32, kind="Internal").ap()
    spill_x2 = nc.dram_tensor("spill_x2", [NF, 128, T], F32, kind="Internal").ap()
    spill_xf = nc.dram_tensor("spill_xf", [NF, 128, T], F32, kind="Internal").ap()
    dbg = {}
    for name, spec in dbgset.items():
        shape, ddt = (spec, F32) if not (isinstance(spec[-1], str)) else (spec[0], BF16)
        if isinstance(spec, tuple) and len(spec) == 2 and isinstance(spec[1], str):
            shape, ddt = spec[0], (BF16 if spec[1] == "bf16" else F32)
        dbg[name] = nc.dram_tensor(name, list(shape), ddt, kind="ExternalOutput").ap()

    es = contextlib.ExitStack()
    with es:
        ARENA_BYTES = 180 * 1024
        arena_h = es.enter_context(nc.sbuf_tensor("ARENA", [128, ARENA_BYTES // 4], F32))
        ar = Arena(arena_h, ARENA_BYTES)

        def sb_(name, shape, dt=F32):
            return es.enter_context(nc.sbuf_tensor(name, shape, dt))
        ident_f = sb_("ident_f", [128, 128])
        ident_b = sb_("ident_b", [128, 128], BF16)
        tri_u = sb_("tri_u_s", [128, 128], BF16)
        tri_l = sb_("tri_l_s", [128, 128], BF16)
        ones_r = sb_("ones_r", [65, 512])
        ones_cb = sb_("ones_cb", [128, 1], BF16)
        eps_ln = sb_("eps_ln", [1, 1])
        brow = sb_("brow", [1, D], BF16)
        brow2 = sb_("brow2", [1, D], BF16)
        ones_rb = sb_("ones_rb", [1, 512], BF16)
        b1c_s = sb_("b1c_s", [128, 2 * NHC])
        b2c_s = sb_("b2c_s", [128, NF])
        bo_c_s = sb_("bo_c_s", [128, NF])
        qs_c = sb_("qs_c", [128, 1])
        ks_c = sb_("ks_c", [128, 1])
        lng_c = sb_("lng_c", [128, NF])
        lnb_c = sb_("lnb_c", [128, NF])
        m_f = sb_("m_f", [128, 2])
        m_bf = sb_("m_bf", [128, 2], BF16)
        rowbuf = sb_("rowbuf", [128, T])
        rowb2 = sb_("rowb2", [1, T])

        PGA = es.enter_context(nc.psum_tensor("PGA", [128, T], F32))
        PGB = es.enter_context(nc.psum_tensor("PGB", [128, T], F32))

        p = Prog(nc, es)
        p.sems["STX"] = es.enter_context(nc.semaphore("STORE"))
        p.counts["STX"] = 0
        taps = {}

        def dma(eng, out, in_, waits=(), stream=None):
            assert stream is not None
            p.stream(stream)
            return p.emit(eng, lambda e, o=out, i=in_: e.dma_start(out=o, in_=i),
                          waits=waits, inc=stream, n=16)

        def mm(out, lhsT, rhs, start, stop, waits=(), inc="T", tpos=None):
            return p.emit("tensor",
                          lambda e, o=out, lt=lhsT, r=rhs, s1=start, s2=stop, tp=tpos:
                          e.matmul(o, lt, r, start=s1, stop=s2, tile_position=tp),
                          waits=waits, inc=inc)

        def vop(fn, waits=(), inc="V"):
            return p.emit("vector", fn, waits=waits, inc=inc)

        def sop(fn, waits=(), inc="S"):
            return p.emit("scalar", fn, waits=waits, inc=inc)

        def gop(fn, waits=(), inc="G"):
            return p.emit("gpsimd", fn, waits=waits, inc=inc)

        def tap(name, ap_list, waits):
            if name not in dbgset:
                return
            dr = dbg[name]
            if ap_list == "dram_xf":
                for i in range(NF):
                    p.emit("sync", lambda e, o=dr[i], i2=spill_xf[i]:
                           e.dma_start(out=o, in_=i2), waits=waits, inc="STX", n=16)
            else:
                for i, ap in enumerate(ap_list):
                    p.emit("sync", lambda e, o=dr[i], i2=ap: e.dma_start(out=o, in_=i2),
                           waits=waits, inc="STX", n=16)
            p.emit("sync", None, waits=[("STX", p.counts["STX"])])
            p.barrier()


        tok_c = {}
        for nm, dst in [("ident_f32", ident_f), ("ident_bf", ident_b),
                        ("tri_u", tri_u), ("tri_l", tri_l)]:
            tok_c[nm] = dma("sync", dst[:], dins[nm], stream="cst")
        vop(lambda e: e.memset(eps_ln[:], 1e-5))
        tok_ones = vop(lambda e: e.memset(ones_r[:], 1.0))
        tok_onesb = vop(lambda e: e.memset(ones_rb[:], 1.0))
        tok_ones_cb = vop(lambda e: e.memset(ones_cb[:], 1.0))

        ar.alloc("XR", 64 * 1024)
        ar.alloc("XBF", 32 * 1024)
        ar.alloc("CC", 8 * 1024)
        ar.alloc("CS", 8 * 1024)
        ar.alloc("G0", 8 * 1024)
        ar.alloc("G1", 8 * 1024)
        ar.alloc("XTS", 8 * 1024)
        ar.alloc("QKS", 8 * 1024)

        XR = [ar.view("XR", 0, 128, c * T, T, F32) for c in range(NF)]
        XBF = [ar.view("XBF", 0, 128, c * T, T, BF16) for c in range(NF)]
        CCOS = ar.view("CC", 0, 128, 0, T, F32)
        CSIN = ar.view("CS", 0, 128, 0, T, F32)
        G0 = ar.view("G0", 0, 128, 0, T, F32)
        G1 = ar.view("G1", 0, 128, 0, T, F32)
        XT = [ar.view("XTS", 0, 128, i * D, D, F32) for i in range(2)]

        tok_c["cos"] = dma("sync", CCOS, dins["g_cos"], stream="cst")
        tok_c["sin"] = dma("sync", CSIN, dins["g_sin"], stream="cst")
        for _k in list(tok_c):
            tok_c[_k] = ("cst", p.counts["cst"])

        def wslice(gw, fc, o0, no):
            return gw[:, fc * D + o0: fc * D + o0 + no]

        ev_tok = [None] * NT
        bankfree = [None] * 4
        tv = ts = None
        for i in range(NT):
            b = i % 2
            tl = dma("sync", XT[b], X[i * 128:(i + 1) * 128, :],
                     waits=[ev_tok[i - 2] if i >= 2 else None], stream=f"xt{b}")
            for fc in range(NF):
                bank = (i * NF + fc) % 4
                ps = PGA[:, bank * 512: bank * 512 + 128]
                tmm = mm(ps, XT[b][:, fc * 128:(fc + 1) * 128], ident_f[:], True, True,
                         waits=[tl, tok_c["ident_f32"], bankfree[bank]])
                tv = vop(lambda e, ps=ps, d=XR[fc], c0=i * 128:
                         e.tensor_copy(d[:, c0:c0 + 128], ps), waits=[tmm])
                ts = sop(lambda e, ps=ps, d=XBF[fc], c0=i * 128:
                         e.copy(d[:, c0:c0 + 128], ps), waits=[tmm, tv])
                bankfree[bank] = [tv, ts]
            ev_tok[i] = [tv, ts]
        t_xdone = [tv, ts]

        for fc in range(NF):
            dma("sync", spill_x[fc], XR[fc], waits=[t_xdone], stream="spx")
        tok_xspill = ("spx", p.counts["spx"])

        if LVL == 0:
            tap("XR", XR, [t_xdone])
            tap("XBF", XBF, [t_xdone])

        p.barrier()

        if LVL >= 1:
            ar.free("XR")
            ar.free("XTS")
            ar.alloc("QT", 32 * 1024)
            ar.alloc("KT", 32 * 1024)
            QT = [ar.view("QT", 0, 128, c * T, T, BF16) for c in range(NF)]
            KT = [ar.view("KT", 0, 128, c * T, T, BF16) for c in range(NF)]
            QKS = [ar.view("QKS", 0, 128, i * D, D, BF16) for i in range(4)]

            def load_colblock(dram, col0, slot, waits=()):
                src = dram[:, col0:col0 + 128].rearrange("(c p) m -> p c m", p=128)
                dst = QKS[slot].rearrange("p (c m) -> p c m", c=NF)
                return dma("gpsimd", dst, src, waits=waits, stream=f"qks{slot}")

            slot_state = {"i": 0, "cons": [None, None, None, None]}

            def qk_proj(dst, wdram, browx, tok_b, grp_free):
                outs = []
                t2 = None
                for oc in range(NF):
                    sl = slot_state["i"] % 4
                    slot_state["i"] += 1
                    tok_w = load_colblock(wdram, oc * 128, sl,
                                          waits=[slot_state["cons"][sl]])
                    g = oc % 2
                    PG = PGA if g == 0 else PGB
                    tmm = None
                    for tc in range(4):
                        mm(PG[:, tc * 512:(tc + 1) * 512], browx[:, oc * 128:(oc + 1) * 128],
                           ones_rb[:], True, False,
                           waits=[tok_b, tok_w, grp_free[g], tok_onesb])
                    for fc in range(NF):
                        for tc in range(4):
                            tmm = mm(PG[:, tc * 512:(tc + 1) * 512],
                                     QKS[sl][:, fc * 128:(fc + 1) * 128],
                                     XBF[fc][:, tc * 512:(tc + 1) * 512],
                                     False, fc == NF - 1)
                    slot_state["cons"][sl] = tmm
                    t1 = vop(lambda e, PG=PG: e.tensor_mul(G0[:], PG[:], CCOS),
                             waits=[tmm, tok_c["cos"]])
                    t2 = vop(lambda e, PG=PG: e.stream_shuffle(G1[:], PG[:], SHUF_SWAP))
                    vop(lambda e: e.tensor_mul(G1[:], G1[:], CSIN), waits=[tok_c["sin"]])
                    t4 = vop(lambda e, d=dst[oc]: e.tensor_add(d[:], G0[:], G1[:]))
                    grp_free[g] = t2
                    outs.append(t4)
                return outs, grp_free

            tok_bq2 = dma("gpsimd", brow2[:], dins["bq"], stream="brow2")
            tok_qt, gfree = qk_proj(QT, dins["wq"], brow2, tok_bq2, [None, None])
            tok_bk2 = dma("gpsimd", brow2[:], dins["bk"], waits=[p.now("T")], stream="brow2")
            tok_kt, gfree = qk_proj(KT, dins["wk"], brow2, tok_bk2, gfree)
            tok_rope_done = p.now("V")

            ar.free("CC")
            ar.free("CS")
            ar.free("QKS")
            ar.alloc("GW", 16 * 1024)
            for i in range(NT):
                ar.alloc(f"V{i}", 1040 * 2)
            VT = [ar.view(f"V{i}", 0, 128, 0, 1040, BF16) for i in range(NT)]
            GWv = ar.view("GW", 0, 128, 0, NF * D, BF16)

            tok_wv = dma("gpsimd", GWv.rearrange("p (c m) -> p c m", c=NF),
                         dins["wv"].rearrange("(c p) m -> p c m", p=128),
                         waits=[tok_rope_done, p.now("T")], stream="gwv")
            tok_bv2 = dma("gpsimd", brow[:], dins["bv"], stream="brow")
            tok_vones = []
            for i in range(NT):
                tok_vones.append(vop(lambda e, v=VT[i]: e.memset(v[:, 64:1040:65], 1.0),
                                     waits=[tok_rope_done]))

            vproj_done = []
            half_free = [gfree[0], gfree[1]]
            tmm = None
            for i in range(NT):
                hh = i % 2
                ps = PGA[:, hh * 1024:(hh + 1) * 1024] if hh == 0 else PGB[:, 0:1024]
                mm(ps[:, 0:512], ones_rb[0:1, 0:128], brow[:, 0:512], True, False,
                   waits=[tok_bv2, half_free[hh], tok_wv, tok_ones])
                mm(ps[:, 512:1024], ones_rb[0:1, 0:128], brow[:, 512:1024], True, False)
                for fc in range(NF):
                    mm(ps[:, 0:512], XBF[fc][:, i * 128:(i + 1) * 128],
                       wslice(GWv, fc, 0, 512), False, fc == NF - 1)
                    tmm = mm(ps[:, 512:1024], XBF[fc][:, i * 128:(i + 1) * 128],
                             wslice(GWv, fc, 512, 512), False, fc == NF - 1)
                tev = vop(lambda e, ps=ps, v=VT[i]:
                          e.tensor_copy(v[:, 0:1040].rearrange("p (h c) -> p h c", h=16)[:, :, 0:64],
                                        ps.rearrange("p (h c) -> p h c", h=16)),
                          waits=[tmm, tok_vones[i]])
                half_free[hh] = tev
                vproj_done.append(tev)
            ar.free("GW")

            tap("QT", QT, tok_qt)
            tap("KT", KT, tok_kt)
            tap("VT", VT, vproj_done)

            if LVL >= 2:
                p.barrier()
                ar.free("XBF")

                ar.alloc("PT", 8 * 1024)
                ar.alloc("EXT", 8 * 1024)
                ar.alloc("EXT2", 8 * 1024)
                ar.alloc("SQG", 8 * 1024)
                for c in range(NF):
                    ar.alloc(f"OT{c}", 4 * 1024)
                PT = [ar.view("PT", 0, 128, i * T, T, BF16) for i in range(2)]
                QX = ar.view("EXT", 0, 65, 0, T, BF16)
                KX = ar.view("EXT", 0, 65, T, T, BF16)
                OT = [ar.view(f"OT{c}", 0, 128, 0, T, BF16) for c in range(NF)]
                ROW_L = [rowbuf[0:1, :], rowb2[0:1, :]]
                GB64 = [ar.view("G0", 0, 64, 0, T, F32), ar.view("G1", 0, 64, 0, T, F32)]
                QX2 = [QX, ar.view("EXT2", 0, 65, 0, T, BF16)]
                KX2 = [KX, ar.view("EXT2", 0, 65, T, T, BF16)]
                PT2 = [[PT[0][:, 0:1024], PT[0][:, 1024:2048]],
                       [PT[1][:, 0:1024], PT[1][:, 1024:2048]]]
                SQG = [ar.view("SQG", 0, 128, i * T, T, BF16) for i in range(2)]
                tok_kxones = [vop(lambda e: e.memset(KX2[0][64:65, :], 1.0)),
                              vop(lambda e: e.memset(KX2[1][64:65, :], 1.0))]

                ot_done = []
                lane_state = [
                    dict(prev_norm=None, prev_bcopy=None, prev_exp=None,
                         pt_av=[None, None], sq_cons=None),
                    dict(prev_norm=None, prev_bcopy=None, prev_exp=None,
                         pt_av=[None, None], sq_cons=None),
                ]

                def gen_head(h, lane):
                    st = lane_state[lane]
                    PG = PGA if lane == 0 else PGB
                    ch, sub = h // 2, (h % 2) * 64
                    QXl, KXl = QX2[lane], KX2[lane]
                    tq = vop(lambda e: e.tensor_copy(QXl[0:64, :], QT[ch][sub:sub + 64, :]),
                             waits=[tok_qt[ch], st["prev_exp"]])
                    tk = vop(lambda e: e.tensor_copy(KXl[0:64, :], KT[ch][sub:sub + 64, :]),
                             waits=[tok_kt[ch], tok_kxones[lane], st["prev_exp"]])
                    sqg = SQG[lane]
                    tsq = sop(lambda e: e.activation(sqg[sub:sub + 64, :],
                                                     QT[ch][sub:sub + 64, :], AF.Square),
                              waits=[tok_qt[ch], st["sq_cons"]])
                    ssq = []
                    for tc in range(4):
                        ssq.append(mm(PG[0:1, tc * 512:(tc + 1) * 512],
                                      ones_cb[sub:sub + 64, :],
                                      sqg[sub:sub + 64, tc * 512:(tc + 1) * 512],
                                      True, True,
                                      waits=[tsq, tok_ones_cb,
                                             st["prev_bcopy"], st["prev_norm"]],
                                      tpos=(sub, 0)))
                    yield
                    tsk = sop(lambda e: e.activation(sqg[sub:sub + 64, :],
                                                     KT[ch][sub:sub + 64, :], AF.Square),
                              waits=[tok_kt[ch]] + ssq)
                    ssk = []
                    for tc in range(4):
                        ssk.append(mm(PG[32:33, tc * 512:(tc + 1) * 512],
                                      ones_cb[sub:sub + 64, :],
                                      sqg[sub:sub + 64, tc * 512:(tc + 1) * 512],
                                      True, True, waits=[tsk], tpos=(sub, 32)))
                    st["sq_cons"] = ssk[-1]
                    yield
                    mk2 = m_f[0:1, lane:lane + 1]
                    tmx = vop(lambda e, PG=PG, mk2=mk2:
                              e.reduce_max(mk2, PG[32:33, :], axis=AX.X), waits=ssk)
                    tsx = sop(lambda e, mk2=mk2: e.activation(mk2, mk2, AF.Sqrt),
                              waits=[tmx])
                    tnegmk = vop(lambda e, mk2=mk2: e.tensor_scalar_mul(mk2, mk2, -1.0),
                                 waits=[tsx])
                    rslot = ROW_L[lane]
                    tr1 = sop(lambda e, PG=PG, rslot=rslot:
                              e.activation(rslot, PG[0:1, :], AF.Sqrt), waits=ssq)
                    srow = sop(lambda e, rslot=rslot, mk2=mk2, QXl=QXl:
                               e.activation(QXl[64:65, :], rslot, AF.Copy, scale=mk2),
                               waits=[tr1, tnegmk, tq])
                    scal_mrow = srow
                    prev_max = [ssq[-1], ssk[-1], tr1, tmx]
                    yield
                    norm_toks = []
                    for qh in range(2):
                        q0 = qh * 1024
                        rot_free = prev_max if qh == 0 else norm_toks[-1]
                        avtoks = []
                        tav = None
                        for kt_i in range(NT):
                            ptb = PT2[kt_i % 2][lane]
                            mms = []
                            for qc in range(2):
                                tmm = mm(PG[:, qc * 512:(qc + 1) * 512],
                                         KXl[:, kt_i * 128:(kt_i + 1) * 128],
                                         QXl[:, q0 + qc * 512:q0 + (qc + 1) * 512],
                                         True, True,
                                         waits=[tk, scal_mrow, rot_free])
                                mms.append(tmm)
                            texp = sop(lambda e, ptb=ptb, PG=PG:
                                       e.activation(ptb[:], PG[:, 0:1024], AF.Exp,
                                                    scale=GSCALE),
                                       waits=mms + [st["pt_av"][kt_i % 2]])
                            rot_free = texp
                            yield
                            for qc in range(2):
                                tav = mm(PG[0:65, 1024 + qc * 512:1024 + (qc + 1) * 512],
                                         VT[kt_i][:, h * 65:(h + 1) * 65],
                                         ptb[:, qc * 512:(qc + 1) * 512],
                                         kt_i == 0, kt_i == NT - 1,
                                         waits=[texp, vproj_done[kt_i],
                                                (prev_max if qh == 0 else norm_toks[-1])
                                                if kt_i == 0 else None])
                                avtoks.append(tav)
                            st["pt_av"][kt_i % 2] = tav
                            yield
                        sop(lambda e, PG=PG, q0=q0, lane=lane:
                            e.activation(ROW_L[lane][0:1, q0:q0 + 1024],
                                         PG[64:65, 1024:2048], AF.Ln),
                            waits=avtoks[-2:])
                        trc = sop(lambda e, q0=q0, lane=lane:
                                  e.activation(ROW_L[lane][0:1, q0:q0 + 1024],
                                               ROW_L[lane][0:1, q0:q0 + 1024],
                                               AF.Exp, scale=-1.0))
                        yield
                        yield
                        tb = []
                        for qc in range(2):
                            tb.append(mm(PG[0:64, qc * 512:(qc + 1) * 512], ones_r[0:1, 0:64],
                                         ROW_L[lane][0:1, q0 + qc * 512:q0 + (qc + 1) * 512],
                                         True, True, waits=[trc, rot_free]))
                        tbc = sop(lambda e, PG=PG, lane=lane:
                                  e.copy(GB64[lane][:, 0:1024], PG[0:64, 0:1024]), waits=tb)
                        tnm = vop(lambda e, PG=PG, lane=lane, q0=q0:
                                  e.tensor_mul(OT[ch][sub:sub + 64, q0:q0 + 1024],
                                               PG[0:64, 1024:2048], GB64[lane][:, 0:1024]),
                                  waits=[tbc] + avtoks[-2:])
                        norm_toks.append([tbc, tnm])
                        yield
                    st["prev_bcopy"] = norm_toks[-1][0]
                    st["prev_norm"] = norm_toks[-1][1]
                    st["prev_exp"] = rot_free
                    ot_done.append(norm_toks[-1][1])

                active = {0: gen_head(0, 0), 1: gen_head(1, 1)}
                nxt = [2, 3]
                while active:
                    for lane in (0, 1):
                        g = active.get(lane)
                        if g is None:
                            continue
                        try:
                            next(g)
                        except StopIteration:
                            if nxt[lane] < NH:
                                active[lane] = gen_head(nxt[lane], lane)
                                nxt[lane] += 2
                            else:
                                del active[lane]

                tap("OT", OT, ot_done)

                if LVL >= 3:
                    p.barrier()
                    for i in range(NT):
                        ar.free(f"V{i}")
                    ar.free("PT")
                    ar.free("EXT")
                    ar.free("EXT2")
                    ar.free("SQG")
                    ar.free("QT")
                    ar.free("KT")
                    for c in range(NF):
                        ar.alloc(f"XRb{c}", 8 * 1024)
                    ar.alloc("WOS", 4 * 1024)
                    XR2 = [ar.view(f"XRb{c}", 0, 128, 0, T, F32) for c in range(NF)]
                    WOS = [ar.view("WOS", 0, 128, i * D, D, BF16) for i in range(2)]
                    wos_cons = [None, None]

                    tok_bo = dma("gpsimd", bo_c_s[:], dins["bo_c"], stream="boc")
                    for c in range(NF):
                        dma("sync", XR2[c], spill_x[c], waits=[tok_xspill], stream="xrl")
                    tok_xrl = [("xrl", p.counts["xrl"])] * NF

                    grp = [None, None]
                    resid_done = []
                    for oc in range(NF):
                        g = oc % 2
                        PG = PGA if g == 0 else PGB
                        sl = oc % 2
                        tok_wo = dma("gpsimd",
                                     WOS[sl].rearrange("p (c m) -> p c m", c=NF),
                                     dins["wo"][:, oc * 128:(oc + 1) * 128]
                                     .rearrange("(c p) m -> p c m", p=128),
                                     waits=[wos_cons[sl]], stream=f"wos{sl}")
                        tmm = None
                        for fc in range(NF):
                            for tc in range(4):
                                tmm = mm(PG[:, tc * 512:(tc + 1) * 512],
                                         WOS[sl][:, fc * 128:(fc + 1) * 128],
                                         OT[fc][:, tc * 512:(tc + 1) * 512],
                                         fc == 0, fc == NF - 1,
                                         waits=[ot_done[-1], tok_wo] +
                                               ([grp[g]] if fc == 0 else []))
                        wos_cons[sl] = tmm
                        tadd = vop(lambda e, c=oc, PG=PG, oc=oc:
                                   e.scalar_tensor_tensor(
                                       XR2[c][:], PG[:], bo_c_s[:, oc:oc + 1],
                                       XR2[c][:], ALU.add, ALU.add),
                                   waits=[tmm, tok_xrl[oc], tok_bo])
                        grp[g] = tadd
                        resid_done.append(tadd)
                    for c in range(NF):
                        ar.alloc(f"ZBc{c}", 4 * 1024)
                    ar.alloc("SQ", 4 * 1024)
                    ar.free("WOS")
                    for c in range(NF):
                        ar.free(f"OT{c}")
                    ZB = [ar.view(f"ZBc{c}", 0, 128, 0, T, BF16) for c in range(NF)]
                    SQ = ar.view("SQ", 0, 128, 0, T, BF16)

                    def ln_stats(xr_list, zb_list, waits_per_chunk, pg_free):
                        tok_sq_mm = None
                        zb_toks = []
                        for fc in range(NF):
                            if fc % 2 == 0:
                                tz = sop(lambda e, zf=zb_list[fc], xf=xr_list[fc]:
                                         e.copy(zf[:], xf[:]),
                                         waits=[waits_per_chunk[fc]])
                            else:
                                tz = gop(lambda e, zf=zb_list[fc], xf=xr_list[fc]:
                                         e.tensor_copy(zf[:], xf[:]),
                                         waits=[waits_per_chunk[fc]])
                            zb_toks.append(tz)
                            tsq = sop(lambda e, zf=zb_list[fc]: e.activation(SQ, zf[:], AF.Square),
                                      waits=[tz, tok_sq_mm])
                            for tc in range(4):
                                mm(PGA[0:1, tc * 512:(tc + 1) * 512], ones_cb[:],
                                   zb_list[fc][:, tc * 512:(tc + 1) * 512],
                                   fc == 0, fc == NF - 1,
                                   waits=[tz, tok_ones_cb, pg_free if fc == 0 else None])
                                tok_sq_mm = mm(PGA[32:33, tc * 512:(tc + 1) * 512], ones_cb[:],
                                               SQ[:, tc * 512:(tc + 1) * 512],
                                               fc == 0, fc == NF - 1, waits=[tsq],
                                               tpos=(0, 32))
                        return tok_sq_mm, zb_toks

                    tok_stat, _ = ln_stats(XR2, ZB, resid_done, grp[0])

                    s_mu = sop(lambda e: e.activation(rowbuf[0:1, :], PGA[0:1, :], AF.Copy,
                                                      scale=1.0 / D), waits=[tok_stat])
                    s_e2 = sop(lambda e: e.activation(G1[0:1, :], PGA[32:33, :], AF.Copy,
                                                      scale=1.0 / D))
                    s_m2 = sop(lambda e: e.activation(G0[0:1, :], rowbuf[0:1, :], AF.Square))
                    v_var = vop(lambda e: e.tensor_sub(G1[0:1, :], G1[0:1, :],
                                                       G0[0:1, :]), waits=[s_m2, s_e2])
                    s_ln = sop(lambda e: e.activation(G0[0:1, :], G1[0:1, :], AF.Ln,
                                                      bias=eps_ln[:]), waits=[v_var])
                    s_rstd = sop(lambda e: e.activation(G1[0:1, :], G0[0:1, :], AF.Exp,
                                                        scale=-0.5))

                    tmb = []
                    for tc in range(4):
                        tmb.append(mm(PGA[:, tc * 512:(tc + 1) * 512], ones_r[0:1, 0:128],
                                      rowbuf[0:1, tc * 512:(tc + 1) * 512], True, True,
                                      waits=[s_e2, s_mu]))
                    trb = []
                    for tc in range(4):
                        trb.append(mm(PGB[:, tc * 512:(tc + 1) * 512], ones_r[0:1, 0:128],
                                      G1[0:1, tc * 512:(tc + 1) * 512], True, True,
                                      waits=[s_rstd, grp[1]]))
                    x2_done = []
                    tok_x2sp = []
                    for fc in range(NF):
                        t1 = vop(lambda e, c=fc: e.tensor_sub(G0[:], XR2[c][:], PGA[:]),
                                 waits=tmb + [tok_stat, s_rstd])
                        t2 = vop(lambda e, c=fc: e.tensor_mul(XR2[c][:], G0[:], PGB[:]),
                                 waits=trb)
                        x2_done.append(t2)
                        tok_x2sp.append(dma("sync", spill_x2[fc], XR2[fc], waits=[t2],
                                            stream=f"sp2{fc}"))
                    tap("X2", XR2, x2_done)

                    if LVL >= 4:
                        for c in range(NF):
                            ar.free(f"ZBc{c}")
                        ar.free("SQ")

                        def make_xs(XS, XF, xf_ready, xg=()):
                            xs_tok = []
                            for c in range(NF):
                                if c < 4:
                                    tt = sop(lambda e, c=c: e.copy(XS[c][:], XF[c][:]),
                                             waits=[xf_ready[c]] + list(xg))
                                else:
                                    tv = vop(lambda e, c=c: e.memset(XS[c][:, 0:1], 0.0),
                                             waits=[xf_ready[c]] + list(xg))
                                    tt = sop(lambda e, c=c:
                                             e.copy(XS[c][:, 1:T], XF[c][:, 0:T - 1]),
                                             waits=[xf_ready[c]] + list(xg))
                                    tt = [tt, tv]
                                xs_tok.append(tt)
                            return xs_tok

                        def emit_ffn_layer(l, XF, xf_ready, pg_init, region_free=(),
                                           xf_src=None, final=False, x2_src=None):
                            LWd = dins
                            ar.alloc("SQL", 4 * 1024)
                            ar.alloc("SQL2", 4 * 1024)
                            ar.alloc("QKS2", 4 * 1024)
                            ar.alloc("GWL", 8 * 1024)
                            pre_guard = [p.now("T"), p.now("S")]
                            t_qs = dma("sync", qs_c[:], LWd[f"l{l}_qs"],
                                       waits=[p.now("V")], stream="qs")
                            t_ks = dma("sync", ks_c[:], LWd[f"l{l}_ks"],
                                       waits=[p.now("V")], stream="ks")
                            t_lng = dma("sync", lng_c[:], LWd[f"l{l}_lng"],
                                        waits=[p.now("V")], stream="lng")
                            t_lnb = dma("sync", lnb_c[:], LWd[f"l{l}_lnb"],
                                        waits=[p.now("V")], stream="lnb")

                            for c in range(NF):
                                ar.alloc(f"XS{c}", 4 * 1024)
                            XS = [ar.view(f"XS{c}", 0, 128, 0, T, BF16) for c in range(NF)]
                            xs_tok = make_xs(XS, XF, xf_ready,
                                             xg=[p.now("T")] if l > 0 else []) \
                                + list(region_free)
                            for c in range(NF):
                                ar.free(f"XFl{c}" if l > 0 else f"XRb{c}")

                            for c in range(NF):
                                ar.alloc(f"QKT{c}", 4 * 1024)
                            for i in range(NT):
                                ar.alloc(f"vL{i}", 520 * 2)
                            for i in range(4):
                                ar.alloc(f"NRM{i}", 8 * 1024)
                            ar.alloc("ROWS0", 8 * 1024)
                            ar.alloc("ROWS1", 8 * 1024)
                            QKT = [ar.view(f"QKT{c}", 0, 128, 0, T, BF16) for c in range(NF)]
                            VL = [ar.view(f"vL{i}", 0, 128, 0, 520, BF16) for i in range(NT)]
                            SQLv = [ar.view("SQL", 0, 128, 0, T, BF16),
                                    ar.view("SQL2", 0, 128, 0, T, BF16)]
                            QKS2 = [ar.view("QKS2", 0, 128, i * D, D, BF16) for i in range(2)]
                            GWL = ar.view("GWL", 0, 128, 0, NF * 512, BF16)
                            NRM = [ar.view(f"NRM{i}", 0, 128, 0, T, F32) for i in range(4)]
                            ROWS = [ar.view("ROWS0", 0, 128, 0, T, F32),
                                    ar.view("ROWS1", 0, 128, 0, T, F32)]
                            tok_wvl = dma("gpsimd",
                                          GWL.rearrange("p (c m) -> p c m", c=NF),
                                          LWd[f"l{l}_wv"].rearrange("(c p) m -> p c m", p=128),
                                          waits=pre_guard, stream="gwv")

                            s1state = [dict(pg_free=pg_init[0], cons=None),
                                       dict(pg_free=pg_init[1], cons=None)]
                            raw_done = [None] * NF
                            rows_done = [None] * NF

                            def s1gen(oc, lane):
                                st = s1state[lane]
                                PG = PGA if lane == 0 else PGB
                                SQl = SQLv[lane]
                                nb = 64 * (oc % 2)
                                rA = NRM[oc // 2][nb:nb + 1, :]
                                rB = NRM[oc // 2][nb + 32:nb + 33, :]
                                tok_w = dma("gpsimd",
                                            QKS2[lane].rearrange("p (c m) -> p c m", c=NF),
                                            LWd[f"l{l}_wqk"][:, oc * 128:(oc + 1) * 128]
                                            .rearrange("(c p) m -> p c m", p=128),
                                            waits=[st["cons"]] + pre_guard,
                                            stream=f"qks2{lane}")
                                tmm = None
                                for fc in range(NF):
                                    for tc in range(4):
                                        tmm = mm(PG[:, tc * 512:(tc + 1) * 512],
                                                 QKS2[lane][:, fc * 128:(fc + 1) * 128],
                                                 XS[fc][:, tc * 512:(tc + 1) * 512],
                                                 fc == 0, fc == NF - 1,
                                                 waits=[tok_w, st["pg_free"], xs_tok[fc]] +
                                                       (list(region_free)
                                                        if oc < 2 and fc == 0 else []))
                                st["cons"] = tmm
                                yield
                                tcpy = sop(lambda e, d=QKT[oc], PG=PG: e.copy(d[:], PG[:]),
                                           waits=[tmm])
                                tsq = sop(lambda e, PG=PG, SQl=SQl:
                                          e.activation(SQl[:], PG[:], AF.Square))
                                raw_done[oc] = tcpy
                                yield
                                ssA, ssB = [], []
                                for tc in range(4):
                                    ssA.append(mm(PG[0:1, tc * 512:(tc + 1) * 512],
                                                  ones_cb[0:64, :],
                                                  SQl[0:64, tc * 512:(tc + 1) * 512],
                                                  True, True, waits=[tsq, tok_ones_cb]))
                                    ssB.append(mm(PG[32:33, tc * 512:(tc + 1) * 512],
                                                  ones_cb[64:128, :],
                                                  SQl[64:128, tc * 512:(tc + 1) * 512],
                                                  True, True, tpos=(64, 32)))
                                yield
                                vA = vop(lambda e, rA=rA, PG=PG: e.tensor_scalar_max(
                                    rA, PG[0:1, :], 1e-24), waits=ssA)
                                vB = vop(lambda e, rB=rB, PG=PG: e.tensor_scalar_max(
                                    rB, PG[32:33, :], 1e-24), waits=ssB)
                                sop(lambda e, rA=rA: e.activation(rA, rA, AF.Ln), waits=[vA])
                                sA2 = sop(lambda e, rA=rA: e.activation(rA, rA, AF.Exp,
                                                                        scale=-0.5))
                                sop(lambda e, rB=rB: e.activation(rB, rB, AF.Ln), waits=[vB])
                                sB2 = sop(lambda e, rB=rB: e.activation(rB, rB, AF.Exp,
                                                                        scale=-0.5))
                                st["pg_free"] = [vA, vB]
                                rows_done[oc] = [sA2, sB2]

                            s1act = {0: s1gen(0, 0), 1: s1gen(1, 1)}
                            s1nxt = [2, 3]
                            while s1act:
                                for lane in (0, 1):
                                    g = s1act.get(lane)
                                    if g is None:
                                        continue
                                    try:
                                        next(g)
                                    except StopIteration:
                                        if s1nxt[lane] < NF:
                                            s1act[lane] = s1gen(s1nxt[lane], lane)
                                            s1nxt[lane] += 2
                                        else:
                                            del s1act[lane]

                            vl_ones = [vop(lambda e, v=VL[i]: e.memset(v[:, 64:520:65], 1.0),
                                           waits=xs_tok) for i in range(NT)]
                            vl_done = []
                            vhf = [s1state[0]["pg_free"], s1state[1]["pg_free"]]
                            for i in range(NT):
                                hh = i % 2
                                ps = PGA[:, 0:512] if hh == 0 else PGB[:, 0:512]
                                tmm = None
                                for fc in range(NF):
                                    tmm = mm(ps, XS[fc][:, i * 128:(i + 1) * 128],
                                             GWL[:, fc * 512:(fc + 1) * 512],
                                             fc == 0, fc == NF - 1,
                                             waits=[tok_wvl, vhf[hh]])
                                tev = vop(lambda e, ps=ps, v=VL[i]:
                                          e.tensor_copy(
                                              v[:, 0:520].rearrange("p (h c) -> p h c", h=8)[:, :, 0:64],
                                              ps.rearrange("p (h c) -> p h c", h=8)),
                                          waits=[tmm, vl_ones[i]])
                                vhf[hh] = tev
                                vl_done.append(tev)

                            for c in range(NF):
                                ar.free(f"XS{c}")
                            ar.free("GWL")
                            for nmx in ("XCaq", "XCsq", "XCak", "XCsk"):
                                ar.alloc(nmx, 8 * 1024)
                            XAQ = ar.view("XCaq", 0, 128, 0, T, F32)
                            XSQ = ar.view("XCsq", 0, 128, 0, T, F32)
                            XAK = ar.view("XCak", 0, 128, 0, T, F32)
                            XSK = ar.view("XCsk", 0, 128, 0, T, F32)
                            xcg = [p.now("T"), p.now("V")]
                            dma("sync", XAQ, dins["x_aq"], waits=xcg, stream="xc")
                            dma("sync", XSQ, dins["x_sq"], stream="xc")
                            dma("sync", XAK, dins["x_ak"], stream="xc")
                            dma("sync", XSK, dins["x_sk"], stream="xc")
                            tok_xc = ("xc", p.counts["xc"])

                            qk_outs = []
                            s2_pg = [vhf[0], vhf[1]]
                            rows_cons = [None, None]
                            for oc in range(NF):
                                lane = oc % 2
                                is_q = oc < 4
                                PG = PGA if lane == 0 else PGB
                                RW = ROWS[lane]
                                nb = 64 * (oc % 2)
                                rA = NRM[oc // 2][nb:nb + 1, :]
                                rB = NRM[oc // 2][nb + 32:nb + 33, :]
                                scpA = sop(lambda e, rA=rA, RW=RW: e.copy(RW[0:1, :], rA),
                                           waits=[rows_done[oc][0], rows_cons[lane]])
                                scpB = sop(lambda e, rB=rB, RW=RW: e.copy(RW[64:65, :], rB),
                                           waits=[rows_done[oc][1]])
                                btoks = []
                                for tc in range(4):
                                    btoks.append(mm(PG[0:64, tc * 512:(tc + 1) * 512],
                                                    ones_r[0:1, 0:64],
                                                    RW[0:1, tc * 512:(tc + 1) * 512],
                                                    True, True,
                                                    waits=[scpA, s2_pg[lane], tok_ones]))
                                    btoks.append(mm(PG[64:128, tc * 512:(tc + 1) * 512],
                                                    ones_r[64:65, 0:64],
                                                    RW[64:65, tc * 512:(tc + 1) * 512],
                                                    True, True, tpos=(64, 64),
                                                    waits=[scpB]))
                                rows_cons[lane] = btoks
                                scol = qs_c if is_q else ks_c
                                tqn = vop(lambda e, sc=scol, s=QKT[oc], PG=PG:
                                          e.scalar_tensor_tensor(G0[:], s[:], sc[:], PG[:],
                                                                 ALU.mult, ALU.mult),
                                          waits=btoks + [t_qs, t_ks, raw_done[oc]])
                                s2_pg[lane] = tqn
                                CA = XAQ if is_q else XAK
                                CS2 = XSQ if is_q else XSK
                                tsh = vop(lambda e: e.stream_shuffle(rowbuf[:], G0[:],
                                                                     SHUF_SWAP))
                                tm1 = vop(lambda e, CA=CA: e.tensor_mul(G1[:], G0[:], CA),
                                          waits=[tok_xc])
                                tm2 = gop(lambda e, CS2=CS2:
                                          e.tensor_mul(rowbuf[:], rowbuf[:], CS2),
                                          waits=[tsh, tok_xc])
                                t4 = vop(lambda e, d=QKT[oc]: e.tensor_add(d[:], G1[:],
                                                                           rowbuf[:]),
                                         waits=[tm1, tm2])
                                qk_outs.append(t4)
                            grpf = [s2_pg[0], s2_pg[1]]

                            tap(f"QKT_l{l}", QKT, qk_outs)
                            tap(f"VL_l{l}", VL, vl_done)
                            for nmx in ("XCaq", "XCsq", "XCak", "XCsk"):
                                ar.free(nmx)
                            ar.free("SQL"); ar.free("SQL2"); ar.free("QKS2")
                            for i in range(4):
                                ar.free(f"NRM{i}")
                            ar.free("ROWS0"); ar.free("ROWS1")
                            for c in range(4):
                                ar.alloc(f"OTL{c}", 4 * 1024)
                            ar.alloc("EXTL", 8 * 1024)
                            ar.alloc("EXTL2", 8 * 1024)
                            ar.alloc("PTL", 4 * 1024)
                            ar.alloc("LWO", 8 * 1024)
                            for c in range(NF):
                                ar.alloc(f"XFl{c}", 8 * 1024)
                            XFl = [ar.view(f"XFl{c}", 0, 128, 0, T, F32)
                                   for c in range(NF)]
                            xsrc_buf, xsrc_tok = xf_src
                            xg = [p.now("T"), p.now("V"), p.now("S")]
                            tok_xlr = [dma("sync", XFl[c], xsrc_buf[c],
                                           waits=[xsrc_tok[c]] + xg,
                                           stream=f"xlr{l}{c}")
                                       for c in range(NF)]
                            OTL = [ar.view(f"OTL{c}", 0, 128, 0, T, BF16) for c in range(4)]
                            QXL = [ar.view("EXTL", 0, 65, 0, T, BF16),
                                   ar.view("EXTL2", 0, 65, 0, T, BF16)]
                            KXL = [ar.view("EXTL", 0, 65, T, T, BF16),
                                   ar.view("EXTL2", 0, 65, T, T, BF16)]
                            PTL = [[ar.view("PTL", 0, 128, i * 512, 512, BF16)
                                    for i in range(2)],
                                   [ar.view("PTL", 0, 128, (2 + i) * 512, 512, BF16)
                                    for i in range(2)]]
                            LWO = ar.view("LWO", 0, 128, 0, 4 * D, BF16)
                            tok_wol = dma("gpsimd",
                                          LWO.rearrange("p (c m) -> p c m", c=4),
                                          LWd[f"l{l}_wo"].rearrange("(c p) m -> p c m", p=128),
                                          waits=[p.now("T"), p.now("V"), p.now("S")],
                                          stream="lwo")
                            tok_kxo = [vop(lambda e: e.memset(KXL[0][64:65, :], 1.0),
                                           waits=[p.now("S")]),
                                       vop(lambda e: e.memset(KXL[1][64:65, :], 1.0),
                                           waits=[p.now("S")])]

                            otl_done = []
                            lstate = [dict(prev_norm=None, prev_bcopy=None, prev_exp=None,
                                           pt_av=[None, None]),
                                      dict(prev_norm=None, prev_bcopy=None, prev_exp=None,
                                           pt_av=[None, None])]

                            def lgen_head(h, lane):
                                st = lstate[lane]
                                PG = PGA if lane == 0 else PGB
                                ch, sub = h // 2, (h % 2) * 64
                                QXl, KXl = QXL[lane], KXL[lane]
                                mf, mb = m_f[:, lane:lane + 1], m_bf[:, lane:lane + 1]
                                tqx = vop(lambda e: e.tensor_copy(QXl[0:64, :],
                                                                  QKT[ch][sub:sub + 64, :]),
                                          waits=[qk_outs[ch], st["prev_norm"],
                                                 st["prev_exp"]])
                                tkx = vop(lambda e: e.tensor_copy(KXl[0:64, :],
                                                                  QKT[4 + ch][sub:sub + 64, :]),
                                          waits=[qk_outs[4 + ch], tok_kxo[lane]])
                                for w in range(NW):
                                    kcol0 = (w - 1) * 512 if w > 0 else 0
                                    kw = 1024 if w > 0 else 512
                                    nkt = kw // 128
                                    scal_mrow = st.get("scal_mrow")
                                    prev_max = st.get("win_free")
                                    for qt_i in range(4):
                                        q0 = w * 512 + qt_i * 128
                                        ktoks = []
                                        for kc in range(kw // 512):
                                            tmm = mm(PG[:, kc * 512:(kc + 1) * 512],
                                                     QKT[ch][sub:sub + 64, q0:q0 + 128],
                                                     QKT[4 + ch][sub:sub + 64,
                                                                 kcol0 + kc * 512:kcol0 + (kc + 1) * 512],
                                                     True, True,
                                                     waits=[tqx, tkx, prev_max])
                                            ktoks.append(tmm)
                                        tmax = vop(lambda e, kw=kw, PG=PG, mf=mf:
                                                   e.reduce_max(mf, PG[:, 0:kw:MAXSTRIDE],
                                                                axis=AX.X), waits=ktoks)
                                        tneg = vop(lambda e, mf=mf, mb=mb:
                                                   e.tensor_scalar_mul(mb, mf, -1.0))
                                        yield
                                        tmr = mm(PG[0:1, 1536 + qt_i * 128:1536 + (qt_i + 1) * 128],
                                                 mb, ident_b[:], True, True,
                                                 waits=[tneg, tok_c["ident_bf"], scal_mrow,
                                                        st["pt_av"][0] if (w == 0 and qt_i == 0)
                                                        else None,
                                                        st["pt_av"][1] if (w == 0 and qt_i == 0)
                                                        else None])
                                        scal_mrow = sop(lambda e, q0=q0, qt_i=qt_i, PG=PG,
                                                        QXl=QXl:
                                                        e.copy(QXl[64:65, q0:q0 + 128],
                                                               PG[0:1, 1536 + qt_i * 128:1536 + (qt_i + 1) * 128]),
                                                        waits=[tmr, tqx])
                                        prev_max = [tmax, scal_mrow]
                                        yield
                                    st["scal_mrow"] = scal_mrow
                                    avtoks = []
                                    rot_free = [prev_max, prev_max]
                                    own0 = (kw - 512) // 128
                                    order = [(own0 + i2, i2 * 128, 512, "own") for i2 in range(4)]
                                    order += [(i2, 0, (i2 + 1) * 128, "prev")
                                              for i2 in range(nkt - 4)]
                                    for idx, (kt, qlo, qhi, kind) in enumerate(order):
                                        width = qhi - qlo
                                        rb = idx % 2
                                        ptb = PTL[lane][rb]
                                        tmm = mm(PG[:, rb * 512:rb * 512 + width],
                                                 KXl[:, kcol0 + kt * 128:kcol0 + (kt + 1) * 128],
                                                 QXl[:, w * 512 + qlo:w * 512 + qhi],
                                                 True, True,
                                                 waits=[tkx, scal_mrow, rot_free[rb]])
                                        texp = sop(lambda e, rb=rb, ptb=ptb, qlo=qlo,
                                                   width=width, PG=PG:
                                                   e.activation(ptb[:, qlo:qlo + width],
                                                                PG[:, rb * 512:rb * 512 + width],
                                                                AF.Exp),
                                                   waits=[tmm, st["pt_av"][rb]])
                                        rot_free[rb] = texp
                                        if kind == "own":
                                            tmask = vop(lambda e, ptb=ptb, qlo=qlo:
                                                        e.tensor_mul(ptb[:, qlo:qlo + 128],
                                                                     ptb[:, qlo:qlo + 128],
                                                                     tri_u[:]),
                                                        waits=[texp, tok_c["tri_u"]])
                                        else:
                                            tmask = vop(lambda e, ptb=ptb, qhi=qhi:
                                                        e.tensor_mul(ptb[:, qhi - 128:qhi],
                                                                     ptb[:, qhi - 128:qhi],
                                                                     tri_l[:]),
                                                        waits=[texp, tok_c["tri_l"]])
                                        yield
                                        gkt = kcol0 // 128 + kt
                                        tav = mm(PG[0:65, 1024 + qlo:1024 + qhi],
                                                 VL[gkt][:, h * 65:(h + 1) * 65],
                                                 ptb[:, qlo:qhi],
                                                 idx == 0, idx == len(order) - 1,
                                                 waits=[tmask, vl_done[gkt],
                                                        st["prev_norm"] if idx == 0 else None])
                                        st["pt_av"][rb] = tav
                                        avtoks.append(tav)
                                        yield
                                    sop(lambda e, PG=PG, lane=lane:
                                        e.activation(ROW_L[lane][0:1, 0:512],
                                                     PG[64:65, 1024:1536], AF.Ln),
                                        waits=avtoks)
                                    trc = sop(lambda e, lane=lane:
                                              e.activation(ROW_L[lane][0:1, 0:512],
                                                           ROW_L[lane][0:1, 0:512],
                                                           AF.Exp, scale=-1.0))
                                    yield
                                    yield
                                    tb = mm(PG[0:64, 0:512], ones_r[0:1, 0:64],
                                            ROW_L[lane][0:1, 0:512], True, True,
                                            waits=[trc, rot_free[0]])
                                    tbc = sop(lambda e, PG=PG, lane=lane:
                                              e.copy(GB64[lane][:, 0:512], PG[0:64, 0:512]),
                                              waits=[tb])
                                    tnm = vop(lambda e, PG=PG, lane=lane, w=w:
                                              e.tensor_mul(OTL[ch][sub:sub + 64,
                                                               w * 512:(w + 1) * 512],
                                                           PG[0:64, 1024:1536],
                                                           GB64[lane][:, 0:512]),
                                              waits=[tbc] + avtoks)
                                    st["prev_bcopy"] = tbc
                                    st["prev_norm"] = tnm
                                    st["prev_exp"] = rot_free[1]
                                    st["win_free"] = [rot_free[0], rot_free[1], tbc, tnm]
                                    otl_done.append(tnm)
                                    yield

                            lact = {0: lgen_head(0, 0), 1: lgen_head(1, 1)}
                            lnxt = [2, 3]
                            while lact:
                                for lane in (0, 1):
                                    g = lact.get(lane)
                                    if g is None:
                                        continue
                                    try:
                                        next(g)
                                    except StopIteration:
                                        if lnxt[lane] < LH:
                                            lact[lane] = lgen_head(lnxt[lane], lane)
                                            lnxt[lane] += 2
                                        else:
                                            del lact[lane]
                            prev_norm = [lstate[0]["prev_norm"], lstate[1]["prev_norm"]]
                            prev_bcopy = [lstate[0]["prev_bcopy"], lstate[1]["prev_bcopy"]]

                            tap(f"OTL_l{l}", OTL, otl_done)
                            attn_res = []
                            grpw = [lstate[0]["win_free"], lstate[1]["win_free"]]
                            for oc in range(NF):
                                g = oc % 2
                                PG = PGA if g == 0 else PGB
                                tmm = None
                                for fc in range(4):
                                    for tc in range(4):
                                        tmm = mm(PG[:, tc * 512:(tc + 1) * 512],
                                                 LWO[:, fc * D + oc * 128:fc * D + (oc + 1) * 128],
                                                 OTL[fc][:, tc * 512:(tc + 1) * 512],
                                                 fc == 0, fc == 3,
                                                 waits=[tok_wol, grpw[g], otl_done[-1],
                                                        prev_norm])
                                tadd = vop(lambda e, PG=PG, c=oc:
                                           e.tensor_add(XFl[c][:], XFl[c][:], PG[:]),
                                           waits=[tmm, tok_xlr[oc]])
                                grpw[g] = tadd
                                attn_res.append(tadd)

                            for c in range(NF):
                                ar.free(f"QKT{c}")
                            for i in range(NT):
                                ar.free(f"vL{i}")
                            ar.free("EXTL"); ar.free("EXTL2")
                            ar.free("PTL")
                            for c in range(NF):
                                ar.alloc(f"HB{c}", 4 * 1024)
                            for c in range(NF):
                                ar.alloc(f"ZB2c{c}", 4 * 1024)
                            ar.alloc("SQ2", 4 * 1024)
                            for c in range(4):
                                ar.free(f"OTL{c}")
                            ar.free("LWO")
                            HB = [ar.view(f"HB{c}", 0, 128, 0, T, BF16) for c in range(NF)]
                            ZB2 = [ar.view(f"ZB2c{c}", 0, 128, 0, T, BF16) for c in range(NF)]
                            SQ2 = ar.view("SQ2", 0, 128, 0, T, BF16)

                            tok_sq2 = None
                            tok_s_l = None
                            for fc in range(NF):
                                if fc % 2 == 0:
                                    tz = sop(lambda e, c=fc: e.copy(ZB2[c][:], XFl[c][:]),
                                             waits=[attn_res[fc]])
                                else:
                                    tz = gop(lambda e, c=fc:
                                             e.tensor_copy(ZB2[c][:], XFl[c][:]),
                                             waits=[attn_res[fc]])
                                tsq = sop(lambda e, c=fc: e.activation(SQ2, ZB2[c][:], AF.Square),
                                          waits=[tz, tok_sq2])
                                half = 0 if fc < 4 else 1
                                srow = 0 if half == 0 else 32
                                for tc in range(4):
                                    tok_s_l = mm(PGA[srow:srow + 1, tc * 512:(tc + 1) * 512],
                                                 ones_cb[:],
                                                 ZB2[fc][:, tc * 512:(tc + 1) * 512],
                                                 fc % 4 == 0, fc % 4 == 3,
                                                 waits=[tz] + ([grpw[0]] if fc == 0 else []),
                                                 tpos=None if half == 0 else (0, 32))
                                    tok_sq2 = mm(PGB[srow:srow + 1, tc * 512:(tc + 1) * 512],
                                                 ones_cb[:],
                                                 SQ2[:, tc * 512:(tc + 1) * 512],
                                                 fc % 4 == 0, fc % 4 == 3,
                                                 waits=[tsq] + ([grpw[1]] if fc == 0 else []),
                                                 tpos=None if half == 0 else (0, 32))
                            tok_stat2 = [tok_sq2, tok_s_l]
                            r_s2 = sop(lambda e: e.copy(rowbuf[0:1, :], PGA[32:33, :]),
                                       waits=[tok_stat2])
                            v_mu = vop(lambda e: e.tensor_add(G0[0:1, 1:T], PGA[0:1, 1:T],
                                                              rowbuf[0:1, 0:T - 1]),
                                       waits=[r_s2, tok_stat2])
                            v_mu0 = vop(lambda e: e.tensor_copy(G0[0:1, 0:1], PGA[0:1, 0:1]))
                            s_mu = sop(lambda e: e.activation(G1[0:1, :], G0[0:1, :], AF.Copy,
                                                              scale=1.0 / D),
                                       waits=[v_mu, v_mu0])
                            r_q2 = sop(lambda e: e.copy(rowbuf[0:1, :], PGB[32:33, :]))
                            v_e2 = vop(lambda e: e.tensor_add(G0[0:1, 1:T], PGB[0:1, 1:T],
                                                              rowbuf[0:1, 0:T - 1]),
                                       waits=[r_q2, s_mu])
                            v_e20 = vop(lambda e: e.tensor_copy(G0[0:1, 0:1], PGB[0:1, 0:1]))
                            s_e2 = sop(lambda e: e.activation(rowbuf[0:1, :], G0[0:1, :], AF.Copy,
                                                              scale=1.0 / D),
                                       waits=[v_e2, v_e20])
                            s_m2 = sop(lambda e: e.activation(G0[0:1, :], G1[0:1, :], AF.Square))
                            v_var = vop(lambda e: e.tensor_sub(rowbuf[0:1, :], rowbuf[0:1, :],
                                                               G0[0:1, :]), waits=[s_m2, s_e2])
                            s_ln2 = sop(lambda e: e.activation(G0[0:1, :], rowbuf[0:1, :], AF.Ln,
                                                               bias=eps_ln[:]), waits=[v_var])
                            s_rstd2 = sop(lambda e: e.activation(rowbuf[0:1, :], G0[0:1, :],
                                                                 AF.Exp, scale=-0.5))
                            tmb2, trb2 = [], []
                            tmb2, trb2 = [], []
                            for tc in range(4):
                                tmb2.append(mm(PGA[:, tc * 512:(tc + 1) * 512], ones_r[0:1, 0:128],
                                               G1[0:1, tc * 512:(tc + 1) * 512], True, True,
                                               waits=[s_mu, s_e2, s_m2, r_s2, s_rstd2]))
                                trb2.append(mm(PGB[:, tc * 512:(tc + 1) * 512], ones_r[0:1, 0:128],
                                               rowbuf[0:1, tc * 512:(tc + 1) * 512], True, True,
                                               waits=[s_rstd2, r_q2]))
                            hb_done = []
                            for fc in range(NF):
                                if fc < 4:
                                    t1 = vop(lambda e, c=fc: e.tensor_sub(G0[:], XFl[c][:], PGA[:]),
                                             waits=tmb2 + [s_rstd2, s_ln2, attn_res[fc]])
                                else:
                                    vop(lambda e: e.tensor_scalar_mul(G0[:, 0:1], PGA[:, 0:1], -1.0),
                                        waits=tmb2 + [s_rstd2, s_ln2, attn_res[fc]])
                                    t1 = vop(lambda e, c=fc: e.tensor_sub(G0[:, 1:T],
                                                                          XFl[c][:, 0:T - 1],
                                                                          PGA[:, 1:T]))
                                t2 = vop(lambda e: e.tensor_mul(G1[:], G0[:], PGB[:]),
                                         waits=trb2)
                                t3 = vop(lambda e, c=fc: e.tensor_scalar(
                                    HB[c][:], G1[:], lng_c[:, c:c + 1], lnb_c[:, c:c + 1],
                                    ALU.mult, ALU.add),
                                    waits=[t2, t_lng, t_lnb])
                                hb_done.append([t2, t3])
                            tap(f"HB_l{l}", HB, hb_done)
                            for c in range(NF):
                                ar.free(f"ZB2c{c}")
                            ar.free("SQ2")

                            ar.alloc("W1S", 4 * 1024)
                            ar.alloc("W2Sa", 6 * 1024)
                            ar.alloc("W2Sb", 6 * 1024)
                            W1S = [ar.view("W1S", 0, 128, i * D, D, BF16) for i in range(2)]
                            NB0 = NHC // 2
                            wstate = {"cons": [[t3, t3], [t3, t3]]}
                            tok_b1c = dma("gpsimd", b1c_s[:], LWd[f"l{l}_b1c"],
                                          waits=[p.now("V"), p.now("S")], stream="b1c")
                            tok_b2c = dma("gpsimd", b2c_s[:], LWd[f"l{l}_b2c"],
                                          waits=[p.now("V")], stream="b2c")
                            w2state = [[p.now("T"), p.now("S")]] * 2
                            mlp_res = []
                            mlp_add = []
                            pgl = [hb_done[-1], hb_done[-1]]
                            for blk in range(2):
                                ccs = list(range(0, NB0)) if blk == 0 else \
                                      list(range(NB0, NHC))
                                nblk = len(ccs)
                                for cc in ccs:
                                    ar.alloc(f"H2c{cc}", 4 * 1024)
                                H2 = {cc: ar.view(f"H2c{cc}", 0, 128, 0, T, BF16)
                                      for cc in ccs}
                                h2_done = {}
                                for cc in ccs:
                                    toks_pair = {}
                                    for ag in range(2):
                                        mcc = 2 * cc + ag
                                        sl = mcc % 2
                                        tok_w1 = dma("gpsimd",
                                                     W1S[sl].rearrange("p (c m) -> p c m", c=NF),
                                                     LWd[f"l{l}_w1"][:, mcc * 128:(mcc + 1) * 128]
                                                     .rearrange("(c p) m -> p c m", p=128),
                                                     waits=[wstate["cons"][sl][0]] +
                                                           (hb_done[-1] if mcc < 2 else []),
                                                     stream=f"w1s{sl}")
                                        PG = PGA if ag == 0 else PGB
                                        tmm = None
                                        for fc in range(NF):
                                            for tc in range(4):
                                                tmm = mm(PG[:, tc * 512:(tc + 1) * 512],
                                                         W1S[sl][:, fc * 128:(fc + 1) * 128],
                                                         HB[fc][:, tc * 512:(tc + 1) * 512],
                                                         fc == 0, fc == NF - 1,
                                                         waits=[hb_done[fc], tok_w1] +
                                                               ([pgl[ag]] if fc == 0
                                                                else []))
                                        wstate["cons"][sl] = [tmm, tmm]
                                        toks_pair[ag] = tmm
                                    tgel = sop(lambda e, cc=cc:
                                               e.activation(G0[:], PGB[:], AF.Gelu,
                                                            bias=b1c_s[:, 2 * cc + 1:
                                                                       2 * cc + 2]),
                                               waits=[toks_pair[1], tok_b1c])
                                    tmul = vop(lambda e, d=H2[cc], cc=cc:
                                               e.scalar_tensor_tensor(
                                                   d[:], PGA[:],
                                                   b1c_s[:, 2 * cc:2 * cc + 1], G0[:],
                                                   ALU.add, ALU.mult),
                                               waits=[tgel, toks_pair[0], tok_b1c])
                                    pgl = [tmul, tgel]
                                    h2_done[cc] = tmul
                                if blk == 1:
                                    for c in range(NF):
                                        ar.free(f"HB{c}")
                                    ar.free("W1S")
                                    if final:
                                        ar.alloc("X2Ra", 8 * 1024)
                                        ar.alloc("X2Rb", 8 * 1024)
                                        X2R = [ar.view("X2Ra", 0, 128, 0, T, F32),
                                               ar.view("X2Rb", 0, 128, 0, T, F32)]
                                        x2_buf, x2_tok = x2_src
                                        x2g = [p.now("T"), p.now("V"), p.now("S")]
                                        x2prev = [None, None]
                                for oq in range(4):
                                    sl = oq % 2
                                    tok_w2 = dma("gpsimd",
                                                 ar.view(f"W2S{'ab'[sl]}", 0, 128, 0,
                                                         nblk * 256, BF16)
                                                 .rearrange("p (c m) -> p c m", c=nblk),
                                                 LWd[f"l{l}_w2"][ccs[0] * 128:
                                                                 (ccs[0] + nblk) * 128,
                                                                 oq * 256:(oq + 1) * 256]
                                                 .rearrange("(c p) m -> p c m", p=128),
                                                 waits=[w2state[sl]], stream=f"w2s{sl}")
                                    W2Sv = ar.view(f"W2S{'ab'[sl]}", 0, 128, 0,
                                                   nblk * 256, BF16)
                                    tmm = {}
                                    for i_f, cc in enumerate(ccs):
                                        for och in range(2):
                                            PG = PGA if och == 0 else PGB
                                            for tc in range(4):
                                                tmm[och] = mm(
                                                    PG[:, tc * 512:(tc + 1) * 512],
                                                    W2Sv[:, i_f * 256 + och * 128:
                                                         i_f * 256 + (och + 1) * 128],
                                                    H2[cc][:, tc * 512:(tc + 1) * 512],
                                                    i_f == 0, i_f == nblk - 1,
                                                    waits=[h2_done[cc], tok_w2] +
                                                          ([pgl[och]] if i_f == 0
                                                           else []))
                                    w2state[sl] = [tmm[0], tmm[1]]
                                    for och in range(2):
                                        oc = oq * 2 + och
                                        PG = PGA if och == 0 else PGB
                                        if blk == 0:
                                            tadd = vop(lambda e, PG=PG, c=oc, oc=oc:
                                                       e.scalar_tensor_tensor(
                                                           XFl[c][:], PG[:],
                                                           b2c_s[:, oc:oc + 1],
                                                           XFl[c][:], ALU.add, ALU.add),
                                                       waits=[tmm[och], tok_b2c])
                                        else:
                                            tadd = vop(lambda e, PG=PG, c=oc:
                                                       e.tensor_add(XFl[c][:],
                                                                    XFl[c][:], PG[:]),
                                                       waits=[tmm[och]])
                                        pgl[och] = tadd
                                        if blk == 1:
                                            mlp_add.append(tadd)
                                            if final:
                                                tld = dma("sync", X2R[oc % 2], x2_buf[oc],
                                                          waits=[x2_tok[oc],
                                                                 x2prev[oc % 2]] + x2g,
                                                          stream=f"x2l{oc % 2}")
                                                tz2 = vop(lambda e, c=oc:
                                                          e.tensor_add(XFl[c][:], XFl[c][:],
                                                                       X2R[c % 2][:]),
                                                          waits=[tld])
                                                x2prev[oc % 2] = tz2
                                                pgl[och] = tadd
                                                mlp_res.append(tz2)
                                            else:
                                                mlp_res.append(
                                                    dma("sync", spill_xf[oc], XFl[oc][:],
                                                        waits=[tadd], stream=f"sxf{oc}"))
                                for cc in ccs:
                                    ar.free(f"H2c{cc}")
                            ar.free("W2Sa")
                            ar.free("W2Sb")
                            if final:
                                ar.free("X2Ra"); ar.free("X2Rb")
                            return mlp_res, [pgl[0], pgl[1]], XFl, mlp_add

                        mlp_res0, pgf, XFl0, mlp_add0 = emit_ffn_layer(
                            0, XR2, x2_done, [x2_done[-1], x2_done[-1]],
                            region_free=list(tok_x2sp),
                            xf_src=(spill_x2, tok_x2sp))
                        tap("XF0", "dram_xf", mlp_res0)

                        if LVL >= 5:
                            mlp_res1, pgf, XFl1, mlp_add1 = emit_ffn_layer(
                                1, XFl0, mlp_add0, pgf,
                                region_free=list(mlp_res0),
                                xf_src=(spill_xf, mlp_res0),
                                final=(LVL >= 6),
                                x2_src=(spill_x2, tok_x2sp))
                            tap("XF1", XFl1, mlp_res1)

                        if LVL >= 6:
                            for c in range(NF):
                                ar.alloc(f"ZB3c{c}", 4 * 1024)
                            ar.alloc("SQ3", 4 * 1024)
                            XZ = XFl1
                            ZB3 = [ar.view(f"ZB3c{c}", 0, 128, 0, T, BF16) for c in range(NF)]
                            SQ3 = ar.view("SQ3", 0, 128, 0, T, BF16)
                            z_done = mlp_res1
                            tguard = [p.now("T"), p.now("V")]

                            tok_sq3 = None
                            for fc in range(NF):
                                if fc % 2 == 0:
                                    tz = sop(lambda e, c=fc: e.copy(ZB3[c][:], XZ[c][:]),
                                             waits=[z_done[fc],
                                                    tguard if fc == 0 else None])
                                else:
                                    tz = gop(lambda e, c=fc:
                                             e.tensor_copy(ZB3[c][:], XZ[c][:]),
                                             waits=[z_done[fc],
                                                    tguard if fc == 1 else None])
                                tsq = sop(lambda e, c=fc: e.activation(SQ3, ZB3[c][:], AF.Square),
                                          waits=[tz, tok_sq3])
                                for tc in range(4):
                                    mm(PGA[0:1, tc * 512:(tc + 1) * 512], ones_cb[:],
                                       ZB3[fc][:, tc * 512:(tc + 1) * 512],
                                       fc == 0, fc == NF - 1,
                                       waits=[tz] + ([pgf[0], p.now("S")]
                                                     if fc == 0 else []))
                                    tok_sq3 = mm(PGA[32:33, tc * 512:(tc + 1) * 512], ones_cb[:],
                                                 SQ3[:, tc * 512:(tc + 1) * 512],
                                                 fc == 0, fc == NF - 1, waits=[tsq],
                                                 tpos=(0, 32))
                            s_mu = sop(lambda e: e.activation(rowbuf[0:1, :], PGA[0:1, :], AF.Copy,
                                                              scale=1.0 / D), waits=[tok_sq3])
                            s_e2 = sop(lambda e: e.activation(G1[0:1, :], PGA[32:33, :], AF.Copy,
                                                              scale=1.0 / D))
                            s_m2 = sop(lambda e: e.activation(G0[0:1, :], rowbuf[0:1, :], AF.Square))
                            v_var = vop(lambda e: e.tensor_sub(G1[0:1, :], G1[0:1, :],
                                                               G0[0:1, :]), waits=[s_m2, s_e2])
                            s_ln = sop(lambda e: e.activation(G0[0:1, :], G1[0:1, :], AF.Ln,
                                                              bias=eps_ln[:]), waits=[v_var])
                            s_rstd = sop(lambda e: e.activation(G1[0:1, :], G0[0:1, :], AF.Exp,
                                                                scale=-0.5))
                            tmb = []
                            for tc in range(4):
                                tmb.append(mm(PGA[:, tc * 512:(tc + 1) * 512], ones_r[0:1, 0:128],
                                              rowbuf[0:1, tc * 512:(tc + 1) * 512], True, True,
                                              waits=[s_e2, s_mu]))
                            trb = []
                            for tc in range(4):
                                trb.append(mm(PGB[:, tc * 512:(tc + 1) * 512], ones_r[0:1, 0:128],
                                              G1[0:1, tc * 512:(tc + 1) * 512], True, True,
                                              waits=[s_rstd, pgf[1]]))
                            out_done = []
                            for fc in range(NF):
                                t1 = vop(lambda e, c=fc: e.tensor_sub(G0[:], XZ[c][:], PGA[:]),
                                         waits=tmb + [tok_sq3, s_rstd])
                                t2 = vop(lambda e, c=fc: e.tensor_mul(XZ[c][:], G0[:], PGB[:]),
                                         waits=trb)
                                out_done.append(t2)

                            otile = [G0, G1]
                            prev_ev = [out_done[-1], out_done[-1]]
                            st_tok = [None, None]
                            for i in range(NT):
                                hh = i % 2
                                ps = PGA[:, 0:1024] if hh == 0 else PGB[:, 0:1024]
                                tmm = None
                                for fc in range(NF):
                                    tmm = mm(ps[:, fc * 128:(fc + 1) * 128],
                                             XZ[fc][:, i * 128:(i + 1) * 128], ident_f[:],
                                             True, True,
                                             waits=[out_done[fc], tok_c["ident_f32"],
                                                    prev_ev[hh]])
                                tev = vop(lambda e, ps=ps, gb=otile[hh]:
                                          e.tensor_copy(gb[:, 0:1024], ps),
                                          waits=[tmm, st_tok[hh]])
                                p.stream(f"sto{hh}")
                                st_tok[hh] = p.emit("sync",
                                       lambda e, i=i, gb=otile[hh]:
                                       e.dma_start(out=OUT[i * 128:(i + 1) * 128, :],
                                                   in_=gb[:, 0:1024]),
                                       waits=[tev], inc=f"sto{hh}", n=16)
                                prev_ev[hh] = tev

                        p.barrier()


        p.emit("sync", None,
               waits=[("STX", p.counts["STX"])] +
                     [(s, p.counts[s]) for s in ("sto0", "sto1") if s in p.counts])
        if "out_dummy" not in dbgset and LVL < 6:
            zs = rowbuf[0:128, 0:D]
            tz = vop(lambda e: e.memset(zs, 0.0))
            for i in range(NT):
                p.emit("sync", lambda e, o=OUT[i * 128:(i + 1) * 128, :], i2=zs:
                       e.dma_start(out=o, in_=i2), waits=[tz], inc="STX", n=16)
            p.emit("sync", None, waits=[("STX", p.counts["STX"])])

        block = es.enter_context(nc.Block())
        p.install(block)

    return nc


def kernel(**inputs):
    consts = host_constants()
    w = prep_weights(inputs)
    nc = build_nc(upto="all")
    x = np.asarray(inputs["L0_attn_x"])
    in_maps = []
    for core in range(8):
        im = {"x": np.ascontiguousarray(x[core])}
        im.update({k: np.asarray(v) for k, v in w.items()})
        im.update(consts)
        in_maps.append(im)
    res = run_bass_kernel_spmd(nc, in_maps, core_ids=list(range(8)))
    out = np.stack([r["out"] for r in res.results], axis=0)
    return out.astype(np.float32)


# version 15
# speedup vs baseline: 1.0109x; 1.0085x over previous
import contextlib
import numpy as np
import ml_dtypes
import concourse.bass as bass
import concourse.mybir as mybir
from concourse.bass_utils import run_bass_kernel_spmd

F32 = mybir.dt.float32
BF16 = mybir.dt.bfloat16
AF = mybir.ActivationFunctionType
ALU = mybir.AluOpType
AX = mybir.AxisListType
BF = ml_dtypes.bfloat16

D = 1024
T = 2048
NT = T // 128
NF = D // 128
NH = 16
HD = 64
GSCALE = HD ** -0.5
LH = 8
WIN = 512
NW = T // WIN
HID = 2730
HIDP = 2816
NHC = HIDP // 128
QK_SCALE = 8.0
MAXSTRIDE = 8

ENGINES = ("sync", "tensor", "vector", "scalar", "gpsimd")
SHUF_SWAP = [i ^ 1 for i in range(32)]


def _pair_perm():
    pp = np.empty(HD, np.int64)
    pp[0::2] = np.arange(32)
    pp[1::2] = np.arange(32) + 32
    return pp


def host_constants():
    c = {}
    c["ident_f32"] = np.eye(128, dtype=np.float32)
    c["ident_bf"] = np.eye(128, dtype=np.float32).astype(BF)
    j = np.arange(32, dtype=np.float64)
    inv = 1.0 / (10000.0 ** (j / 32.0))
    t = np.arange(T, dtype=np.float64)
    ang = t[None, :] * inv[:, None]
    cos = np.cos(ang); sin = np.sin(ang)
    cos64 = np.empty((64, T)); sin64 = np.empty((64, T))
    cos64[0::2] = cos; cos64[1::2] = cos
    sin64[0::2] = -sin; sin64[1::2] = sin
    c["g_cos"] = np.tile(cos64, (2, 1)).astype(np.float32)
    c["g_sin"] = np.tile(sin64, (2, 1)).astype(np.float32)

    half_idx = np.arange(0, HD, 2, dtype=np.float64)
    invf = 1.0 / (10000.0 ** (half_idx / HD))
    angl = t[None, :] * invf[:, None]
    cosl = np.cos(angl); sinl = np.sin(angl)
    base = (half_idx + 0.4 * HD) / (1.4 * HD)
    power = (t - T // 2) / float(WIN // 2)
    scale = base[:, None] ** power[None, :]

    def inter(even, odd):
        out = np.empty((64, T))
        out[0::2] = even; out[1::2] = odd
        return np.tile(out, (2, 1)).astype(np.float32)

    aq = cosl * scale * QK_SCALE; sq = sinl * scale * QK_SCALE
    ak = cosl / scale; sk = sinl / scale
    c["x_aq"] = inter(aq, aq)
    c["x_sq"] = inter(-sq, sq)
    c["x_ak"] = inter(ak, ak)
    c["x_sk"] = inter(-sk, sk)

    kk = np.arange(128)[:, None]; qq = np.arange(128)[None, :]
    c["tri_u"] = (qq >= kk).astype(np.float32).astype(BF)
    c["tri_l"] = (qq <= kk).astype(np.float32).astype(BF)
    return c


def prep_weights(inp):
    w = {}
    perm = _pair_perm()
    gperm = np.concatenate([perm + 64 * h for h in range(NH)])
    w["wq"] = np.ascontiguousarray(np.asarray(inp["L0_attn_wq_w"])[:, gperm])
    w["wk"] = np.ascontiguousarray(np.asarray(inp["L0_attn_wk_w"])[:, gperm])
    w["wv"] = np.ascontiguousarray(np.asarray(inp["L0_attn_wv_w"]))
    w["wo"] = np.ascontiguousarray(np.asarray(inp["L0_attn_wo_w"]))
    swap = np.arange(128) ^ 1
    bq = np.asarray(inp["L0_attn_wq_b"])[gperm].reshape(NF, 128).T
    bk = np.asarray(inp["L0_attn_wk_b"])[gperm].reshape(NF, 128).T
    w["bqk_c"] = np.ascontiguousarray(
        np.concatenate([bq, bq[swap], bk, bk[swap]], axis=1))
    w["bv"] = np.ascontiguousarray(np.asarray(inp["L0_attn_wv_b"]).reshape(1, D))
    w["bo_c"] = np.ascontiguousarray(
        np.asarray(inp["L0_attn_wo_b"]).reshape(NF, 128).T)

    lperm = np.concatenate([perm + 64 * h for h in range(LH)])
    for l in range(2):
        wqkv = np.asarray(inp["L0_ffn_wqkv"][l])
        w[f"l{l}_wqk"] = np.ascontiguousarray(
            np.concatenate([wqkv[:, 0:512][:, lperm], wqkv[:, 512:1024][:, lperm]], axis=1))
        w[f"l{l}_wv"] = np.ascontiguousarray(wqkv[:, 1024:1536])
        w[f"l{l}_qs"] = np.ascontiguousarray(
            np.tile(np.asarray(inp["L0_ffn_q_scale"][l])[perm], 2).reshape(128, 1))
        w[f"l{l}_ks"] = np.ascontiguousarray(
            np.tile(np.asarray(inp["L0_ffn_k_scale"][l])[perm], 2).reshape(128, 1))
        w[f"l{l}_wo"] = np.ascontiguousarray(np.asarray(inp["L0_ffn_wo"][l]))
        w[f"l{l}_lng"] = np.ascontiguousarray(np.asarray(inp["L0_ffn_ln_g"][l]).reshape(NF, 128).T)
        w[f"l{l}_lnb"] = np.ascontiguousarray(np.asarray(inp["L0_ffn_ln_b"][l]).reshape(NF, 128).T)
        w1 = np.asarray(inp["L0_ffn_w1"][l]); b1 = np.asarray(inp["L0_ffn_b1"][l])
        a_w = np.zeros((D, HIDP), np.float32); a_w[:, :HID] = w1[:, :HID]
        g_w = np.zeros((D, HIDP), np.float32); g_w[:, :HID] = w1[:, HID:]
        a_b = np.zeros(HIDP, np.float32); a_b[:HID] = b1[:HID]
        g_b = np.zeros(HIDP, np.float32); g_b[:HID] = b1[HID:]
        w1p = np.empty((D, 2 * HIDP), np.float32)
        b1p = np.empty((1, 2 * HIDP), np.float32)
        for cc in range(NHC):
            s = slice(cc * 128, (cc + 1) * 128)
            w1p[:, (2 * cc) * 128:(2 * cc + 1) * 128] = a_w[:, s]
            w1p[:, (2 * cc + 1) * 128:(2 * cc + 2) * 128] = g_w[:, s]
            b1p[0, (2 * cc) * 128:(2 * cc + 1) * 128] = a_b[s]
            b1p[0, (2 * cc + 1) * 128:(2 * cc + 2) * 128] = g_b[s]
        w[f"l{l}_w1"] = w1p
        w[f"l{l}_b1c"] = np.ascontiguousarray(b1p.reshape(2 * NHC, 128).T)
        w2p = np.zeros((HIDP, D), np.float32)
        w2p[:HID] = np.asarray(inp["L0_ffn_w2"][l])
        w[f"l{l}_w2"] = w2p
        w[f"l{l}_b2c"] = np.ascontiguousarray(
            np.asarray(inp["L0_ffn_b2"][l]).reshape(NF, 128).T)
    return w


def _flat(waits):
    out = []
    for wv in waits:
        if wv is None:
            continue
        if isinstance(wv, (list, tuple)) and (len(wv) == 0 or not isinstance(wv[0], str)):
            out.extend(x for x in _flat(wv))
        else:
            out.append(wv)
    return out


class Prog:
    def __init__(self, nc, es):
        self.nc = nc
        self.es = es
        self.ops = {e: [] for e in ENGINES}
        self.sems = {}
        self.counts = {}
        self.last_wait = {e: {} for e in ENGINES}
        self.prev_op = {}
        for name in ("T", "V", "S", "G"):
            self.sems[name] = es.enter_context(nc.semaphore(name))
            self.counts[name] = 0

    def stream(self, name):
        if name not in self.sems:
            self.sems[name] = self.es.enter_context(self.nc.semaphore(name))
            self.counts[name] = 0
        return name

    def emit(self, eng, fn, waits=(), inc=None, n=1):
        wlist = []
        lw = self.last_wait[eng]
        allw = list(_flat(waits))
        if fn is not None and eng in ("vector", "scalar"):
            prev = self.prev_op.get(eng)
            if prev is not None:
                allw.append(prev)
        for (s, v) in allw:
            if lw.get(s, 0) >= v:
                continue
            lw[s] = v
            wlist.append((s, v))
        tok = None
        if inc is not None:
            self.counts[inc] += n
            tok = (inc, self.counts[inc])
            if fn is not None:
                self.prev_op[eng] = tok
        self.ops[eng].append((fn, tuple(wlist), (inc, n) if inc else None))
        return tok

    def barrier(self):
        toks = [(s, v) for s, v in self.counts.items() if v > 0]
        for e in ENGINES:
            self.emit(e, None, waits=toks)

    def now(self, sem):
        return (sem, self.counts[sem])

    def install(self, block):
        for ename in ENGINES:
            ops = self.ops[ename]
            sems = self.sems

            def body(eng, ops=ops):
                for fn, waits, inc in ops:
                    for (s, v) in waits:
                        eng.wait_ge(sems[s], v)
                    inst = fn(eng) if fn is not None else None
                    if inc is not None:
                        s, n = inc
                        assert inst is not None
                        inst.then_inc(sems[s], n)
            getattr(block, ename)(body)


class Arena:
    def __init__(self, handle, total_bytes):
        self.base = handle.ap()
        self.free_list = [(0, total_bytes)]
        self.live = {}

    def alloc(self, name, nbytes, align=128):
        assert name not in self.live
        for i, (off, sz) in enumerate(self.free_list):
            aoff = (off + align - 1) // align * align
            pad = aoff - off
            if sz >= nbytes + pad:
                pieces = []
                if pad:
                    pieces.append((off, pad))
                rest = sz - nbytes - pad
                if rest:
                    pieces.append((aoff + nbytes, rest))
                self.free_list[i:i + 1] = pieces
                self.live[name] = (aoff, nbytes)
                return
        raise MemoryError(f"arena OOM: {name} needs {nbytes}; free={self.free_list}")

    def free(self, name):
        off, sz = self.live.pop(name)
        fl = sorted(self.free_list + [(off, sz)])
        merged = []
        for o, s in fl:
            if merged and merged[-1][0] + merged[-1][1] == o:
                merged[-1] = (merged[-1][0], merged[-1][1] + s)
            else:
                merged.append((o, s))
        self.free_list = merged

    def view(self, name, part0, nparts, elem0, nelems, dtype):
        off, sz = self.live[name]
        esz = 4 if dtype == F32 else 2
        assert (elem0 + nelems) * esz <= sz, (name, elem0, nelems, sz)
        bb = self.base.bitcast(dtype)
        st = off // esz
        return bb[part0:part0 + nparts, st + elem0: st + elem0 + nelems]


def build_nc(upto="all", debug=()):
    nc = bass.Bass("TRN2", target_bir_lowering=False)
    LEVELS = {"p0": 0, "p1": 1, "attn": 2, "ln1": 3, "ffn0": 4, "ffn1": 5, "all": 6}
    LVL = LEVELS[upto]
    dbgset = dict(debug)

    X = nc.dram_tensor("x", [T, D], F32, kind="ExternalInput").ap()
    OUT = nc.dram_tensor("out", [T, D], F32, kind="ExternalOutput").ap()
    dins = {}

    def din(name, shape, dt=F32):
        dins[name] = nc.dram_tensor(name, list(shape), dt, kind="ExternalInput").ap()
    for nm, shp in [("wq", (D, D)), ("wk", (D, D)), ("wv", (D, D)), ("wo", (D, D)),
                    ("bqk_c", (128, 4 * NF)), ("bv", (1, D)),
                    ("bo_c", (128, NF)),
                    ("ident_f32", (128, 128)), ("g_cos", (128, T)), ("g_sin", (128, T)),
                    ("x_aq", (128, T)), ("x_sq", (128, T)), ("x_ak", (128, T)),
                    ("x_sk", (128, T))]:
        din(nm, shp)
    din("ident_bf", (128, 128), BF16)
    din("tri_u", (128, 128), BF16)
    din("tri_l", (128, 128), BF16)
    for l in range(2):
        din(f"l{l}_wqk", (D, D)); din(f"l{l}_wv", (D, 512))
        din(f"l{l}_qs", (128, 1)); din(f"l{l}_ks", (128, 1))
        din(f"l{l}_wo", (512, D))
        din(f"l{l}_lng", (128, NF)); din(f"l{l}_lnb", (128, NF))
        din(f"l{l}_w1", (D, 2 * HIDP)); din(f"l{l}_b1c", (128, 2 * NHC))
        din(f"l{l}_w2", (HIDP, D)); din(f"l{l}_b2c", (128, NF))

    spill_x = nc.dram_tensor("spill_x", [NF, 128, T], F32, kind="Internal").ap()
    spill_x2 = nc.dram_tensor("spill_x2", [NF, 128, T], F32, kind="Internal").ap()
    spill_xf = nc.dram_tensor("spill_xf", [NF, 128, T], F32, kind="Internal").ap()
    dbg = {}
    for name, spec in dbgset.items():
        shape, ddt = (spec, F32) if not (isinstance(spec[-1], str)) else (spec[0], BF16)
        if isinstance(spec, tuple) and len(spec) == 2 and isinstance(spec[1], str):
            shape, ddt = spec[0], (BF16 if spec[1] == "bf16" else F32)
        dbg[name] = nc.dram_tensor(name, list(shape), ddt, kind="ExternalOutput").ap()

    es = contextlib.ExitStack()
    with es:
        ARENA_BYTES = 180 * 1024
        arena_h = es.enter_context(nc.sbuf_tensor("ARENA", [128, ARENA_BYTES // 4], F32))
        ar = Arena(arena_h, ARENA_BYTES)

        def sb_(name, shape, dt=F32):
            return es.enter_context(nc.sbuf_tensor(name, shape, dt))
        ident_f = sb_("ident_f", [128, 128])
        ident_b = sb_("ident_b", [128, 128], BF16)
        tri_u = sb_("tri_u_s", [128, 128], BF16)
        tri_l = sb_("tri_l_s", [128, 128], BF16)
        ones_r = sb_("ones_r", [65, 512])
        ones_cb = sb_("ones_cb", [128, 1], BF16)
        eps_ln = sb_("eps_ln", [1, 1])
        brow = sb_("brow", [1, D], BF16)
        brow2 = sb_("brow2", [1, D], BF16)
        ones_rb = sb_("ones_rb", [1, 512], BF16)
        bqk_s = sb_("bqk_s", [128, 4 * NF])
        b1c_s = sb_("b1c_s", [128, 2 * NHC])
        b2c_s = sb_("b2c_s", [128, NF])
        bo_c_s = sb_("bo_c_s", [128, NF])
        qs_c = sb_("qs_c", [128, 1])
        ks_c = sb_("ks_c", [128, 1])
        lng_c = sb_("lng_c", [128, NF])
        lnb_c = sb_("lnb_c", [128, NF])
        m_f = sb_("m_f", [128, 2])
        m_bf = sb_("m_bf", [128, 2], BF16)
        rowbuf = sb_("rowbuf", [128, T])
        rowb2 = sb_("rowb2", [1, T])

        PGA = es.enter_context(nc.psum_tensor("PGA", [128, T], F32))
        PGB = es.enter_context(nc.psum_tensor("PGB", [128, T], F32))

        p = Prog(nc, es)
        p.sems["STX"] = es.enter_context(nc.semaphore("STORE"))
        p.counts["STX"] = 0
        taps = {}

        def dma(eng, out, in_, waits=(), stream=None):
            assert stream is not None
            p.stream(stream)
            return p.emit(eng, lambda e, o=out, i=in_: e.dma_start(out=o, in_=i),
                          waits=waits, inc=stream, n=16)

        def mm(out, lhsT, rhs, start, stop, waits=(), inc="T", tpos=None):
            return p.emit("tensor",
                          lambda e, o=out, lt=lhsT, r=rhs, s1=start, s2=stop, tp=tpos:
                          e.matmul(o, lt, r, start=s1, stop=s2, tile_position=tp),
                          waits=waits, inc=inc)

        def vop(fn, waits=(), inc="V"):
            return p.emit("vector", fn, waits=waits, inc=inc)

        def sop(fn, waits=(), inc="S"):
            return p.emit("scalar", fn, waits=waits, inc=inc)

        def gop(fn, waits=(), inc="G"):
            return p.emit("gpsimd", fn, waits=waits, inc=inc)

        def tap(name, ap_list, waits):
            if name not in dbgset:
                return
            dr = dbg[name]
            if ap_list == "dram_xf":
                for i in range(NF):
                    p.emit("sync", lambda e, o=dr[i], i2=spill_xf[i]:
                           e.dma_start(out=o, in_=i2), waits=waits, inc="STX", n=16)
            else:
                for i, ap in enumerate(ap_list):
                    p.emit("sync", lambda e, o=dr[i], i2=ap: e.dma_start(out=o, in_=i2),
                           waits=waits, inc="STX", n=16)
            p.emit("sync", None, waits=[("STX", p.counts["STX"])])
            p.barrier()


        tok_c = {}
        for nm, dst in [("ident_f32", ident_f), ("ident_bf", ident_b),
                        ("tri_u", tri_u), ("tri_l", tri_l)]:
            tok_c[nm] = dma("sync", dst[:], dins[nm], stream="cst")
        vop(lambda e: e.memset(eps_ln[:], 1e-5))
        tok_ones = vop(lambda e: e.memset(ones_r[:], 1.0))
        tok_onesb = vop(lambda e: e.memset(ones_rb[:], 1.0))
        tok_ones_cb = vop(lambda e: e.memset(ones_cb[:], 1.0))

        ar.alloc("XR", 64 * 1024)
        ar.alloc("XBF", 32 * 1024)
        ar.alloc("CC", 8 * 1024)
        ar.alloc("CS", 8 * 1024)
        ar.alloc("G0", 8 * 1024)
        ar.alloc("G1", 8 * 1024)
        ar.alloc("XTS", 8 * 1024)
        ar.alloc("QKS", 8 * 1024)

        XR = [ar.view("XR", 0, 128, c * T, T, F32) for c in range(NF)]
        XBF = [ar.view("XBF", 0, 128, c * T, T, BF16) for c in range(NF)]
        CCOS = ar.view("CC", 0, 128, 0, T, F32)
        CSIN = ar.view("CS", 0, 128, 0, T, F32)
        G0 = ar.view("G0", 0, 128, 0, T, F32)
        G1 = ar.view("G1", 0, 128, 0, T, F32)
        XT = [ar.view("XTS", 0, 128, i * D, D, F32) for i in range(2)]

        tok_c["cos"] = dma("sync", CCOS, dins["g_cos"], stream="cst")
        tok_c["sin"] = dma("sync", CSIN, dins["g_sin"], stream="cst")
        for _k in list(tok_c):
            tok_c[_k] = ("cst", p.counts["cst"])

        def wslice(gw, fc, o0, no):
            return gw[:, fc * D + o0: fc * D + o0 + no]

        ev_tok = [None] * NT
        bankfree = [None] * 4
        tv = ts = None
        for i in range(NT):
            b = i % 2
            tl = dma("sync", XT[b], X[i * 128:(i + 1) * 128, :],
                     waits=[ev_tok[i - 2] if i >= 2 else None], stream=f"xt{b}")
            for fc in range(NF):
                bank = (i * NF + fc) % 4
                ps = PGA[:, bank * 512: bank * 512 + 128]
                tmm = mm(ps, XT[b][:, fc * 128:(fc + 1) * 128], ident_f[:], True, True,
                         waits=[tl, tok_c["ident_f32"], bankfree[bank]])
                tv = vop(lambda e, ps=ps, d=XR[fc], c0=i * 128:
                         e.tensor_copy(d[:, c0:c0 + 128], ps), waits=[tmm])
                ts = sop(lambda e, ps=ps, d=XBF[fc], c0=i * 128:
                         e.copy(d[:, c0:c0 + 128], ps), waits=[tmm, tv])
                bankfree[bank] = [tv, ts]
            ev_tok[i] = [tv, ts]
        t_xdone = [tv, ts]

        for fc in range(NF):
            dma("sync", spill_x[fc], XR[fc], waits=[t_xdone], stream="spx")
        tok_xspill = ("spx", p.counts["spx"])

        if LVL == 0:
            tap("XR", XR, [t_xdone])
            tap("XBF", XBF, [t_xdone])

        p.barrier()

        if LVL >= 1:
            ar.free("XR")
            ar.free("XTS")
            ar.alloc("QT", 32 * 1024)
            ar.alloc("KT", 32 * 1024)
            QT = [ar.view("QT", 0, 128, c * T, T, BF16) for c in range(NF)]
            KT = [ar.view("KT", 0, 128, c * T, T, BF16) for c in range(NF)]
            QKS = [ar.view("QKS", 0, 128, i * D, D, BF16) for i in range(4)]

            def load_colblock(dram, col0, slot, waits=()):
                src = dram[:, col0:col0 + 128].rearrange("(c p) m -> p c m", p=128)
                dst = QKS[slot].rearrange("p (c m) -> p c m", c=NF)
                return dma("gpsimd", dst, src, waits=waits, stream=f"qks{slot}")

            slot_state = {"i": 0, "cons": [None, None, None, None]}

            def qk_proj(dst, wdram, bc0, grp_free):
                outs = []
                t2 = None
                for oc in range(NF):
                    sl = slot_state["i"] % 4
                    slot_state["i"] += 1
                    tok_w = load_colblock(wdram, oc * 128, sl,
                                          waits=[slot_state["cons"][sl]])
                    g = oc % 2
                    PG = PGA if g == 0 else PGB
                    tmm = None
                    for fc in range(NF):
                        for tc in range(4):
                            tmm = mm(PG[:, tc * 512:(tc + 1) * 512],
                                     QKS[sl][:, fc * 128:(fc + 1) * 128],
                                     XBF[fc][:, tc * 512:(tc + 1) * 512],
                                     fc == 0, fc == NF - 1,
                                     waits=[tok_w] + ([grp_free[g]] if fc == 0
                                                      else []))
                    slot_state["cons"][sl] = tmm
                    t1 = vop(lambda e, PG=PG, oc=oc:
                             e.scalar_tensor_tensor(G0[:], PG[:],
                                                    bqk_s[:, bc0 + oc:bc0 + oc + 1],
                                                    CCOS, ALU.add, ALU.mult),
                             waits=[tmm, tok_c["cos"], tok_bqk])
                    t2 = vop(lambda e, PG=PG: e.stream_shuffle(G1[:], PG[:], SHUF_SWAP))
                    vop(lambda e, oc=oc:
                        e.scalar_tensor_tensor(G1[:], G1[:],
                                               bqk_s[:, bc0 + NF + oc:bc0 + NF + oc + 1],
                                               CSIN, ALU.add, ALU.mult),
                        waits=[tok_c["sin"]])
                    t4 = vop(lambda e, d=dst[oc]: e.tensor_add(d[:], G0[:], G1[:]))
                    grp_free[g] = t2
                    outs.append(t4)
                return outs, grp_free

            tok_bqk = dma("gpsimd", bqk_s[:], dins["bqk_c"], stream="brow2")
            tok_qt, gfree = qk_proj(QT, dins["wq"], 0, [None, None])
            tok_kt, gfree = qk_proj(KT, dins["wk"], 2 * NF, gfree)
            tok_rope_done = p.now("V")

            ar.free("CC")
            ar.free("CS")
            ar.free("QKS")
            ar.alloc("GW", 16 * 1024)
            for i in range(NT):
                ar.alloc(f"V{i}", 1040 * 2)
            VT = [ar.view(f"V{i}", 0, 128, 0, 1040, BF16) for i in range(NT)]
            GWv = ar.view("GW", 0, 128, 0, NF * D, BF16)

            tok_wv = dma("gpsimd", GWv.rearrange("p (c m) -> p c m", c=NF),
                         dins["wv"].rearrange("(c p) m -> p c m", p=128),
                         waits=[tok_rope_done, p.now("T")], stream="gwv")
            tok_bv2 = dma("gpsimd", brow[:], dins["bv"], stream="brow")
            tok_vones = []
            for i in range(NT):
                tok_vones.append(vop(lambda e, v=VT[i]: e.memset(v[:, 64:1040:65], 1.0),
                                     waits=[tok_rope_done]))

            vproj_done = []
            half_free = [gfree[0], gfree[1]]
            tmm = None
            for i in range(NT):
                hh = i % 2
                ps = PGA[:, hh * 1024:(hh + 1) * 1024] if hh == 0 else PGB[:, 0:1024]
                mm(ps[:, 0:512], ones_rb[0:1, 0:128], brow[:, 0:512], True, False,
                   waits=[tok_bv2, half_free[hh], tok_wv, tok_ones])
                mm(ps[:, 512:1024], ones_rb[0:1, 0:128], brow[:, 512:1024], True, False)
                for fc in range(NF):
                    mm(ps[:, 0:512], XBF[fc][:, i * 128:(i + 1) * 128],
                       wslice(GWv, fc, 0, 512), False, fc == NF - 1)
                    tmm = mm(ps[:, 512:1024], XBF[fc][:, i * 128:(i + 1) * 128],
                             wslice(GWv, fc, 512, 512), False, fc == NF - 1)
                tev = vop(lambda e, ps=ps, v=VT[i]:
                          e.tensor_copy(v[:, 0:1040].rearrange("p (h c) -> p h c", h=16)[:, :, 0:64],
                                        ps.rearrange("p (h c) -> p h c", h=16)),
                          waits=[tmm, tok_vones[i]])
                half_free[hh] = tev
                vproj_done.append(tev)
            ar.free("GW")

            tap("QT", QT, tok_qt)
            tap("KT", KT, tok_kt)
            tap("VT", VT, vproj_done)

            if LVL >= 2:
                p.barrier()
                ar.free("XBF")

                ar.alloc("PT", 8 * 1024)
                ar.alloc("EXT", 8 * 1024)
                ar.alloc("EXT2", 8 * 1024)
                ar.alloc("SQG", 8 * 1024)
                for c in range(NF):
                    ar.alloc(f"OT{c}", 4 * 1024)
                PT = [ar.view("PT", 0, 128, i * T, T, BF16) for i in range(2)]
                QX = ar.view("EXT", 0, 65, 0, T, BF16)
                KX = ar.view("EXT", 0, 65, T, T, BF16)
                OT = [ar.view(f"OT{c}", 0, 128, 0, T, BF16) for c in range(NF)]
                ROW_L = [rowbuf[0:1, :], rowb2[0:1, :]]
                GB64 = [ar.view("G0", 0, 64, 0, T, F32), ar.view("G1", 0, 64, 0, T, F32)]
                QX2 = [QX, ar.view("EXT2", 0, 65, 0, T, BF16)]
                KX2 = [KX, ar.view("EXT2", 0, 65, T, T, BF16)]
                PT2 = [[PT[0][:, 0:1024], PT[0][:, 1024:2048]],
                       [PT[1][:, 0:1024], PT[1][:, 1024:2048]]]
                SQG = [ar.view("SQG", 0, 128, i * T, T, BF16) for i in range(2)]
                tok_kxones = [vop(lambda e: e.memset(KX2[0][64:65, :], 1.0)),
                              vop(lambda e: e.memset(KX2[1][64:65, :], 1.0))]

                ot_done = []
                lane_state = [
                    dict(prev_norm=None, prev_bcopy=None, prev_exp=None,
                         pt_av=[None, None], sq_cons=None),
                    dict(prev_norm=None, prev_bcopy=None, prev_exp=None,
                         pt_av=[None, None], sq_cons=None),
                ]

                def gen_head(h, lane):
                    st = lane_state[lane]
                    PG = PGA if lane == 0 else PGB
                    ch, sub = h // 2, (h % 2) * 64
                    QXl, KXl = QX2[lane], KX2[lane]
                    tq = vop(lambda e: e.tensor_copy(QXl[0:64, :], QT[ch][sub:sub + 64, :]),
                             waits=[tok_qt[ch], st["prev_exp"]])
                    tk = vop(lambda e: e.tensor_copy(KXl[0:64, :], KT[ch][sub:sub + 64, :]),
                             waits=[tok_kt[ch], tok_kxones[lane], st["prev_exp"]])
                    sqg = SQG[lane]
                    tsq = sop(lambda e: e.activation(sqg[sub:sub + 64, :],
                                                     QT[ch][sub:sub + 64, :], AF.Square),
                              waits=[tok_qt[ch], st["sq_cons"]])
                    ssq = []
                    for tc in range(4):
                        ssq.append(mm(PG[0:1, tc * 512:(tc + 1) * 512],
                                      ones_cb[sub:sub + 64, :],
                                      sqg[sub:sub + 64, tc * 512:(tc + 1) * 512],
                                      True, True,
                                      waits=[tsq, tok_ones_cb,
                                             st["prev_bcopy"], st["prev_norm"]],
                                      tpos=(sub, 0)))
                    yield
                    tsk = sop(lambda e: e.activation(sqg[sub:sub + 64, :],
                                                     KT[ch][sub:sub + 64, :], AF.Square),
                              waits=[tok_kt[ch]] + ssq)
                    ssk = []
                    for tc in range(4):
                        ssk.append(mm(PG[32:33, tc * 512:(tc + 1) * 512],
                                      ones_cb[sub:sub + 64, :],
                                      sqg[sub:sub + 64, tc * 512:(tc + 1) * 512],
                                      True, True, waits=[tsk], tpos=(sub, 32)))
                    st["sq_cons"] = ssk[-1]
                    yield
                    mk2 = m_f[0:1, lane:lane + 1]
                    tmx = vop(lambda e, PG=PG, mk2=mk2:
                              e.reduce_max(mk2, PG[32:33, :], axis=AX.X), waits=ssk)
                    tsx = sop(lambda e, mk2=mk2: e.activation(mk2, mk2, AF.Sqrt),
                              waits=[tmx])
                    tnegmk = vop(lambda e, mk2=mk2: e.tensor_scalar_mul(mk2, mk2, -1.0),
                                 waits=[tsx])
                    rslot = ROW_L[lane]
                    tr1 = sop(lambda e, PG=PG, rslot=rslot:
                              e.activation(rslot, PG[0:1, :], AF.Sqrt), waits=ssq)
                    srow = sop(lambda e, rslot=rslot, mk2=mk2, QXl=QXl:
                               e.activation(QXl[64:65, :], rslot, AF.Copy, scale=mk2),
                               waits=[tr1, tnegmk, tq])
                    scal_mrow = srow
                    prev_max = [ssq[-1], ssk[-1], tr1, tmx]
                    yield
                    norm_toks = []
                    for qh in range(2):
                        q0 = qh * 1024
                        rot_free = prev_max if qh == 0 else norm_toks[-1]
                        avtoks = []
                        tav = None
                        for kt_i in range(NT):
                            ptb = PT2[kt_i % 2][lane]
                            mms = []
                            for qc in range(2):
                                tmm = mm(PG[:, qc * 512:(qc + 1) * 512],
                                         KXl[:, kt_i * 128:(kt_i + 1) * 128],
                                         QXl[:, q0 + qc * 512:q0 + (qc + 1) * 512],
                                         True, True,
                                         waits=[tk, scal_mrow, rot_free])
                                mms.append(tmm)
                            texp = sop(lambda e, ptb=ptb, PG=PG:
                                       e.activation(ptb[:], PG[:, 0:1024], AF.Exp,
                                                    scale=GSCALE),
                                       waits=mms + [st["pt_av"][kt_i % 2]])
                            rot_free = texp
                            yield
                            for qc in range(2):
                                tav = mm(PG[0:65, 1024 + qc * 512:1024 + (qc + 1) * 512],
                                         VT[kt_i][:, h * 65:(h + 1) * 65],
                                         ptb[:, qc * 512:(qc + 1) * 512],
                                         kt_i == 0, kt_i == NT - 1,
                                         waits=[texp, vproj_done[kt_i],
                                                (prev_max if qh == 0 else norm_toks[-1])
                                                if kt_i == 0 else None])
                                avtoks.append(tav)
                            st["pt_av"][kt_i % 2] = tav
                            yield
                        sop(lambda e, PG=PG, q0=q0, lane=lane:
                            e.activation(ROW_L[lane][0:1, q0:q0 + 1024],
                                         PG[64:65, 1024:2048], AF.Ln),
                            waits=avtoks[-2:])
                        trc = sop(lambda e, q0=q0, lane=lane:
                                  e.activation(ROW_L[lane][0:1, q0:q0 + 1024],
                                               ROW_L[lane][0:1, q0:q0 + 1024],
                                               AF.Exp, scale=-1.0))
                        yield
                        yield
                        tb = []
                        for qc in range(2):
                            tb.append(mm(PG[0:64, qc * 512:(qc + 1) * 512], ones_r[0:1, 0:64],
                                         ROW_L[lane][0:1, q0 + qc * 512:q0 + (qc + 1) * 512],
                                         True, True, waits=[trc, rot_free]))
                        tbc = sop(lambda e, PG=PG, lane=lane:
                                  e.copy(GB64[lane][:, 0:1024], PG[0:64, 0:1024]), waits=tb)
                        tnm = vop(lambda e, PG=PG, lane=lane, q0=q0:
                                  e.tensor_mul(OT[ch][sub:sub + 64, q0:q0 + 1024],
                                               PG[0:64, 1024:2048], GB64[lane][:, 0:1024]),
                                  waits=[tbc] + avtoks[-2:])
                        norm_toks.append([tbc, tnm])
                        yield
                    st["prev_bcopy"] = norm_toks[-1][0]
                    st["prev_norm"] = norm_toks[-1][1]
                    st["prev_exp"] = rot_free
                    ot_done.append(norm_toks[-1][1])

                active = {0: gen_head(0, 0), 1: gen_head(1, 1)}
                nxt = [2, 3]
                while active:
                    for lane in (0, 1):
                        g = active.get(lane)
                        if g is None:
                            continue
                        try:
                            next(g)
                        except StopIteration:
                            if nxt[lane] < NH:
                                active[lane] = gen_head(nxt[lane], lane)
                                nxt[lane] += 2
                            else:
                                del active[lane]

                tap("OT", OT, ot_done)

                if LVL >= 3:
                    p.barrier()
                    for i in range(NT):
                        ar.free(f"V{i}")
                    ar.free("PT")
                    ar.free("EXT")
                    ar.free("EXT2")
                    ar.free("SQG")
                    ar.free("QT")
                    ar.free("KT")
                    for c in range(NF):
                        ar.alloc(f"XRb{c}", 8 * 1024)
                    ar.alloc("WOS", 4 * 1024)
                    XR2 = [ar.view(f"XRb{c}", 0, 128, 0, T, F32) for c in range(NF)]
                    WOS = [ar.view("WOS", 0, 128, i * D, D, BF16) for i in range(2)]
                    wos_cons = [None, None]

                    tok_bo = dma("gpsimd", bo_c_s[:], dins["bo_c"], stream="boc")
                    for c in range(NF):
                        dma("sync", XR2[c], spill_x[c], waits=[tok_xspill], stream="xrl")
                    tok_xrl = [("xrl", p.counts["xrl"])] * NF

                    grp = [None, None]
                    resid_done = []
                    for oc in range(NF):
                        g = oc % 2
                        PG = PGA if g == 0 else PGB
                        sl = oc % 2
                        tok_wo = dma("gpsimd",
                                     WOS[sl].rearrange("p (c m) -> p c m", c=NF),
                                     dins["wo"][:, oc * 128:(oc + 1) * 128]
                                     .rearrange("(c p) m -> p c m", p=128),
                                     waits=[wos_cons[sl]], stream=f"wos{sl}")
                        tmm = None
                        for fc in range(NF):
                            for tc in range(4):
                                tmm = mm(PG[:, tc * 512:(tc + 1) * 512],
                                         WOS[sl][:, fc * 128:(fc + 1) * 128],
                                         OT[fc][:, tc * 512:(tc + 1) * 512],
                                         fc == 0, fc == NF - 1,
                                         waits=[ot_done[-1], tok_wo] +
                                               ([grp[g]] if fc == 0 else []))
                        wos_cons[sl] = tmm
                        tadd = vop(lambda e, c=oc, PG=PG, oc=oc:
                                   e.scalar_tensor_tensor(
                                       XR2[c][:], PG[:], bo_c_s[:, oc:oc + 1],
                                       XR2[c][:], ALU.add, ALU.add),
                                   waits=[tmm, tok_xrl[oc], tok_bo])
                        grp[g] = tadd
                        resid_done.append(tadd)
                    for c in range(NF):
                        ar.alloc(f"ZBc{c}", 4 * 1024)
                    ar.alloc("SQ", 4 * 1024)
                    ar.free("WOS")
                    for c in range(NF):
                        ar.free(f"OT{c}")
                    ZB = [ar.view(f"ZBc{c}", 0, 128, 0, T, BF16) for c in range(NF)]
                    SQ = ar.view("SQ", 0, 128, 0, T, BF16)

                    def ln_stats(xr_list, zb_list, waits_per_chunk, pg_free):
                        tok_sq_mm = None
                        zb_toks = []
                        for fc in range(NF):
                            tz = sop(lambda e, zf=zb_list[fc], xf=xr_list[fc]:
                                     e.copy(zf[:], xf[:]),
                                     waits=[waits_per_chunk[fc]])
                            zb_toks.append(tz)
                            tsq = sop(lambda e, zf=zb_list[fc]: e.activation(SQ, zf[:], AF.Square),
                                      waits=[tz, tok_sq_mm])
                            for tc in range(4):
                                mm(PGA[0:1, tc * 512:(tc + 1) * 512], ones_cb[:],
                                   zb_list[fc][:, tc * 512:(tc + 1) * 512],
                                   fc == 0, fc == NF - 1,
                                   waits=[tz, tok_ones_cb, pg_free if fc == 0 else None])
                                tok_sq_mm = mm(PGA[32:33, tc * 512:(tc + 1) * 512], ones_cb[:],
                                               SQ[:, tc * 512:(tc + 1) * 512],
                                               fc == 0, fc == NF - 1, waits=[tsq],
                                               tpos=(0, 32))
                        return tok_sq_mm, zb_toks

                    tok_stat, _ = ln_stats(XR2, ZB, resid_done, grp[0])

                    s_mu = sop(lambda e: e.activation(rowbuf[0:1, :], PGA[0:1, :], AF.Copy,
                                                      scale=1.0 / D), waits=[tok_stat])
                    s_e2 = sop(lambda e: e.activation(G1[0:1, :], PGA[32:33, :], AF.Copy,
                                                      scale=1.0 / D))
                    s_m2 = sop(lambda e: e.activation(G0[0:1, :], rowbuf[0:1, :], AF.Square))
                    v_var = vop(lambda e: e.tensor_sub(G1[0:1, :], G1[0:1, :],
                                                       G0[0:1, :]), waits=[s_m2, s_e2])
                    s_ln = sop(lambda e: e.activation(G0[0:1, :], G1[0:1, :], AF.Ln,
                                                      bias=eps_ln[:]), waits=[v_var])
                    s_rstd = sop(lambda e: e.activation(G1[0:1, :], G0[0:1, :], AF.Exp,
                                                        scale=-0.5))

                    tmb = []
                    for tc in range(4):
                        tmb.append(mm(PGA[:, tc * 512:(tc + 1) * 512], ones_r[0:1, 0:128],
                                      rowbuf[0:1, tc * 512:(tc + 1) * 512], True, True,
                                      waits=[s_e2, s_mu]))
                    trb = []
                    for tc in range(4):
                        trb.append(mm(PGB[:, tc * 512:(tc + 1) * 512], ones_r[0:1, 0:128],
                                      G1[0:1, tc * 512:(tc + 1) * 512], True, True,
                                      waits=[s_rstd, grp[1]]))
                    x2_done = []
                    tok_x2sp = []
                    for fc in range(NF):
                        t1 = vop(lambda e, c=fc: e.tensor_sub(G0[:], XR2[c][:], PGA[:]),
                                 waits=tmb + [tok_stat, s_rstd])
                        t2 = vop(lambda e, c=fc: e.tensor_mul(XR2[c][:], G0[:], PGB[:]),
                                 waits=trb)
                        x2_done.append(t2)
                        tok_x2sp.append(dma("sync", spill_x2[fc], XR2[fc], waits=[t2],
                                            stream=f"sp2{fc}"))
                    tap("X2", XR2, x2_done)

                    if LVL >= 4:
                        for c in range(NF):
                            ar.free(f"ZBc{c}")
                        ar.free("SQ")

                        def make_xs(XS, XF, xf_ready, xg=()):
                            xs_tok = []
                            for c in range(NF):
                                if c < 4:
                                    tt = sop(lambda e, c=c: e.copy(XS[c][:], XF[c][:]),
                                             waits=[xf_ready[c]] + list(xg))
                                else:
                                    tv = vop(lambda e, c=c: e.memset(XS[c][:, 0:1], 0.0),
                                             waits=[xf_ready[c]] + list(xg))
                                    tt = sop(lambda e, c=c:
                                             e.copy(XS[c][:, 1:T], XF[c][:, 0:T - 1]),
                                             waits=[xf_ready[c]] + list(xg))
                                    tt = [tt, tv]
                                xs_tok.append(tt)
                            return xs_tok

                        def emit_ffn_layer(l, XF, xf_ready, pg_init, region_free=(),
                                           xf_src=None, final=False, x2_src=None):
                            LWd = dins
                            ar.alloc("SQL", 4 * 1024)
                            ar.alloc("SQL2", 4 * 1024)
                            ar.alloc("QKS2", 4 * 1024)
                            ar.alloc("GWL", 8 * 1024)
                            pre_guard = [p.now("T"), p.now("S")]
                            t_qs = dma("sync", qs_c[:], LWd[f"l{l}_qs"],
                                       waits=[p.now("V")], stream="qs")
                            t_ks = dma("sync", ks_c[:], LWd[f"l{l}_ks"],
                                       waits=[p.now("V")], stream="ks")
                            t_lng = dma("sync", lng_c[:], LWd[f"l{l}_lng"],
                                        waits=[p.now("V")], stream="lng")
                            t_lnb = dma("sync", lnb_c[:], LWd[f"l{l}_lnb"],
                                        waits=[p.now("V")], stream="lnb")

                            for c in range(NF):
                                ar.alloc(f"XS{c}", 4 * 1024)
                            XS = [ar.view(f"XS{c}", 0, 128, 0, T, BF16) for c in range(NF)]
                            xs_tok = make_xs(XS, XF, xf_ready,
                                             xg=[p.now("T")] if l > 0 else []) \
                                + list(region_free)
                            for c in range(NF):
                                ar.free(f"XFl{c}" if l > 0 else f"XRb{c}")

                            for c in range(NF):
                                ar.alloc(f"QKT{c}", 4 * 1024)
                            for i in range(NT):
                                ar.alloc(f"vL{i}", 520 * 2)
                            for i in range(4):
                                ar.alloc(f"NRM{i}", 8 * 1024)
                            ar.alloc("ROWS0", 8 * 1024)
                            ar.alloc("ROWS1", 8 * 1024)
                            QKT = [ar.view(f"QKT{c}", 0, 128, 0, T, BF16) for c in range(NF)]
                            VL = [ar.view(f"vL{i}", 0, 128, 0, 520, BF16) for i in range(NT)]
                            SQLv = [ar.view("SQL", 0, 128, 0, T, BF16),
                                    ar.view("SQL2", 0, 128, 0, T, BF16)]
                            QKS2 = [ar.view("QKS2", 0, 128, i * D, D, BF16) for i in range(2)]
                            GWL = ar.view("GWL", 0, 128, 0, NF * 512, BF16)
                            NRM = [ar.view(f"NRM{i}", 0, 128, 0, T, F32) for i in range(4)]
                            ROWS = [ar.view("ROWS0", 0, 128, 0, T, F32),
                                    ar.view("ROWS1", 0, 128, 0, T, F32)]
                            tok_wvl = dma("gpsimd",
                                          GWL.rearrange("p (c m) -> p c m", c=NF),
                                          LWd[f"l{l}_wv"].rearrange("(c p) m -> p c m", p=128),
                                          waits=pre_guard, stream="gwv")

                            s1state = [dict(pg_free=pg_init[0], cons=None),
                                       dict(pg_free=pg_init[1], cons=None)]
                            raw_done = [None] * NF
                            rows_done = [None] * NF

                            def s1gen(oc, lane):
                                st = s1state[lane]
                                PG = PGA if lane == 0 else PGB
                                SQl = SQLv[lane]
                                nb = 64 * (oc % 2)
                                rA = NRM[oc // 2][nb:nb + 1, :]
                                rB = NRM[oc // 2][nb + 32:nb + 33, :]
                                tok_w = dma("gpsimd",
                                            QKS2[lane].rearrange("p (c m) -> p c m", c=NF),
                                            LWd[f"l{l}_wqk"][:, oc * 128:(oc + 1) * 128]
                                            .rearrange("(c p) m -> p c m", p=128),
                                            waits=[st["cons"]] + pre_guard,
                                            stream=f"qks2{lane}")
                                tmm = None
                                for fc in range(NF):
                                    for tc in range(4):
                                        tmm = mm(PG[:, tc * 512:(tc + 1) * 512],
                                                 QKS2[lane][:, fc * 128:(fc + 1) * 128],
                                                 XS[fc][:, tc * 512:(tc + 1) * 512],
                                                 fc == 0, fc == NF - 1,
                                                 waits=[tok_w, st["pg_free"], xs_tok[fc]] +
                                                       (list(region_free)
                                                        if oc < 2 and fc == 0 else []))
                                st["cons"] = tmm
                                yield
                                tcpy = sop(lambda e, d=QKT[oc], PG=PG: e.copy(d[:], PG[:]),
                                           waits=[tmm])
                                tsq = sop(lambda e, PG=PG, SQl=SQl:
                                          e.activation(SQl[:], PG[:], AF.Square))
                                raw_done[oc] = tcpy
                                yield
                                ssA, ssB = [], []
                                for tc in range(4):
                                    ssA.append(mm(PG[0:1, tc * 512:(tc + 1) * 512],
                                                  ones_cb[0:64, :],
                                                  SQl[0:64, tc * 512:(tc + 1) * 512],
                                                  True, True, waits=[tsq, tok_ones_cb]))
                                    ssB.append(mm(PG[32:33, tc * 512:(tc + 1) * 512],
                                                  ones_cb[64:128, :],
                                                  SQl[64:128, tc * 512:(tc + 1) * 512],
                                                  True, True, tpos=(64, 32)))
                                yield
                                vA = vop(lambda e, rA=rA, PG=PG: e.tensor_scalar_max(
                                    rA, PG[0:1, :], 1e-24), waits=ssA)
                                vB = vop(lambda e, rB=rB, PG=PG: e.tensor_scalar_max(
                                    rB, PG[32:33, :], 1e-24), waits=ssB)
                                sop(lambda e, rA=rA: e.activation(rA, rA, AF.Ln), waits=[vA])
                                sA2 = sop(lambda e, rA=rA: e.activation(rA, rA, AF.Exp,
                                                                        scale=-0.5))
                                sop(lambda e, rB=rB: e.activation(rB, rB, AF.Ln), waits=[vB])
                                sB2 = sop(lambda e, rB=rB: e.activation(rB, rB, AF.Exp,
                                                                        scale=-0.5))
                                st["pg_free"] = [vA, vB]
                                rows_done[oc] = [sA2, sB2]

                            s1act = {0: s1gen(0, 0), 1: s1gen(1, 1)}
                            s1nxt = [2, 3]
                            while s1act:
                                for lane in (0, 1):
                                    g = s1act.get(lane)
                                    if g is None:
                                        continue
                                    try:
                                        next(g)
                                    except StopIteration:
                                        if s1nxt[lane] < NF:
                                            s1act[lane] = s1gen(s1nxt[lane], lane)
                                            s1nxt[lane] += 2
                                        else:
                                            del s1act[lane]

                            vl_ones = [vop(lambda e, v=VL[i]: e.memset(v[:, 64:520:65], 1.0),
                                           waits=xs_tok) for i in range(NT)]
                            vl_done = []
                            vhf = [s1state[0]["pg_free"], s1state[1]["pg_free"]]
                            for i in range(NT):
                                hh = i % 2
                                ps = PGA[:, 0:512] if hh == 0 else PGB[:, 0:512]
                                tmm = None
                                for fc in range(NF):
                                    tmm = mm(ps, XS[fc][:, i * 128:(i + 1) * 128],
                                             GWL[:, fc * 512:(fc + 1) * 512],
                                             fc == 0, fc == NF - 1,
                                             waits=[tok_wvl, vhf[hh]])
                                tev = vop(lambda e, ps=ps, v=VL[i]:
                                          e.tensor_copy(
                                              v[:, 0:520].rearrange("p (h c) -> p h c", h=8)[:, :, 0:64],
                                              ps.rearrange("p (h c) -> p h c", h=8)),
                                          waits=[tmm, vl_ones[i]])
                                vhf[hh] = tev
                                vl_done.append(tev)

                            for c in range(NF):
                                ar.free(f"XS{c}")
                            ar.free("GWL")
                            for nmx in ("XCaq", "XCsq", "XCak", "XCsk"):
                                ar.alloc(nmx, 8 * 1024)
                            XAQ = ar.view("XCaq", 0, 128, 0, T, F32)
                            XSQ = ar.view("XCsq", 0, 128, 0, T, F32)
                            XAK = ar.view("XCak", 0, 128, 0, T, F32)
                            XSK = ar.view("XCsk", 0, 128, 0, T, F32)
                            xcg = [p.now("T"), p.now("V")]
                            dma("sync", XAQ, dins["x_aq"], waits=xcg, stream="xc")
                            dma("sync", XSQ, dins["x_sq"], stream="xc")
                            dma("sync", XAK, dins["x_ak"], stream="xc")
                            dma("sync", XSK, dins["x_sk"], stream="xc")
                            tok_xc = ("xc", p.counts["xc"])

                            qk_outs = []
                            s2_pg = [vhf[0], vhf[1]]
                            rows_cons = [None, None]
                            for oc in range(NF):
                                lane = oc % 2
                                is_q = oc < 4
                                PG = PGA if lane == 0 else PGB
                                RW = ROWS[lane]
                                nb = 64 * (oc % 2)
                                rA = NRM[oc // 2][nb:nb + 1, :]
                                rB = NRM[oc // 2][nb + 32:nb + 33, :]
                                scpA = sop(lambda e, rA=rA, RW=RW: e.copy(RW[0:1, :], rA),
                                           waits=[rows_done[oc][0], rows_cons[lane]])
                                scpB = sop(lambda e, rB=rB, RW=RW: e.copy(RW[64:65, :], rB),
                                           waits=[rows_done[oc][1]])
                                btoks = []
                                for tc in range(4):
                                    btoks.append(mm(PG[0:64, tc * 512:(tc + 1) * 512],
                                                    ones_r[0:1, 0:64],
                                                    RW[0:1, tc * 512:(tc + 1) * 512],
                                                    True, True,
                                                    waits=[scpA, s2_pg[lane], tok_ones]))
                                    btoks.append(mm(PG[64:128, tc * 512:(tc + 1) * 512],
                                                    ones_r[64:65, 0:64],
                                                    RW[64:65, tc * 512:(tc + 1) * 512],
                                                    True, True, tpos=(64, 64),
                                                    waits=[scpB]))
                                rows_cons[lane] = btoks
                                scol = qs_c if is_q else ks_c
                                tqn = vop(lambda e, sc=scol, s=QKT[oc], PG=PG:
                                          e.scalar_tensor_tensor(G0[:], s[:], sc[:], PG[:],
                                                                 ALU.mult, ALU.mult),
                                          waits=btoks + [t_qs, t_ks, raw_done[oc]])
                                s2_pg[lane] = tqn
                                CA = XAQ if is_q else XAK
                                CS2 = XSQ if is_q else XSK
                                tsh = vop(lambda e: e.stream_shuffle(rowbuf[:], G0[:],
                                                                     SHUF_SWAP))
                                tm1 = vop(lambda e, CA=CA: e.tensor_mul(G1[:], G0[:], CA),
                                          waits=[tok_xc])
                                tm2 = vop(lambda e, CS2=CS2:
                                          e.tensor_mul(rowbuf[:], rowbuf[:], CS2),
                                          waits=[tok_xc])
                                t4 = vop(lambda e, d=QKT[oc]: e.tensor_add(d[:], G1[:],
                                                                           rowbuf[:]),
                                         waits=[tm1, tm2])
                                qk_outs.append(t4)
                            grpf = [s2_pg[0], s2_pg[1]]

                            tap(f"QKT_l{l}", QKT, qk_outs)
                            tap(f"VL_l{l}", VL, vl_done)
                            for nmx in ("XCaq", "XCsq", "XCak", "XCsk"):
                                ar.free(nmx)
                            ar.free("SQL"); ar.free("SQL2"); ar.free("QKS2")
                            for i in range(4):
                                ar.free(f"NRM{i}")
                            ar.free("ROWS0"); ar.free("ROWS1")
                            for c in range(4):
                                ar.alloc(f"OTL{c}", 4 * 1024)
                            ar.alloc("EXTL", 8 * 1024)
                            ar.alloc("EXTL2", 8 * 1024)
                            ar.alloc("PTL", 4 * 1024)
                            ar.alloc("LWO", 8 * 1024)
                            for c in range(NF):
                                ar.alloc(f"XFl{c}", 8 * 1024)
                            XFl = [ar.view(f"XFl{c}", 0, 128, 0, T, F32)
                                   for c in range(NF)]
                            xsrc_buf, xsrc_tok = xf_src
                            xg = [p.now("T"), p.now("V"), p.now("S")]
                            tok_xlr = [dma("sync", XFl[c], xsrc_buf[c],
                                           waits=[xsrc_tok[c]] + xg,
                                           stream=f"xlr{l}{c}")
                                       for c in range(NF)]
                            OTL = [ar.view(f"OTL{c}", 0, 128, 0, T, BF16) for c in range(4)]
                            QXL = [ar.view("EXTL", 0, 65, 0, T, BF16),
                                   ar.view("EXTL2", 0, 65, 0, T, BF16)]
                            KXL = [ar.view("EXTL", 0, 65, T, T, BF16),
                                   ar.view("EXTL2", 0, 65, T, T, BF16)]
                            PTL = [[ar.view("PTL", 0, 128, i * 512, 512, BF16)
                                    for i in range(2)],
                                   [ar.view("PTL", 0, 128, (2 + i) * 512, 512, BF16)
                                    for i in range(2)]]
                            LWO = ar.view("LWO", 0, 128, 0, 4 * D, BF16)
                            tok_wol = dma("gpsimd",
                                          LWO.rearrange("p (c m) -> p c m", c=4),
                                          LWd[f"l{l}_wo"].rearrange("(c p) m -> p c m", p=128),
                                          waits=[p.now("T"), p.now("V"), p.now("S")],
                                          stream="lwo")
                            tok_kxo = [vop(lambda e: e.memset(KXL[0][64:65, :], 1.0),
                                           waits=[p.now("S")]),
                                       vop(lambda e: e.memset(KXL[1][64:65, :], 1.0),
                                           waits=[p.now("S")])]

                            otl_done = []
                            lstate = [dict(prev_norm=None, prev_bcopy=None, prev_exp=None,
                                           pt_av=[None, None]),
                                      dict(prev_norm=None, prev_bcopy=None, prev_exp=None,
                                           pt_av=[None, None])]

                            def lgen_head(h, lane):
                                st = lstate[lane]
                                PG = PGA if lane == 0 else PGB
                                ch, sub = h // 2, (h % 2) * 64
                                QXl, KXl = QXL[lane], KXL[lane]
                                mf, mb = m_f[:, lane:lane + 1], m_bf[:, lane:lane + 1]
                                tqx = vop(lambda e: e.tensor_copy(QXl[0:64, :],
                                                                  QKT[ch][sub:sub + 64, :]),
                                          waits=[qk_outs[ch], st["prev_norm"],
                                                 st["prev_exp"]])
                                tkx = vop(lambda e: e.tensor_copy(KXl[0:64, :],
                                                                  QKT[4 + ch][sub:sub + 64, :]),
                                          waits=[qk_outs[4 + ch], tok_kxo[lane]])
                                for w in range(NW):
                                    kcol0 = (w - 1) * 512 if w > 0 else 0
                                    kw = 1024 if w > 0 else 512
                                    nkt = kw // 128
                                    scal_mrow = st.get("scal_mrow")
                                    prev_max = st.get("win_free")
                                    for qt_i in range(4):
                                        q0 = w * 512 + qt_i * 128
                                        ktoks = []
                                        for kc in range(kw // 512):
                                            tmm = mm(PG[:, kc * 512:(kc + 1) * 512],
                                                     QKT[ch][sub:sub + 64, q0:q0 + 128],
                                                     QKT[4 + ch][sub:sub + 64,
                                                                 kcol0 + kc * 512:kcol0 + (kc + 1) * 512],
                                                     True, True,
                                                     waits=[tqx, tkx, prev_max])
                                            ktoks.append(tmm)
                                        tmax = vop(lambda e, kw=kw, PG=PG, mf=mf:
                                                   e.reduce_max(mf, PG[:, 0:kw:MAXSTRIDE],
                                                                axis=AX.X), waits=ktoks)
                                        tneg = vop(lambda e, mf=mf, mb=mb:
                                                   e.tensor_scalar_mul(mb, mf, -1.0))
                                        yield
                                        tmr = mm(PG[0:1, 1536 + qt_i * 128:1536 + (qt_i + 1) * 128],
                                                 mb, ident_b[:], True, True,
                                                 waits=[tneg, tok_c["ident_bf"], scal_mrow,
                                                        st["pt_av"][0] if (w == 0 and qt_i == 0)
                                                        else None,
                                                        st["pt_av"][1] if (w == 0 and qt_i == 0)
                                                        else None])
                                        scal_mrow = sop(lambda e, q0=q0, qt_i=qt_i, PG=PG,
                                                        QXl=QXl:
                                                        e.copy(QXl[64:65, q0:q0 + 128],
                                                               PG[0:1, 1536 + qt_i * 128:1536 + (qt_i + 1) * 128]),
                                                        waits=[tmr, tqx])
                                        prev_max = [tmax, scal_mrow]
                                        yield
                                    st["scal_mrow"] = scal_mrow
                                    avtoks = []
                                    rot_free = [prev_max, prev_max]
                                    own0 = (kw - 512) // 128
                                    order = [(own0 + i2, i2 * 128, 512, "own") for i2 in range(4)]
                                    order += [(i2, 0, (i2 + 1) * 128, "prev")
                                              for i2 in range(nkt - 4)]
                                    for idx, (kt, qlo, qhi, kind) in enumerate(order):
                                        width = qhi - qlo
                                        rb = idx % 2
                                        ptb = PTL[lane][rb]
                                        tmm = mm(PG[:, rb * 512:rb * 512 + width],
                                                 KXl[:, kcol0 + kt * 128:kcol0 + (kt + 1) * 128],
                                                 QXl[:, w * 512 + qlo:w * 512 + qhi],
                                                 True, True,
                                                 waits=[tkx, scal_mrow, rot_free[rb]])
                                        texp = sop(lambda e, rb=rb, ptb=ptb, qlo=qlo,
                                                   width=width, PG=PG:
                                                   e.activation(ptb[:, qlo:qlo + width],
                                                                PG[:, rb * 512:rb * 512 + width],
                                                                AF.Exp),
                                                   waits=[tmm, st["pt_av"][rb]])
                                        rot_free[rb] = texp
                                        if kind == "own":
                                            tmask = vop(lambda e, ptb=ptb, qlo=qlo:
                                                        e.tensor_mul(ptb[:, qlo:qlo + 128],
                                                                     ptb[:, qlo:qlo + 128],
                                                                     tri_u[:]),
                                                        waits=[texp, tok_c["tri_u"]])
                                        else:
                                            tmask = vop(lambda e, ptb=ptb, qhi=qhi:
                                                        e.tensor_mul(ptb[:, qhi - 128:qhi],
                                                                     ptb[:, qhi - 128:qhi],
                                                                     tri_l[:]),
                                                        waits=[texp, tok_c["tri_l"]])
                                        yield
                                        gkt = kcol0 // 128 + kt
                                        tav = mm(PG[0:65, 1024 + qlo:1024 + qhi],
                                                 VL[gkt][:, h * 65:(h + 1) * 65],
                                                 ptb[:, qlo:qhi],
                                                 idx == 0, idx == len(order) - 1,
                                                 waits=[tmask, vl_done[gkt],
                                                        st["prev_norm"] if idx == 0 else None])
                                        st["pt_av"][rb] = tav
                                        avtoks.append(tav)
                                        yield
                                    sop(lambda e, PG=PG, lane=lane:
                                        e.activation(ROW_L[lane][0:1, 0:512],
                                                     PG[64:65, 1024:1536], AF.Ln),
                                        waits=avtoks)
                                    trc = sop(lambda e, lane=lane:
                                              e.activation(ROW_L[lane][0:1, 0:512],
                                                           ROW_L[lane][0:1, 0:512],
                                                           AF.Exp, scale=-1.0))
                                    yield
                                    yield
                                    tb = mm(PG[0:64, 0:512], ones_r[0:1, 0:64],
                                            ROW_L[lane][0:1, 0:512], True, True,
                                            waits=[trc, rot_free[0]])
                                    tbc = sop(lambda e, PG=PG, lane=lane:
                                              e.copy(GB64[lane][:, 0:512], PG[0:64, 0:512]),
                                              waits=[tb])
                                    tnm = vop(lambda e, PG=PG, lane=lane, w=w:
                                              e.tensor_mul(OTL[ch][sub:sub + 64,
                                                               w * 512:(w + 1) * 512],
                                                           PG[0:64, 1024:1536],
                                                           GB64[lane][:, 0:512]),
                                              waits=[tbc] + avtoks)
                                    st["prev_bcopy"] = tbc
                                    st["prev_norm"] = tnm
                                    st["prev_exp"] = rot_free[1]
                                    st["win_free"] = [rot_free[0], rot_free[1], tbc, tnm]
                                    otl_done.append(tnm)
                                    yield

                            lact = {0: lgen_head(0, 0), 1: lgen_head(1, 1)}
                            lnxt = [2, 3]
                            while lact:
                                for lane in (0, 1):
                                    g = lact.get(lane)
                                    if g is None:
                                        continue
                                    try:
                                        next(g)
                                    except StopIteration:
                                        if lnxt[lane] < LH:
                                            lact[lane] = lgen_head(lnxt[lane], lane)
                                            lnxt[lane] += 2
                                        else:
                                            del lact[lane]
                            prev_norm = [lstate[0]["prev_norm"], lstate[1]["prev_norm"]]
                            prev_bcopy = [lstate[0]["prev_bcopy"], lstate[1]["prev_bcopy"]]

                            tap(f"OTL_l{l}", OTL, otl_done)
                            attn_res = []
                            grpw = [lstate[0]["win_free"], lstate[1]["win_free"]]
                            for oc in range(NF):
                                g = oc % 2
                                PG = PGA if g == 0 else PGB
                                tmm = None
                                for fc in range(4):
                                    for tc in range(4):
                                        tmm = mm(PG[:, tc * 512:(tc + 1) * 512],
                                                 LWO[:, fc * D + oc * 128:fc * D + (oc + 1) * 128],
                                                 OTL[fc][:, tc * 512:(tc + 1) * 512],
                                                 fc == 0, fc == 3,
                                                 waits=[tok_wol, grpw[g], otl_done[-1],
                                                        prev_norm])
                                tadd = vop(lambda e, PG=PG, c=oc:
                                           e.tensor_add(XFl[c][:], XFl[c][:], PG[:]),
                                           waits=[tmm, tok_xlr[oc]])
                                grpw[g] = tadd
                                attn_res.append(tadd)

                            for c in range(NF):
                                ar.free(f"QKT{c}")
                            for i in range(NT):
                                ar.free(f"vL{i}")
                            ar.free("EXTL"); ar.free("EXTL2")
                            ar.free("PTL")
                            for c in range(NF):
                                ar.alloc(f"HB{c}", 4 * 1024)
                            for c in range(NF):
                                ar.alloc(f"ZB2c{c}", 4 * 1024)
                            ar.alloc("SQ2", 4 * 1024)
                            for c in range(4):
                                ar.free(f"OTL{c}")
                            ar.free("LWO")
                            HB = [ar.view(f"HB{c}", 0, 128, 0, T, BF16) for c in range(NF)]
                            ZB2 = [ar.view(f"ZB2c{c}", 0, 128, 0, T, BF16) for c in range(NF)]
                            SQ2 = ar.view("SQ2", 0, 128, 0, T, BF16)

                            tok_sq2 = None
                            tok_s_l = None
                            for fc in range(NF):
                                tz = sop(lambda e, c=fc: e.copy(ZB2[c][:], XFl[c][:]),
                                         waits=[attn_res[fc]])
                                tsq = sop(lambda e, c=fc: e.activation(SQ2, ZB2[c][:], AF.Square),
                                          waits=[tz, tok_sq2])
                                half = 0 if fc < 4 else 1
                                srow = 0 if half == 0 else 32
                                for tc in range(4):
                                    tok_s_l = mm(PGA[srow:srow + 1, tc * 512:(tc + 1) * 512],
                                                 ones_cb[:],
                                                 ZB2[fc][:, tc * 512:(tc + 1) * 512],
                                                 fc % 4 == 0, fc % 4 == 3,
                                                 waits=[tz] + ([grpw[0]] if fc == 0 else []),
                                                 tpos=None if half == 0 else (0, 32))
                                    tok_sq2 = mm(PGB[srow:srow + 1, tc * 512:(tc + 1) * 512],
                                                 ones_cb[:],
                                                 SQ2[:, tc * 512:(tc + 1) * 512],
                                                 fc % 4 == 0, fc % 4 == 3,
                                                 waits=[tsq] + ([grpw[1]] if fc == 0 else []),
                                                 tpos=None if half == 0 else (0, 32))
                            tok_stat2 = [tok_sq2, tok_s_l]
                            r_s2 = sop(lambda e: e.copy(rowbuf[0:1, :], PGA[32:33, :]),
                                       waits=[tok_stat2])
                            v_mu = vop(lambda e: e.tensor_add(G0[0:1, 1:T], PGA[0:1, 1:T],
                                                              rowbuf[0:1, 0:T - 1]),
                                       waits=[r_s2, tok_stat2])
                            v_mu0 = vop(lambda e: e.tensor_copy(G0[0:1, 0:1], PGA[0:1, 0:1]))
                            s_mu = sop(lambda e: e.activation(G1[0:1, :], G0[0:1, :], AF.Copy,
                                                              scale=1.0 / D),
                                       waits=[v_mu, v_mu0])
                            r_q2 = sop(lambda e: e.copy(rowbuf[0:1, :], PGB[32:33, :]))
                            v_e2 = vop(lambda e: e.tensor_add(G0[0:1, 1:T], PGB[0:1, 1:T],
                                                              rowbuf[0:1, 0:T - 1]),
                                       waits=[r_q2, s_mu])
                            v_e20 = vop(lambda e: e.tensor_copy(G0[0:1, 0:1], PGB[0:1, 0:1]))
                            s_e2 = sop(lambda e: e.activation(rowbuf[0:1, :], G0[0:1, :], AF.Copy,
                                                              scale=1.0 / D),
                                       waits=[v_e2, v_e20])
                            s_m2 = sop(lambda e: e.activation(G0[0:1, :], G1[0:1, :], AF.Square))
                            v_var = vop(lambda e: e.tensor_sub(rowbuf[0:1, :], rowbuf[0:1, :],
                                                               G0[0:1, :]), waits=[s_m2, s_e2])
                            s_ln2 = sop(lambda e: e.activation(G0[0:1, :], rowbuf[0:1, :], AF.Ln,
                                                               bias=eps_ln[:]), waits=[v_var])
                            s_rstd2 = sop(lambda e: e.activation(rowbuf[0:1, :], G0[0:1, :],
                                                                 AF.Exp, scale=-0.5))
                            tmb2, trb2 = [], []
                            tmb2, trb2 = [], []
                            for tc in range(4):
                                tmb2.append(mm(PGA[:, tc * 512:(tc + 1) * 512], ones_r[0:1, 0:128],
                                               G1[0:1, tc * 512:(tc + 1) * 512], True, True,
                                               waits=[s_mu, s_e2, s_m2, r_s2, s_rstd2]))
                                trb2.append(mm(PGB[:, tc * 512:(tc + 1) * 512], ones_r[0:1, 0:128],
                                               rowbuf[0:1, tc * 512:(tc + 1) * 512], True, True,
                                               waits=[s_rstd2, r_q2]))
                            hb_done = []
                            t3p = [None, None]
                            for fc in range(NF):
                                if fc < 4:
                                    t1 = vop(lambda e, c=fc: e.tensor_sub(G0[:], XFl[c][:], PGA[:]),
                                             waits=tmb2 + [s_rstd2, s_ln2, attn_res[fc]])
                                else:
                                    vop(lambda e: e.tensor_scalar_mul(G0[:, 0:1], PGA[:, 0:1], -1.0),
                                        waits=tmb2 + [s_rstd2, s_ln2, attn_res[fc]])
                                    t1 = vop(lambda e, c=fc: e.tensor_sub(G0[:, 1:T],
                                                                          XFl[c][:, 0:T - 1],
                                                                          PGA[:, 1:T]))
                                gb = G1 if fc % 2 == 0 else rowbuf
                                t2 = vop(lambda e, gb=gb: e.tensor_mul(gb[:], G0[:], PGB[:]),
                                         waits=trb2 + [t3p[fc % 2]])
                                t3 = sop(lambda e, c=fc, gb=gb: e.activation(
                                    HB[c][:], gb[:], AF.Identity,
                                    scale=lng_c[:, c:c + 1], bias=lnb_c[:, c:c + 1]),
                                    waits=[t2, t_lng, t_lnb])
                                t3p[fc % 2] = t3
                                hb_done.append([t2, t3])
                            tap(f"HB_l{l}", HB, hb_done)
                            for c in range(NF):
                                ar.free(f"ZB2c{c}")
                            ar.free("SQ2")

                            ar.alloc("W1S", 4 * 1024)
                            ar.alloc("W2Sa", 6 * 1024)
                            ar.alloc("W2Sb", 6 * 1024)
                            W1S = [ar.view("W1S", 0, 128, i * D, D, BF16) for i in range(2)]
                            NB0 = NHC // 2
                            wstate = {"cons": [[t3, t3], [t3, t3]]}
                            tok_b1c = dma("gpsimd", b1c_s[:], LWd[f"l{l}_b1c"],
                                          waits=[p.now("V"), p.now("S")], stream="b1c")
                            tok_b2c = dma("gpsimd", b2c_s[:], LWd[f"l{l}_b2c"],
                                          waits=[p.now("V")], stream="b2c")
                            w2state = [[p.now("T"), p.now("S")]] * 2
                            mlp_res = []
                            mlp_add = []
                            pgl = [hb_done[-1], hb_done[-1]]
                            for blk in range(2):
                                ccs = list(range(0, NB0)) if blk == 0 else \
                                      list(range(NB0, NHC))
                                nblk = len(ccs)
                                for cc in ccs:
                                    ar.alloc(f"H2c{cc}", 4 * 1024)
                                H2 = {cc: ar.view(f"H2c{cc}", 0, 128, 0, T, BF16)
                                      for cc in ccs}
                                h2_done = {}
                                for cc in ccs:
                                    toks_pair = {}
                                    for ag in range(2):
                                        mcc = 2 * cc + ag
                                        sl = mcc % 2
                                        tok_w1 = dma("gpsimd",
                                                     W1S[sl].rearrange("p (c m) -> p c m", c=NF),
                                                     LWd[f"l{l}_w1"][:, mcc * 128:(mcc + 1) * 128]
                                                     .rearrange("(c p) m -> p c m", p=128),
                                                     waits=[wstate["cons"][sl][0]] +
                                                           (hb_done[-1] if mcc < 2 else []),
                                                     stream=f"w1s{sl}")
                                        PG = PGA if ag == 0 else PGB
                                        tmm = None
                                        for fc in range(NF):
                                            for tc in range(4):
                                                tmm = mm(PG[:, tc * 512:(tc + 1) * 512],
                                                         W1S[sl][:, fc * 128:(fc + 1) * 128],
                                                         HB[fc][:, tc * 512:(tc + 1) * 512],
                                                         fc == 0, fc == NF - 1,
                                                         waits=[hb_done[fc], tok_w1] +
                                                               ([pgl[ag]] if fc == 0
                                                                else []))
                                        wstate["cons"][sl] = [tmm, tmm]
                                        toks_pair[ag] = tmm
                                    tgel = sop(lambda e, cc=cc:
                                               e.activation(G0[:], PGB[:], AF.Gelu,
                                                            bias=b1c_s[:, 2 * cc + 1:
                                                                       2 * cc + 2]),
                                               waits=[toks_pair[1], tok_b1c])
                                    tmul = vop(lambda e, d=H2[cc], cc=cc:
                                               e.scalar_tensor_tensor(
                                                   d[:], PGA[:],
                                                   b1c_s[:, 2 * cc:2 * cc + 1], G0[:],
                                                   ALU.add, ALU.mult),
                                               waits=[tgel, toks_pair[0], tok_b1c])
                                    pgl = [tmul, tgel]
                                    h2_done[cc] = tmul
                                if blk == 1:
                                    for c in range(NF):
                                        ar.free(f"HB{c}")
                                    ar.free("W1S")
                                    if final:
                                        ar.alloc("X2Ra", 8 * 1024)
                                        ar.alloc("X2Rb", 8 * 1024)
                                        X2R = [ar.view("X2Ra", 0, 128, 0, T, F32),
                                               ar.view("X2Rb", 0, 128, 0, T, F32)]
                                        x2_buf, x2_tok = x2_src
                                        x2g = [p.now("T"), p.now("V"), p.now("S")]
                                        x2prev = [None, None]
                                for oq in range(4):
                                    sl = oq % 2
                                    tok_w2 = dma("gpsimd",
                                                 ar.view(f"W2S{'ab'[sl]}", 0, 128, 0,
                                                         nblk * 256, BF16)
                                                 .rearrange("p (c m) -> p c m", c=nblk),
                                                 LWd[f"l{l}_w2"][ccs[0] * 128:
                                                                 (ccs[0] + nblk) * 128,
                                                                 oq * 256:(oq + 1) * 256]
                                                 .rearrange("(c p) m -> p c m", p=128),
                                                 waits=[w2state[sl]], stream=f"w2s{sl}")
                                    W2Sv = ar.view(f"W2S{'ab'[sl]}", 0, 128, 0,
                                                   nblk * 256, BF16)
                                    tmm = {}
                                    for i_f, cc in enumerate(ccs):
                                        for och in range(2):
                                            PG = PGA if och == 0 else PGB
                                            for tc in range(4):
                                                tmm[och] = mm(
                                                    PG[:, tc * 512:(tc + 1) * 512],
                                                    W2Sv[:, i_f * 256 + och * 128:
                                                         i_f * 256 + (och + 1) * 128],
                                                    H2[cc][:, tc * 512:(tc + 1) * 512],
                                                    i_f == 0, i_f == nblk - 1,
                                                    waits=[h2_done[cc], tok_w2] +
                                                          ([pgl[och]] if i_f == 0
                                                           else []))
                                    w2state[sl] = [tmm[0], tmm[1]]
                                    for och in range(2):
                                        oc = oq * 2 + och
                                        PG = PGA if och == 0 else PGB
                                        if blk == 0:
                                            tadd = vop(lambda e, PG=PG, c=oc, oc=oc:
                                                       e.scalar_tensor_tensor(
                                                           XFl[c][:], PG[:],
                                                           b2c_s[:, oc:oc + 1],
                                                           XFl[c][:], ALU.add, ALU.add),
                                                       waits=[tmm[och], tok_b2c])
                                        else:
                                            tadd = vop(lambda e, PG=PG, c=oc:
                                                       e.tensor_add(XFl[c][:],
                                                                    XFl[c][:], PG[:]),
                                                       waits=[tmm[och]])
                                        pgl[och] = tadd
                                        if blk == 1:
                                            mlp_add.append(tadd)
                                            if final:
                                                tld = dma("sync", X2R[oc % 2], x2_buf[oc],
                                                          waits=[x2_tok[oc],
                                                                 x2prev[oc % 2]] + x2g,
                                                          stream=f"x2l{oc % 2}")
                                                tz2 = vop(lambda e, c=oc:
                                                          e.tensor_add(XFl[c][:], XFl[c][:],
                                                                       X2R[c % 2][:]),
                                                          waits=[tld])
                                                x2prev[oc % 2] = tz2
                                                pgl[och] = tadd
                                                mlp_res.append(tz2)
                                            else:
                                                mlp_res.append(
                                                    dma("sync", spill_xf[oc], XFl[oc][:],
                                                        waits=[tadd], stream=f"sxf{oc}"))
                                for cc in ccs:
                                    ar.free(f"H2c{cc}")
                            ar.free("W2Sa")
                            ar.free("W2Sb")
                            if final:
                                ar.free("X2Ra"); ar.free("X2Rb")
                            return mlp_res, [pgl[0], pgl[1]], XFl, mlp_add

                        mlp_res0, pgf, XFl0, mlp_add0 = emit_ffn_layer(
                            0, XR2, x2_done, [x2_done[-1], x2_done[-1]],
                            region_free=list(tok_x2sp),
                            xf_src=(spill_x2, tok_x2sp))
                        tap("XF0", "dram_xf", mlp_res0)

                        if LVL >= 5:
                            mlp_res1, pgf, XFl1, mlp_add1 = emit_ffn_layer(
                                1, XFl0, mlp_add0, pgf,
                                region_free=list(mlp_res0),
                                xf_src=(spill_xf, mlp_res0),
                                final=(LVL >= 6),
                                x2_src=(spill_x2, tok_x2sp))
                            tap("XF1", XFl1, mlp_res1)

                        if LVL >= 6:
                            for c in range(NF):
                                ar.alloc(f"ZB3c{c}", 4 * 1024)
                            ar.alloc("SQ3", 4 * 1024)
                            XZ = XFl1
                            ZB3 = [ar.view(f"ZB3c{c}", 0, 128, 0, T, BF16) for c in range(NF)]
                            SQ3 = ar.view("SQ3", 0, 128, 0, T, BF16)
                            z_done = mlp_res1
                            tguard = [p.now("T"), p.now("V")]

                            tok_sq3 = None
                            for fc in range(NF):
                                tz = sop(lambda e, c=fc: e.copy(ZB3[c][:], XZ[c][:]),
                                         waits=[z_done[fc],
                                                tguard if fc == 0 else None])
                                tsq = sop(lambda e, c=fc: e.activation(SQ3, ZB3[c][:], AF.Square),
                                          waits=[tz, tok_sq3])
                                for tc in range(4):
                                    mm(PGA[0:1, tc * 512:(tc + 1) * 512], ones_cb[:],
                                       ZB3[fc][:, tc * 512:(tc + 1) * 512],
                                       fc == 0, fc == NF - 1,
                                       waits=[tz] + ([pgf[0], p.now("S")]
                                                     if fc == 0 else []))
                                    tok_sq3 = mm(PGA[32:33, tc * 512:(tc + 1) * 512], ones_cb[:],
                                                 SQ3[:, tc * 512:(tc + 1) * 512],
                                                 fc == 0, fc == NF - 1, waits=[tsq],
                                                 tpos=(0, 32))
                            s_mu = sop(lambda e: e.activation(rowbuf[0:1, :], PGA[0:1, :], AF.Copy,
                                                              scale=1.0 / D), waits=[tok_sq3])
                            s_e2 = sop(lambda e: e.activation(G1[0:1, :], PGA[32:33, :], AF.Copy,
                                                              scale=1.0 / D))
                            s_m2 = sop(lambda e: e.activation(G0[0:1, :], rowbuf[0:1, :], AF.Square))
                            v_var = vop(lambda e: e.tensor_sub(G1[0:1, :], G1[0:1, :],
                                                               G0[0:1, :]), waits=[s_m2, s_e2])
                            s_ln = sop(lambda e: e.activation(G0[0:1, :], G1[0:1, :], AF.Ln,
                                                              bias=eps_ln[:]), waits=[v_var])
                            s_rstd = sop(lambda e: e.activation(G1[0:1, :], G0[0:1, :], AF.Exp,
                                                                scale=-0.5))
                            tmb = []
                            for tc in range(4):
                                tmb.append(mm(PGA[:, tc * 512:(tc + 1) * 512], ones_r[0:1, 0:128],
                                              rowbuf[0:1, tc * 512:(tc + 1) * 512], True, True,
                                              waits=[s_e2, s_mu]))
                            trb = []
                            for tc in range(4):
                                trb.append(mm(PGB[:, tc * 512:(tc + 1) * 512], ones_r[0:1, 0:128],
                                              G1[0:1, tc * 512:(tc + 1) * 512], True, True,
                                              waits=[s_rstd, pgf[1]]))
                            out_done = []
                            for fc in range(NF):
                                t1 = vop(lambda e, c=fc: e.tensor_sub(G0[:], XZ[c][:], PGA[:]),
                                         waits=tmb + [tok_sq3, s_rstd])
                                t2 = vop(lambda e, c=fc: e.tensor_mul(XZ[c][:], G0[:], PGB[:]),
                                         waits=trb)
                                out_done.append(t2)

                            otile = [G0, G1]
                            prev_ev = [out_done[-1], out_done[-1]]
                            st_tok = [None, None]
                            for i in range(NT):
                                hh = i % 2
                                ps = PGA[:, 0:1024] if hh == 0 else PGB[:, 0:1024]
                                tmm = None
                                for fc in range(NF):
                                    tmm = mm(ps[:, fc * 128:(fc + 1) * 128],
                                             XZ[fc][:, i * 128:(i + 1) * 128], ident_f[:],
                                             True, True,
                                             waits=[out_done[fc], tok_c["ident_f32"],
                                                    prev_ev[hh]])
                                tev = vop(lambda e, ps=ps, gb=otile[hh]:
                                          e.tensor_copy(gb[:, 0:1024], ps),
                                          waits=[tmm, st_tok[hh]])
                                p.stream(f"sto{hh}")
                                st_tok[hh] = p.emit("sync",
                                       lambda e, i=i, gb=otile[hh]:
                                       e.dma_start(out=OUT[i * 128:(i + 1) * 128, :],
                                                   in_=gb[:, 0:1024]),
                                       waits=[tev], inc=f"sto{hh}", n=16)
                                prev_ev[hh] = tev

                        p.barrier()


        p.emit("sync", None,
               waits=[("STX", p.counts["STX"])] +
                     [(s, p.counts[s]) for s in ("sto0", "sto1") if s in p.counts])
        if "out_dummy" not in dbgset and LVL < 6:
            zs = rowbuf[0:128, 0:D]
            tz = vop(lambda e: e.memset(zs, 0.0))
            for i in range(NT):
                p.emit("sync", lambda e, o=OUT[i * 128:(i + 1) * 128, :], i2=zs:
                       e.dma_start(out=o, in_=i2), waits=[tz], inc="STX", n=16)
            p.emit("sync", None, waits=[("STX", p.counts["STX"])])

        block = es.enter_context(nc.Block())
        p.install(block)

    return nc


def kernel(**inputs):
    consts = host_constants()
    w = prep_weights(inputs)
    nc = build_nc(upto="all")
    x = np.asarray(inputs["L0_attn_x"])
    in_maps = []
    for core in range(8):
        im = {"x": np.ascontiguousarray(x[core])}
        im.update({k: np.asarray(v) for k, v in w.items()})
        im.update(consts)
        in_maps.append(im)
    res = run_bass_kernel_spmd(nc, in_maps, core_ids=list(range(8)))
    out = np.stack([r["out"] for r in res.results], axis=0)
    return out.astype(np.float32)


# version 16
# speedup vs baseline: 1.0163x; 1.0054x over previous
import contextlib
import numpy as np
import ml_dtypes
import concourse.bass as bass
import concourse.mybir as mybir
from concourse.bass_utils import run_bass_kernel_spmd

F32 = mybir.dt.float32
BF16 = mybir.dt.bfloat16
AF = mybir.ActivationFunctionType
ALU = mybir.AluOpType
AX = mybir.AxisListType
BF = ml_dtypes.bfloat16

D = 1024
T = 2048
NT = T // 128
NF = D // 128
NH = 16
HD = 64
GSCALE = HD ** -0.5
LH = 8
WIN = 512
NW = T // WIN
HID = 2730
HIDP = 2816
NHC = HIDP // 128
QK_SCALE = 8.0
MAXSTRIDE = 8

ENGINES = ("sync", "tensor", "vector", "scalar", "gpsimd")
SHUF_SWAP = [i ^ 1 for i in range(32)]


def _pair_perm():
    pp = np.empty(HD, np.int64)
    pp[0::2] = np.arange(32)
    pp[1::2] = np.arange(32) + 32
    return pp


def host_constants():
    c = {}
    c["ident_f32"] = np.eye(128, dtype=np.float32)
    c["ident_bf"] = np.eye(128, dtype=np.float32).astype(BF)
    j = np.arange(32, dtype=np.float64)
    inv = 1.0 / (10000.0 ** (j / 32.0))
    t = np.arange(T, dtype=np.float64)
    ang = t[None, :] * inv[:, None]
    cos = np.cos(ang); sin = np.sin(ang)
    cos64 = np.empty((64, T)); sin64 = np.empty((64, T))
    cos64[0::2] = cos; cos64[1::2] = cos
    sin64[0::2] = -sin; sin64[1::2] = sin
    c["g_cos"] = np.tile(cos64, (2, 1)).astype(np.float32)
    c["g_sin"] = np.tile(sin64, (2, 1)).astype(np.float32)

    half_idx = np.arange(0, HD, 2, dtype=np.float64)
    invf = 1.0 / (10000.0 ** (half_idx / HD))
    angl = t[None, :] * invf[:, None]
    cosl = np.cos(angl); sinl = np.sin(angl)
    base = (half_idx + 0.4 * HD) / (1.4 * HD)
    power = (t - T // 2) / float(WIN // 2)
    scale = base[:, None] ** power[None, :]

    def inter(even, odd):
        out = np.empty((64, T))
        out[0::2] = even; out[1::2] = odd
        return np.tile(out, (2, 1)).astype(np.float32)

    aq = cosl * scale * QK_SCALE; sq = sinl * scale * QK_SCALE
    ak = cosl / scale; sk = sinl / scale
    c["x_aq"] = inter(aq, aq)
    c["x_sq"] = inter(-sq, sq)
    c["x_ak"] = inter(ak, ak)
    c["x_sk"] = inter(-sk, sk)

    kk = np.arange(128)[:, None]; qq = np.arange(128)[None, :]
    c["tri_u"] = (qq >= kk).astype(np.float32).astype(BF)
    c["tri_l"] = (qq <= kk).astype(np.float32).astype(BF)
    return c


def prep_weights(inp):
    w = {}
    perm = _pair_perm()
    gperm = np.concatenate([perm + 64 * h for h in range(NH)])
    w["wq"] = np.ascontiguousarray(np.asarray(inp["L0_attn_wq_w"])[:, gperm])
    w["wk"] = np.ascontiguousarray(np.asarray(inp["L0_attn_wk_w"])[:, gperm])
    w["wv"] = np.ascontiguousarray(np.asarray(inp["L0_attn_wv_w"]))
    w["wo"] = np.ascontiguousarray(np.asarray(inp["L0_attn_wo_w"]))
    swap = np.arange(128) ^ 1
    bq = np.asarray(inp["L0_attn_wq_b"])[gperm].reshape(NF, 128).T
    bk = np.asarray(inp["L0_attn_wk_b"])[gperm].reshape(NF, 128).T
    w["bqk_c"] = np.ascontiguousarray(
        np.concatenate([bq, bq[swap], bk, bk[swap]], axis=1))
    w["bv"] = np.ascontiguousarray(np.asarray(inp["L0_attn_wv_b"]).reshape(1, D))
    w["bo_c"] = np.ascontiguousarray(
        np.asarray(inp["L0_attn_wo_b"]).reshape(NF, 128).T)

    lperm = np.concatenate([perm + 64 * h for h in range(LH)])
    for l in range(2):
        wqkv = np.asarray(inp["L0_ffn_wqkv"][l])
        w[f"l{l}_wqk"] = np.ascontiguousarray(
            np.concatenate([wqkv[:, 0:512][:, lperm], wqkv[:, 512:1024][:, lperm]], axis=1))
        w[f"l{l}_wv"] = np.ascontiguousarray(wqkv[:, 1024:1536])
        w[f"l{l}_qs"] = np.ascontiguousarray(
            np.tile(np.asarray(inp["L0_ffn_q_scale"][l])[perm], 2).reshape(128, 1))
        w[f"l{l}_ks"] = np.ascontiguousarray(
            np.tile(np.asarray(inp["L0_ffn_k_scale"][l])[perm], 2).reshape(128, 1))
        w[f"l{l}_wo"] = np.ascontiguousarray(np.asarray(inp["L0_ffn_wo"][l]))
        w[f"l{l}_lng"] = np.ascontiguousarray(np.asarray(inp["L0_ffn_ln_g"][l]).reshape(NF, 128).T)
        w[f"l{l}_lnb"] = np.ascontiguousarray(np.asarray(inp["L0_ffn_ln_b"][l]).reshape(NF, 128).T)
        w1 = np.asarray(inp["L0_ffn_w1"][l]); b1 = np.asarray(inp["L0_ffn_b1"][l])
        a_w = np.zeros((D, HIDP), np.float32); a_w[:, :HID] = w1[:, :HID]
        g_w = np.zeros((D, HIDP), np.float32); g_w[:, :HID] = w1[:, HID:]
        a_b = np.zeros(HIDP, np.float32); a_b[:HID] = b1[:HID]
        g_b = np.zeros(HIDP, np.float32); g_b[:HID] = b1[HID:]
        w1p = np.empty((D, 2 * HIDP), np.float32)
        b1p = np.empty((1, 2 * HIDP), np.float32)
        for cc in range(NHC):
            s = slice(cc * 128, (cc + 1) * 128)
            w1p[:, (2 * cc) * 128:(2 * cc + 1) * 128] = a_w[:, s]
            w1p[:, (2 * cc + 1) * 128:(2 * cc + 2) * 128] = g_w[:, s]
            b1p[0, (2 * cc) * 128:(2 * cc + 1) * 128] = a_b[s]
            b1p[0, (2 * cc + 1) * 128:(2 * cc + 2) * 128] = g_b[s]
        w[f"l{l}_w1"] = w1p
        w[f"l{l}_b1c"] = np.ascontiguousarray(b1p.reshape(2 * NHC, 128).T)
        w2p = np.zeros((HIDP, D), np.float32)
        w2p[:HID] = np.asarray(inp["L0_ffn_w2"][l])
        w[f"l{l}_w2"] = w2p
        w[f"l{l}_b2c"] = np.ascontiguousarray(
            np.asarray(inp["L0_ffn_b2"][l]).reshape(NF, 128).T)
    return w


def _flat(waits):
    out = []
    for wv in waits:
        if wv is None:
            continue
        if isinstance(wv, (list, tuple)) and (len(wv) == 0 or not isinstance(wv[0], str)):
            out.extend(x for x in _flat(wv))
        else:
            out.append(wv)
    return out


class Prog:
    def __init__(self, nc, es):
        self.nc = nc
        self.es = es
        self.ops = {e: [] for e in ENGINES}
        self.sems = {}
        self.counts = {}
        self.last_wait = {e: {} for e in ENGINES}
        self.prev_op = {}
        for name in ("T", "V", "S", "G"):
            self.sems[name] = es.enter_context(nc.semaphore(name))
            self.counts[name] = 0

    def stream(self, name):
        if name not in self.sems:
            self.sems[name] = self.es.enter_context(self.nc.semaphore(name))
            self.counts[name] = 0
        return name

    def emit(self, eng, fn, waits=(), inc=None, n=1):
        wlist = []
        lw = self.last_wait[eng]
        allw = list(_flat(waits))
        if fn is not None and eng in ("vector", "scalar"):
            prev = self.prev_op.get(eng)
            if prev is not None:
                allw.append(prev)
        for (s, v) in allw:
            if lw.get(s, 0) >= v:
                continue
            lw[s] = v
            wlist.append((s, v))
        tok = None
        if inc is not None:
            self.counts[inc] += n
            tok = (inc, self.counts[inc])
            if fn is not None:
                self.prev_op[eng] = tok
        self.ops[eng].append((fn, tuple(wlist), (inc, n) if inc else None))
        return tok

    def barrier(self):
        toks = [(s, v) for s, v in self.counts.items() if v > 0]
        for e in ENGINES:
            self.emit(e, None, waits=toks)

    def now(self, sem):
        return (sem, self.counts[sem])

    def install(self, block):
        for ename in ENGINES:
            ops = self.ops[ename]
            sems = self.sems

            def body(eng, ops=ops):
                for fn, waits, inc in ops:
                    for (s, v) in waits:
                        eng.wait_ge(sems[s], v)
                    inst = fn(eng) if fn is not None else None
                    if inc is not None:
                        s, n = inc
                        assert inst is not None
                        inst.then_inc(sems[s], n)
            getattr(block, ename)(body)


class Arena:
    def __init__(self, handle, total_bytes):
        self.base = handle.ap()
        self.free_list = [(0, total_bytes)]
        self.live = {}

    def alloc(self, name, nbytes, align=128):
        assert name not in self.live
        for i, (off, sz) in enumerate(self.free_list):
            aoff = (off + align - 1) // align * align
            pad = aoff - off
            if sz >= nbytes + pad:
                pieces = []
                if pad:
                    pieces.append((off, pad))
                rest = sz - nbytes - pad
                if rest:
                    pieces.append((aoff + nbytes, rest))
                self.free_list[i:i + 1] = pieces
                self.live[name] = (aoff, nbytes)
                return
        raise MemoryError(f"arena OOM: {name} needs {nbytes}; free={self.free_list}")

    def free(self, name):
        off, sz = self.live.pop(name)
        fl = sorted(self.free_list + [(off, sz)])
        merged = []
        for o, s in fl:
            if merged and merged[-1][0] + merged[-1][1] == o:
                merged[-1] = (merged[-1][0], merged[-1][1] + s)
            else:
                merged.append((o, s))
        self.free_list = merged

    def view(self, name, part0, nparts, elem0, nelems, dtype):
        off, sz = self.live[name]
        esz = 4 if dtype == F32 else 2
        assert (elem0 + nelems) * esz <= sz, (name, elem0, nelems, sz)
        bb = self.base.bitcast(dtype)
        st = off // esz
        return bb[part0:part0 + nparts, st + elem0: st + elem0 + nelems]


def build_nc(upto="all", debug=()):
    nc = bass.Bass("TRN2", target_bir_lowering=False)
    LEVELS = {"p0": 0, "p1": 1, "attn": 2, "ln1": 3, "ffn0": 4, "ffn1": 5, "all": 6}
    LVL = LEVELS[upto]
    dbgset = dict(debug)

    X = nc.dram_tensor("x", [T, D], F32, kind="ExternalInput").ap()
    OUT = nc.dram_tensor("out", [T, D], F32, kind="ExternalOutput").ap()
    dins = {}

    def din(name, shape, dt=F32):
        dins[name] = nc.dram_tensor(name, list(shape), dt, kind="ExternalInput").ap()
    for nm, shp in [("wq", (D, D)), ("wk", (D, D)), ("wv", (D, D)), ("wo", (D, D)),
                    ("bqk_c", (128, 4 * NF)), ("bv", (1, D)),
                    ("bo_c", (128, NF)),
                    ("ident_f32", (128, 128)), ("g_cos", (128, T)), ("g_sin", (128, T)),
                    ("x_aq", (128, T)), ("x_sq", (128, T)), ("x_ak", (128, T)),
                    ("x_sk", (128, T))]:
        din(nm, shp)
    din("ident_bf", (128, 128), BF16)
    din("tri_u", (128, 128), BF16)
    din("tri_l", (128, 128), BF16)
    for l in range(2):
        din(f"l{l}_wqk", (D, D)); din(f"l{l}_wv", (D, 512))
        din(f"l{l}_qs", (128, 1)); din(f"l{l}_ks", (128, 1))
        din(f"l{l}_wo", (512, D))
        din(f"l{l}_lng", (128, NF)); din(f"l{l}_lnb", (128, NF))
        din(f"l{l}_w1", (D, 2 * HIDP)); din(f"l{l}_b1c", (128, 2 * NHC))
        din(f"l{l}_w2", (HIDP, D)); din(f"l{l}_b2c", (128, NF))

    spill_x = nc.dram_tensor("spill_x", [NF, 128, T], F32, kind="Internal").ap()
    spill_x2 = nc.dram_tensor("spill_x2", [NF, 128, T], F32, kind="Internal").ap()
    spill_xf = nc.dram_tensor("spill_xf", [NF, 128, T], F32, kind="Internal").ap()
    dbg = {}
    for name, spec in dbgset.items():
        shape, ddt = (spec, F32) if not (isinstance(spec[-1], str)) else (spec[0], BF16)
        if isinstance(spec, tuple) and len(spec) == 2 and isinstance(spec[1], str):
            shape, ddt = spec[0], (BF16 if spec[1] == "bf16" else F32)
        dbg[name] = nc.dram_tensor(name, list(shape), ddt, kind="ExternalOutput").ap()

    es = contextlib.ExitStack()
    with es:
        ARENA_BYTES = 180 * 1024
        arena_h = es.enter_context(nc.sbuf_tensor("ARENA", [128, ARENA_BYTES // 4], F32))
        ar = Arena(arena_h, ARENA_BYTES)

        def sb_(name, shape, dt=F32):
            return es.enter_context(nc.sbuf_tensor(name, shape, dt))
        ident_f = sb_("ident_f", [128, 128])
        ident_b = sb_("ident_b", [128, 128], BF16)
        tri_u = sb_("tri_u_s", [128, 128], BF16)
        tri_l = sb_("tri_l_s", [128, 128], BF16)
        ones_r = sb_("ones_r", [65, 512])
        ones_cb = sb_("ones_cb", [128, 1], BF16)
        eps_ln = sb_("eps_ln", [1, 1])
        brow = sb_("brow", [1, D], BF16)
        brow2 = sb_("brow2", [1, D], BF16)
        ones_rb = sb_("ones_rb", [1, 512], BF16)
        bqk_s = sb_("bqk_s", [128, 4 * NF])
        b1c_s = sb_("b1c_s", [128, 2 * NHC])
        b2c_s = sb_("b2c_s", [128, NF])
        bo_c_s = sb_("bo_c_s", [128, NF])
        qs_c = sb_("qs_c", [128, 1])
        ks_c = sb_("ks_c", [128, 1])
        lng_c = sb_("lng_c", [128, NF])
        lnb_c = sb_("lnb_c", [128, NF])
        m_f = sb_("m_f", [128, 2])
        m_bf = sb_("m_bf", [128, 2], BF16)
        rowbuf = sb_("rowbuf", [128, T])
        rowb2 = sb_("rowb2", [1, T])

        PGA = es.enter_context(nc.psum_tensor("PGA", [128, T], F32))
        PGB = es.enter_context(nc.psum_tensor("PGB", [128, T], F32))

        p = Prog(nc, es)
        p.sems["STX"] = es.enter_context(nc.semaphore("STORE"))
        p.counts["STX"] = 0
        taps = {}

        def dma(eng, out, in_, waits=(), stream=None):
            assert stream is not None
            p.stream(stream)
            return p.emit(eng, lambda e, o=out, i=in_: e.dma_start(out=o, in_=i),
                          waits=waits, inc=stream, n=16)

        def mm(out, lhsT, rhs, start, stop, waits=(), inc="T", tpos=None):
            return p.emit("tensor",
                          lambda e, o=out, lt=lhsT, r=rhs, s1=start, s2=stop, tp=tpos:
                          e.matmul(o, lt, r, start=s1, stop=s2, tile_position=tp),
                          waits=waits, inc=inc)

        def vop(fn, waits=(), inc="V"):
            return p.emit("vector", fn, waits=waits, inc=inc)

        def sop(fn, waits=(), inc="S"):
            return p.emit("scalar", fn, waits=waits, inc=inc)

        def gop(fn, waits=(), inc="G"):
            return p.emit("gpsimd", fn, waits=waits, inc=inc)

        def tap(name, ap_list, waits):
            if name not in dbgset:
                return
            dr = dbg[name]
            if ap_list == "dram_xf":
                for i in range(NF):
                    p.emit("sync", lambda e, o=dr[i], i2=spill_xf[i]:
                           e.dma_start(out=o, in_=i2), waits=waits, inc="STX", n=16)
            else:
                for i, ap in enumerate(ap_list):
                    p.emit("sync", lambda e, o=dr[i], i2=ap: e.dma_start(out=o, in_=i2),
                           waits=waits, inc="STX", n=16)
            p.emit("sync", None, waits=[("STX", p.counts["STX"])])
            p.barrier()


        tok_c = {}
        for nm, dst in [("ident_f32", ident_f), ("ident_bf", ident_b),
                        ("tri_u", tri_u), ("tri_l", tri_l)]:
            tok_c[nm] = dma("sync", dst[:], dins[nm], stream="cst")
        vop(lambda e: e.memset(eps_ln[:], 1e-5))
        tok_ones = vop(lambda e: e.memset(ones_r[:], 1.0))
        tok_onesb = vop(lambda e: e.memset(ones_rb[:], 1.0))
        tok_ones_cb = vop(lambda e: e.memset(ones_cb[:], 1.0))

        ar.alloc("XR", 64 * 1024)
        ar.alloc("XBF", 32 * 1024)
        ar.alloc("CC", 8 * 1024)
        ar.alloc("CS", 8 * 1024)
        ar.alloc("G0", 8 * 1024)
        ar.alloc("G1", 8 * 1024)
        ar.alloc("XTS", 8 * 1024)
        ar.alloc("QKS", 8 * 1024)

        XR = [ar.view("XR", 0, 128, c * T, T, F32) for c in range(NF)]
        XBF = [ar.view("XBF", 0, 128, c * T, T, BF16) for c in range(NF)]
        CCOS = ar.view("CC", 0, 128, 0, T, F32)
        CSIN = ar.view("CS", 0, 128, 0, T, F32)
        G0 = ar.view("G0", 0, 128, 0, T, F32)
        G1 = ar.view("G1", 0, 128, 0, T, F32)
        XT = [ar.view("XTS", 0, 128, i * D, D, F32) for i in range(2)]

        tok_c["cos"] = dma("sync", CCOS, dins["g_cos"], stream="cst")
        tok_c["sin"] = dma("sync", CSIN, dins["g_sin"], stream="cst")
        for _k in list(tok_c):
            tok_c[_k] = ("cst", p.counts["cst"])

        def wslice(gw, fc, o0, no):
            return gw[:, fc * D + o0: fc * D + o0 + no]

        ev_tok = [None] * NT
        bankfree = [None] * 4
        tv = ts = None
        for i in range(NT):
            b = i % 2
            tl = dma("sync", XT[b], X[i * 128:(i + 1) * 128, :],
                     waits=[ev_tok[i - 2] if i >= 2 else None], stream=f"xt{b}")
            for fc in range(NF):
                bank = (i * NF + fc) % 4
                ps = PGA[:, bank * 512: bank * 512 + 128]
                tmm = mm(ps, XT[b][:, fc * 128:(fc + 1) * 128], ident_f[:], True, True,
                         waits=[tl, tok_c["ident_f32"], bankfree[bank]])
                tv = vop(lambda e, ps=ps, d=XR[fc], c0=i * 128:
                         e.tensor_copy(d[:, c0:c0 + 128], ps), waits=[tmm])
                ts = sop(lambda e, ps=ps, d=XBF[fc], c0=i * 128:
                         e.copy(d[:, c0:c0 + 128], ps), waits=[tmm, tv])
                bankfree[bank] = [tv, ts]
            ev_tok[i] = [tv, ts]
        t_xdone = [tv, ts]

        for fc in range(NF):
            dma("sync", spill_x[fc], XR[fc], waits=[t_xdone], stream="spx")
        tok_xspill = ("spx", p.counts["spx"])

        if LVL == 0:
            tap("XR", XR, [t_xdone])
            tap("XBF", XBF, [t_xdone])

        p.barrier()

        if LVL >= 1:
            ar.free("XR")
            ar.free("XTS")
            ar.alloc("QT", 32 * 1024)
            ar.alloc("KT", 32 * 1024)
            QT = [ar.view("QT", 0, 128, c * T, T, BF16) for c in range(NF)]
            KT = [ar.view("KT", 0, 128, c * T, T, BF16) for c in range(NF)]
            QKS = [ar.view("QKS", 0, 128, i * D, D, BF16) for i in range(4)]

            def load_colblock(dram, col0, slot, waits=()):
                src = dram[:, col0:col0 + 128].rearrange("(c p) m -> p c m", p=128)
                dst = QKS[slot].rearrange("p (c m) -> p c m", c=NF)
                return dma("gpsimd", dst, src, waits=waits, stream=f"qks{slot}")

            slot_state = {"i": 0, "cons": [None, None, None, None]}

            def qk_proj(dst, wdram, bc0, grp_free):
                outs = []
                t2 = None
                for oc in range(NF):
                    sl = slot_state["i"] % 4
                    slot_state["i"] += 1
                    tok_w = load_colblock(wdram, oc * 128, sl,
                                          waits=[slot_state["cons"][sl]])
                    g = oc % 2
                    PG = PGA if g == 0 else PGB
                    tmm = None
                    for fc in range(NF):
                        for tc in range(4):
                            tmm = mm(PG[:, tc * 512:(tc + 1) * 512],
                                     QKS[sl][:, fc * 128:(fc + 1) * 128],
                                     XBF[fc][:, tc * 512:(tc + 1) * 512],
                                     fc == 0, fc == NF - 1,
                                     waits=[tok_w] + ([grp_free[g]] if fc == 0
                                                      else []))
                    slot_state["cons"][sl] = tmm
                    t1 = vop(lambda e, PG=PG, oc=oc:
                             e.scalar_tensor_tensor(G0[:], PG[:],
                                                    bqk_s[:, bc0 + oc:bc0 + oc + 1],
                                                    CCOS, ALU.add, ALU.mult),
                             waits=[tmm, tok_c["cos"], tok_bqk])
                    t2 = vop(lambda e, PG=PG: e.stream_shuffle(G1[:], PG[:], SHUF_SWAP))
                    vop(lambda e, oc=oc:
                        e.scalar_tensor_tensor(G1[:], G1[:],
                                               bqk_s[:, bc0 + NF + oc:bc0 + NF + oc + 1],
                                               CSIN, ALU.add, ALU.mult),
                        waits=[tok_c["sin"]])
                    t4 = vop(lambda e, d=dst[oc]: e.tensor_add(d[:], G0[:], G1[:]))
                    grp_free[g] = t2
                    outs.append(t4)
                return outs, grp_free

            tok_bqk = dma("gpsimd", bqk_s[:], dins["bqk_c"], stream="brow2")
            tok_qt, gfree = qk_proj(QT, dins["wq"], 0, [None, None])
            tok_kt, gfree = qk_proj(KT, dins["wk"], 2 * NF, gfree)
            tok_rope_done = p.now("V")

            ar.free("CC")
            ar.free("CS")
            ar.free("QKS")
            ar.alloc("GW", 16 * 1024)
            for i in range(NT):
                ar.alloc(f"V{i}", 1040 * 2)
            VT = [ar.view(f"V{i}", 0, 128, 0, 1040, BF16) for i in range(NT)]
            GWv = ar.view("GW", 0, 128, 0, NF * D, BF16)

            tok_wv = dma("gpsimd", GWv.rearrange("p (c m) -> p c m", c=NF),
                         dins["wv"].rearrange("(c p) m -> p c m", p=128),
                         waits=[tok_rope_done, p.now("T")], stream="gwv")
            tok_bv2 = dma("gpsimd", brow[:], dins["bv"], stream="brow")
            tok_vones = []
            for i in range(NT):
                tok_vones.append(vop(lambda e, v=VT[i]: e.memset(v[:, 64:1040:65], 1.0),
                                     waits=[tok_rope_done]))

            vproj_done = []
            half_free = [gfree[0], gfree[1]]
            tmm = None
            for i in range(NT):
                hh = i % 2
                ps = PGA[:, hh * 1024:(hh + 1) * 1024] if hh == 0 else PGB[:, 0:1024]
                mm(ps[:, 0:512], ones_rb[0:1, 0:128], brow[:, 0:512], True, False,
                   waits=[tok_bv2, half_free[hh], tok_wv, tok_ones])
                mm(ps[:, 512:1024], ones_rb[0:1, 0:128], brow[:, 512:1024], True, False)
                for fc in range(NF):
                    mm(ps[:, 0:512], XBF[fc][:, i * 128:(i + 1) * 128],
                       wslice(GWv, fc, 0, 512), False, fc == NF - 1)
                    tmm = mm(ps[:, 512:1024], XBF[fc][:, i * 128:(i + 1) * 128],
                             wslice(GWv, fc, 512, 512), False, fc == NF - 1)
                tev = vop(lambda e, ps=ps, v=VT[i]:
                          e.tensor_copy(v[:, 0:1040].rearrange("p (h c) -> p h c", h=16)[:, :, 0:64],
                                        ps.rearrange("p (h c) -> p h c", h=16)),
                          waits=[tmm, tok_vones[i]])
                half_free[hh] = tev
                vproj_done.append(tev)
            ar.free("GW")

            tap("QT", QT, tok_qt)
            tap("KT", KT, tok_kt)
            tap("VT", VT, vproj_done)

            if LVL >= 2:
                p.barrier()
                ar.free("XBF")

                ar.alloc("PT", 8 * 1024)
                ar.alloc("EXT", 8 * 1024)
                ar.alloc("EXT2", 8 * 1024)
                ar.alloc("SQG", 8 * 1024)
                for c in range(NF):
                    ar.alloc(f"OT{c}", 4 * 1024)
                PT = [ar.view("PT", 0, 128, i * T, T, BF16) for i in range(2)]
                QX = ar.view("EXT", 0, 65, 0, T, BF16)
                KX = ar.view("EXT", 0, 65, T, T, BF16)
                OT = [ar.view(f"OT{c}", 0, 128, 0, T, BF16) for c in range(NF)]
                ROW_L = [rowbuf[0:1, :], rowb2[0:1, :]]
                GB64 = [ar.view("G0", 0, 64, 0, T, F32), ar.view("G1", 0, 64, 0, T, F32)]
                QX2 = [QX, ar.view("EXT2", 0, 65, 0, T, BF16)]
                KX2 = [KX, ar.view("EXT2", 0, 65, T, T, BF16)]
                PT2 = [[PT[0][:, 0:1024], PT[0][:, 1024:2048]],
                       [PT[1][:, 0:1024], PT[1][:, 1024:2048]]]
                SQG = [ar.view("SQG", 0, 128, i * T, T, BF16) for i in range(2)]
                tok_kxones = [vop(lambda e: e.memset(KX2[0][64:65, :], 1.0)),
                              vop(lambda e: e.memset(KX2[1][64:65, :], 1.0))]

                ot_done = []
                lane_state = [
                    dict(prev_norm=None, prev_bcopy=None, prev_exp=None,
                         pt_av=[None, None], sq_cons=None),
                    dict(prev_norm=None, prev_bcopy=None, prev_exp=None,
                         pt_av=[None, None], sq_cons=None),
                ]

                def gen_head(h, lane):
                    st = lane_state[lane]
                    PG = PGA if lane == 0 else PGB
                    ch, sub = h // 2, (h % 2) * 64
                    QXl, KXl = QX2[lane], KX2[lane]
                    tq = vop(lambda e: e.tensor_copy(QXl[0:64, :], QT[ch][sub:sub + 64, :]),
                             waits=[tok_qt[ch], st["prev_exp"]])
                    tk = vop(lambda e: e.tensor_copy(KXl[0:64, :], KT[ch][sub:sub + 64, :]),
                             waits=[tok_kt[ch], tok_kxones[lane], st["prev_exp"]])
                    sqg = SQG[lane]
                    tsq = sop(lambda e: e.activation(sqg[sub:sub + 64, :],
                                                     QT[ch][sub:sub + 64, :], AF.Square),
                              waits=[tok_qt[ch], st["sq_cons"]])
                    ssq = []
                    for tc in range(4):
                        ssq.append(mm(PG[0:1, tc * 512:(tc + 1) * 512],
                                      ones_cb[sub:sub + 64, :],
                                      sqg[sub:sub + 64, tc * 512:(tc + 1) * 512],
                                      True, True,
                                      waits=[tsq, tok_ones_cb,
                                             st["prev_bcopy"], st["prev_norm"]],
                                      tpos=(sub, 0)))
                    yield
                    tsk = sop(lambda e: e.activation(sqg[sub:sub + 64, :],
                                                     KT[ch][sub:sub + 64, :], AF.Square),
                              waits=[tok_kt[ch]] + ssq)
                    ssk = []
                    for tc in range(4):
                        ssk.append(mm(PG[32:33, tc * 512:(tc + 1) * 512],
                                      ones_cb[sub:sub + 64, :],
                                      sqg[sub:sub + 64, tc * 512:(tc + 1) * 512],
                                      True, True, waits=[tsk], tpos=(sub, 32)))
                    st["sq_cons"] = ssk[-1]
                    yield
                    mk2 = m_f[0:1, lane:lane + 1]
                    tmx = vop(lambda e, PG=PG, mk2=mk2:
                              e.reduce_max(mk2, PG[32:33, :], axis=AX.X), waits=ssk)
                    tsx = sop(lambda e, mk2=mk2: e.activation(mk2, mk2, AF.Sqrt),
                              waits=[tmx])
                    tnegmk = vop(lambda e, mk2=mk2: e.tensor_scalar_mul(mk2, mk2, -1.0),
                                 waits=[tsx])
                    rslot = ROW_L[lane]
                    tr1 = sop(lambda e, PG=PG, rslot=rslot:
                              e.activation(rslot, PG[0:1, :], AF.Sqrt), waits=ssq)
                    srow = sop(lambda e, rslot=rslot, mk2=mk2, QXl=QXl:
                               e.activation(QXl[64:65, :], rslot, AF.Copy, scale=mk2),
                               waits=[tr1, tnegmk, tq])
                    scal_mrow = srow
                    prev_max = [ssq[-1], ssk[-1], tr1, tmx]
                    yield
                    norm_toks = []
                    for qh in range(2):
                        q0 = qh * 1024
                        rot_free = prev_max if qh == 0 else norm_toks[-1]
                        avtoks = []
                        tav = None
                        rot_free = [rot_free, rot_free]
                        for kt_i in range(NT):
                            ptb = PT2[kt_i % 2][lane]
                            texp = [None, None]
                            for qc in range(2):
                                tmm = mm(PG[:, qc * 512:(qc + 1) * 512],
                                         KXl[:, kt_i * 128:(kt_i + 1) * 128],
                                         QXl[:, q0 + qc * 512:q0 + (qc + 1) * 512],
                                         True, True,
                                         waits=[tk, scal_mrow, rot_free[qc]])
                                texp[qc] = sop(lambda e, ptb=ptb, PG=PG, qc=qc:
                                               e.activation(ptb[:, qc * 512:(qc + 1) * 512],
                                                            PG[:, qc * 512:(qc + 1) * 512],
                                                            AF.Exp, scale=GSCALE),
                                               waits=[tmm, st["pt_av"][kt_i % 2]])
                            rot_free = texp
                            yield
                            for qc in range(2):
                                tav = mm(PG[0:65, 1024 + qc * 512:1024 + (qc + 1) * 512],
                                         VT[kt_i][:, h * 65:(h + 1) * 65],
                                         ptb[:, qc * 512:(qc + 1) * 512],
                                         kt_i == 0, kt_i == NT - 1,
                                         waits=[texp[qc], vproj_done[kt_i],
                                                (prev_max if qh == 0 else norm_toks[-1])
                                                if kt_i == 0 else None])
                                avtoks.append(tav)
                            st["pt_av"][kt_i % 2] = tav
                            yield
                        sop(lambda e, PG=PG, q0=q0, lane=lane:
                            e.activation(ROW_L[lane][0:1, q0:q0 + 1024],
                                         PG[64:65, 1024:2048], AF.Ln),
                            waits=avtoks[-2:])
                        trc = sop(lambda e, q0=q0, lane=lane:
                                  e.activation(ROW_L[lane][0:1, q0:q0 + 1024],
                                               ROW_L[lane][0:1, q0:q0 + 1024],
                                               AF.Exp, scale=-1.0))
                        yield
                        yield
                        tb = []
                        for qc in range(2):
                            tb.append(mm(PG[0:64, qc * 512:(qc + 1) * 512], ones_r[0:1, 0:64],
                                         ROW_L[lane][0:1, q0 + qc * 512:q0 + (qc + 1) * 512],
                                         True, True, waits=[trc, rot_free]))
                        tbc = sop(lambda e, PG=PG, lane=lane:
                                  e.copy(GB64[lane][:, 0:1024], PG[0:64, 0:1024]), waits=tb)
                        tnm = vop(lambda e, PG=PG, lane=lane, q0=q0:
                                  e.tensor_mul(OT[ch][sub:sub + 64, q0:q0 + 1024],
                                               PG[0:64, 1024:2048], GB64[lane][:, 0:1024]),
                                  waits=[tbc] + avtoks[-2:])
                        norm_toks.append([tbc, tnm])
                        yield
                    st["prev_bcopy"] = norm_toks[-1][0]
                    st["prev_norm"] = norm_toks[-1][1]
                    st["prev_exp"] = rot_free
                    ot_done.append(norm_toks[-1][1])

                active = {0: gen_head(0, 0), 1: gen_head(1, 1)}
                nxt = [2, 3]
                while active:
                    for lane in (0, 1):
                        g = active.get(lane)
                        if g is None:
                            continue
                        try:
                            next(g)
                        except StopIteration:
                            if nxt[lane] < NH:
                                active[lane] = gen_head(nxt[lane], lane)
                                nxt[lane] += 2
                            else:
                                del active[lane]

                tap("OT", OT, ot_done)

                if LVL >= 3:
                    p.barrier()
                    for i in range(NT):
                        ar.free(f"V{i}")
                    ar.free("PT")
                    ar.free("EXT")
                    ar.free("EXT2")
                    ar.free("SQG")
                    ar.free("QT")
                    ar.free("KT")
                    for c in range(NF):
                        ar.alloc(f"XRb{c}", 8 * 1024)
                    ar.alloc("WOS", 4 * 1024)
                    XR2 = [ar.view(f"XRb{c}", 0, 128, 0, T, F32) for c in range(NF)]
                    WOS = [ar.view("WOS", 0, 128, i * D, D, BF16) for i in range(2)]
                    wos_cons = [None, None]

                    tok_bo = dma("gpsimd", bo_c_s[:], dins["bo_c"], stream="boc")
                    for c in range(NF):
                        dma("sync", XR2[c], spill_x[c], waits=[tok_xspill], stream="xrl")
                    tok_xrl = [("xrl", p.counts["xrl"])] * NF

                    grp = [None, None]
                    resid_done = []
                    for oc in range(NF):
                        g = oc % 2
                        PG = PGA if g == 0 else PGB
                        sl = oc % 2
                        tok_wo = dma("gpsimd",
                                     WOS[sl].rearrange("p (c m) -> p c m", c=NF),
                                     dins["wo"][:, oc * 128:(oc + 1) * 128]
                                     .rearrange("(c p) m -> p c m", p=128),
                                     waits=[wos_cons[sl]], stream=f"wos{sl}")
                        tmm = None
                        for fc in range(NF):
                            for tc in range(4):
                                tmm = mm(PG[:, tc * 512:(tc + 1) * 512],
                                         WOS[sl][:, fc * 128:(fc + 1) * 128],
                                         OT[fc][:, tc * 512:(tc + 1) * 512],
                                         fc == 0, fc == NF - 1,
                                         waits=[ot_done[-1], tok_wo] +
                                               ([grp[g]] if fc == 0 else []))
                        wos_cons[sl] = tmm
                        tadd = vop(lambda e, c=oc, PG=PG, oc=oc:
                                   e.scalar_tensor_tensor(
                                       XR2[c][:], PG[:], bo_c_s[:, oc:oc + 1],
                                       XR2[c][:], ALU.add, ALU.add),
                                   waits=[tmm, tok_xrl[oc], tok_bo])
                        grp[g] = tadd
                        resid_done.append(tadd)
                    for c in range(NF):
                        ar.alloc(f"ZBc{c}", 4 * 1024)
                    ar.alloc("SQ", 4 * 1024)
                    ar.free("WOS")
                    for c in range(NF):
                        ar.free(f"OT{c}")
                    ZB = [ar.view(f"ZBc{c}", 0, 128, 0, T, BF16) for c in range(NF)]
                    SQ = ar.view("SQ", 0, 128, 0, T, BF16)

                    def ln_stats(xr_list, zb_list, waits_per_chunk, pg_free):
                        tok_sq_mm = None
                        zb_toks = []
                        for fc in range(NF):
                            tz = sop(lambda e, zf=zb_list[fc], xf=xr_list[fc]:
                                     e.copy(zf[:], xf[:]),
                                     waits=[waits_per_chunk[fc]])
                            zb_toks.append(tz)
                            tsq = sop(lambda e, zf=zb_list[fc]: e.activation(SQ, zf[:], AF.Square),
                                      waits=[tz, tok_sq_mm])
                            for tc in range(4):
                                mm(PGA[0:1, tc * 512:(tc + 1) * 512], ones_cb[:],
                                   zb_list[fc][:, tc * 512:(tc + 1) * 512],
                                   fc == 0, fc == NF - 1,
                                   waits=[tz, tok_ones_cb, pg_free if fc == 0 else None])
                                tok_sq_mm = mm(PGA[32:33, tc * 512:(tc + 1) * 512], ones_cb[:],
                                               SQ[:, tc * 512:(tc + 1) * 512],
                                               fc == 0, fc == NF - 1, waits=[tsq],
                                               tpos=(0, 32))
                        return tok_sq_mm, zb_toks

                    tok_stat, _ = ln_stats(XR2, ZB, resid_done, grp[0])

                    s_mu = sop(lambda e: e.activation(rowbuf[0:1, :], PGA[0:1, :], AF.Copy,
                                                      scale=1.0 / D), waits=[tok_stat])
                    s_e2 = sop(lambda e: e.activation(G1[0:1, :], PGA[32:33, :], AF.Copy,
                                                      scale=1.0 / D))
                    s_m2 = sop(lambda e: e.activation(G0[0:1, :], rowbuf[0:1, :], AF.Square))
                    v_var = vop(lambda e: e.tensor_sub(G1[0:1, :], G1[0:1, :],
                                                       G0[0:1, :]), waits=[s_m2, s_e2])
                    s_ln = sop(lambda e: e.activation(G0[0:1, :], G1[0:1, :], AF.Ln,
                                                      bias=eps_ln[:]), waits=[v_var])
                    s_rstd = sop(lambda e: e.activation(G1[0:1, :], G0[0:1, :], AF.Exp,
                                                        scale=-0.5))

                    tmb = []
                    for tc in range(4):
                        tmb.append(mm(PGA[:, tc * 512:(tc + 1) * 512], ones_r[0:1, 0:128],
                                      rowbuf[0:1, tc * 512:(tc + 1) * 512], True, True,
                                      waits=[s_e2, s_mu]))
                    trb = []
                    for tc in range(4):
                        trb.append(mm(PGB[:, tc * 512:(tc + 1) * 512], ones_r[0:1, 0:128],
                                      G1[0:1, tc * 512:(tc + 1) * 512], True, True,
                                      waits=[s_rstd, grp[1]]))
                    x2_done = []
                    tok_x2sp = []
                    for fc in range(NF):
                        t1 = vop(lambda e, c=fc: e.tensor_sub(G0[:], XR2[c][:], PGA[:]),
                                 waits=tmb + [tok_stat, s_rstd])
                        t2 = vop(lambda e, c=fc: e.tensor_mul(XR2[c][:], G0[:], PGB[:]),
                                 waits=trb)
                        x2_done.append(t2)
                        tok_x2sp.append(dma("sync", spill_x2[fc], XR2[fc], waits=[t2],
                                            stream=f"sp2{fc}"))
                    tap("X2", XR2, x2_done)

                    if LVL >= 4:
                        for c in range(NF):
                            ar.free(f"ZBc{c}")
                        ar.free("SQ")

                        def make_xs(XS, XF, xf_ready, xg=()):
                            xs_tok = []
                            for c in range(NF):
                                if c < 4:
                                    tt = sop(lambda e, c=c: e.copy(XS[c][:], XF[c][:]),
                                             waits=[xf_ready[c]] + list(xg))
                                else:
                                    tv = vop(lambda e, c=c: e.memset(XS[c][:, 0:1], 0.0),
                                             waits=[xf_ready[c]] + list(xg))
                                    tt = sop(lambda e, c=c:
                                             e.copy(XS[c][:, 1:T], XF[c][:, 0:T - 1]),
                                             waits=[xf_ready[c]] + list(xg))
                                    tt = [tt, tv]
                                xs_tok.append(tt)
                            return xs_tok

                        def emit_ffn_layer(l, XF, xf_ready, pg_init, region_free=(),
                                           xf_src=None, final=False, x2_src=None):
                            LWd = dins
                            ar.alloc("SQL", 4 * 1024)
                            ar.alloc("SQL2", 4 * 1024)
                            ar.alloc("QKS2", 4 * 1024)
                            ar.alloc("GWL", 8 * 1024)
                            pre_guard = [p.now("T"), p.now("S")]
                            t_qs = dma("sync", qs_c[:], LWd[f"l{l}_qs"],
                                       waits=[p.now("V")], stream="qs")
                            t_ks = dma("sync", ks_c[:], LWd[f"l{l}_ks"],
                                       waits=[p.now("V")], stream="ks")
                            t_lng = dma("sync", lng_c[:], LWd[f"l{l}_lng"],
                                        waits=[p.now("V")], stream="lng")
                            t_lnb = dma("sync", lnb_c[:], LWd[f"l{l}_lnb"],
                                        waits=[p.now("V")], stream="lnb")

                            for c in range(NF):
                                ar.alloc(f"XS{c}", 4 * 1024)
                            XS = [ar.view(f"XS{c}", 0, 128, 0, T, BF16) for c in range(NF)]
                            xs_tok = make_xs(XS, XF, xf_ready,
                                             xg=[p.now("T")] if l > 0 else []) \
                                + list(region_free)
                            for c in range(NF):
                                ar.free(f"XFl{c}" if l > 0 else f"XRb{c}")

                            for c in range(NF):
                                ar.alloc(f"QKT{c}", 4 * 1024)
                            for i in range(NT):
                                ar.alloc(f"vL{i}", 520 * 2)
                            for i in range(4):
                                ar.alloc(f"NRM{i}", 8 * 1024)
                            ar.alloc("ROWS0", 8 * 1024)
                            ar.alloc("ROWS1", 8 * 1024)
                            QKT = [ar.view(f"QKT{c}", 0, 128, 0, T, BF16) for c in range(NF)]
                            VL = [ar.view(f"vL{i}", 0, 128, 0, 520, BF16) for i in range(NT)]
                            SQLv = [ar.view("SQL", 0, 128, 0, T, BF16),
                                    ar.view("SQL2", 0, 128, 0, T, BF16)]
                            QKS2 = [ar.view("QKS2", 0, 128, i * D, D, BF16) for i in range(2)]
                            GWL = ar.view("GWL", 0, 128, 0, NF * 512, BF16)
                            NRM = [ar.view(f"NRM{i}", 0, 128, 0, T, F32) for i in range(4)]
                            ROWS = [ar.view("ROWS0", 0, 128, 0, T, F32),
                                    ar.view("ROWS1", 0, 128, 0, T, F32)]
                            tok_wvl = dma("gpsimd",
                                          GWL.rearrange("p (c m) -> p c m", c=NF),
                                          LWd[f"l{l}_wv"].rearrange("(c p) m -> p c m", p=128),
                                          waits=pre_guard, stream="gwv")

                            s1state = [dict(pg_free=pg_init[0], cons=None),
                                       dict(pg_free=pg_init[1], cons=None)]
                            raw_done = [None] * NF
                            rows_done = [None] * NF

                            def s1gen(oc, lane):
                                st = s1state[lane]
                                PG = PGA if lane == 0 else PGB
                                SQl = SQLv[lane]
                                nb = 64 * (oc % 2)
                                rA = NRM[oc // 2][nb:nb + 1, :]
                                rB = NRM[oc // 2][nb + 32:nb + 33, :]
                                tok_w = dma("gpsimd",
                                            QKS2[lane].rearrange("p (c m) -> p c m", c=NF),
                                            LWd[f"l{l}_wqk"][:, oc * 128:(oc + 1) * 128]
                                            .rearrange("(c p) m -> p c m", p=128),
                                            waits=[st["cons"]] + pre_guard,
                                            stream=f"qks2{lane}")
                                tmm = None
                                for fc in range(NF):
                                    for tc in range(4):
                                        tmm = mm(PG[:, tc * 512:(tc + 1) * 512],
                                                 QKS2[lane][:, fc * 128:(fc + 1) * 128],
                                                 XS[fc][:, tc * 512:(tc + 1) * 512],
                                                 fc == 0, fc == NF - 1,
                                                 waits=[tok_w, st["pg_free"], xs_tok[fc]] +
                                                       (list(region_free)
                                                        if oc < 2 and fc == 0 else []))
                                st["cons"] = tmm
                                yield
                                tcpy = sop(lambda e, d=QKT[oc], PG=PG: e.copy(d[:], PG[:]),
                                           waits=[tmm])
                                tsq = sop(lambda e, PG=PG, SQl=SQl:
                                          e.activation(SQl[:], PG[:], AF.Square))
                                raw_done[oc] = tcpy
                                yield
                                ssA, ssB = [], []
                                for tc in range(4):
                                    ssA.append(mm(PG[0:1, tc * 512:(tc + 1) * 512],
                                                  ones_cb[0:64, :],
                                                  SQl[0:64, tc * 512:(tc + 1) * 512],
                                                  True, True, waits=[tsq, tok_ones_cb]))
                                    ssB.append(mm(PG[32:33, tc * 512:(tc + 1) * 512],
                                                  ones_cb[64:128, :],
                                                  SQl[64:128, tc * 512:(tc + 1) * 512],
                                                  True, True, tpos=(64, 32)))
                                yield
                                vA = vop(lambda e, rA=rA, PG=PG: e.tensor_scalar_max(
                                    rA, PG[0:1, :], 1e-24), waits=ssA)
                                vB = vop(lambda e, rB=rB, PG=PG: e.tensor_scalar_max(
                                    rB, PG[32:33, :], 1e-24), waits=ssB)
                                sop(lambda e, rA=rA: e.activation(rA, rA, AF.Ln), waits=[vA])
                                sA2 = sop(lambda e, rA=rA: e.activation(rA, rA, AF.Exp,
                                                                        scale=-0.5))
                                sop(lambda e, rB=rB: e.activation(rB, rB, AF.Ln), waits=[vB])
                                sB2 = sop(lambda e, rB=rB: e.activation(rB, rB, AF.Exp,
                                                                        scale=-0.5))
                                st["pg_free"] = [vA, vB]
                                rows_done[oc] = [sA2, sB2]

                            s1act = {0: s1gen(0, 0), 1: s1gen(1, 1)}
                            s1nxt = [2, 3]
                            while s1act:
                                for lane in (0, 1):
                                    g = s1act.get(lane)
                                    if g is None:
                                        continue
                                    try:
                                        next(g)
                                    except StopIteration:
                                        if s1nxt[lane] < NF:
                                            s1act[lane] = s1gen(s1nxt[lane], lane)
                                            s1nxt[lane] += 2
                                        else:
                                            del s1act[lane]

                            vl_ones = [vop(lambda e, v=VL[i]: e.memset(v[:, 64:520:65], 1.0),
                                           waits=xs_tok) for i in range(NT)]
                            vl_done = []
                            vhf = [s1state[0]["pg_free"], s1state[1]["pg_free"]]
                            for i in range(NT):
                                hh = i % 2
                                ps = PGA[:, 0:512] if hh == 0 else PGB[:, 0:512]
                                tmm = None
                                for fc in range(NF):
                                    tmm = mm(ps, XS[fc][:, i * 128:(i + 1) * 128],
                                             GWL[:, fc * 512:(fc + 1) * 512],
                                             fc == 0, fc == NF - 1,
                                             waits=[tok_wvl, vhf[hh]])
                                tev = vop(lambda e, ps=ps, v=VL[i]:
                                          e.tensor_copy(
                                              v[:, 0:520].rearrange("p (h c) -> p h c", h=8)[:, :, 0:64],
                                              ps.rearrange("p (h c) -> p h c", h=8)),
                                          waits=[tmm, vl_ones[i]])
                                vhf[hh] = tev
                                vl_done.append(tev)

                            for c in range(NF):
                                ar.free(f"XS{c}")
                            ar.free("GWL")
                            for nmx in ("XCaq", "XCsq", "XCak", "XCsk"):
                                ar.alloc(nmx, 8 * 1024)
                            XAQ = ar.view("XCaq", 0, 128, 0, T, F32)
                            XSQ = ar.view("XCsq", 0, 128, 0, T, F32)
                            XAK = ar.view("XCak", 0, 128, 0, T, F32)
                            XSK = ar.view("XCsk", 0, 128, 0, T, F32)
                            xcg = [p.now("T"), p.now("V")]
                            dma("sync", XAQ, dins["x_aq"], waits=xcg, stream="xc")
                            dma("sync", XSQ, dins["x_sq"], stream="xc")
                            dma("sync", XAK, dins["x_ak"], stream="xc")
                            dma("sync", XSK, dins["x_sk"], stream="xc")
                            tok_xc = ("xc", p.counts["xc"])

                            qk_outs = []
                            s2_pg = [vhf[0], vhf[1]]
                            rows_cons = [None, None]
                            for oc in range(NF):
                                lane = oc % 2
                                is_q = oc < 4
                                PG = PGA if lane == 0 else PGB
                                RW = ROWS[lane]
                                nb = 64 * (oc % 2)
                                rA = NRM[oc // 2][nb:nb + 1, :]
                                rB = NRM[oc // 2][nb + 32:nb + 33, :]
                                scpA = sop(lambda e, rA=rA, RW=RW: e.copy(RW[0:1, :], rA),
                                           waits=[rows_done[oc][0], rows_cons[lane]])
                                scpB = sop(lambda e, rB=rB, RW=RW: e.copy(RW[64:65, :], rB),
                                           waits=[rows_done[oc][1]])
                                btoks = []
                                for tc in range(4):
                                    btoks.append(mm(PG[0:64, tc * 512:(tc + 1) * 512],
                                                    ones_r[0:1, 0:64],
                                                    RW[0:1, tc * 512:(tc + 1) * 512],
                                                    True, True,
                                                    waits=[scpA, s2_pg[lane], tok_ones]))
                                    btoks.append(mm(PG[64:128, tc * 512:(tc + 1) * 512],
                                                    ones_r[64:65, 0:64],
                                                    RW[64:65, tc * 512:(tc + 1) * 512],
                                                    True, True, tpos=(64, 64),
                                                    waits=[scpB]))
                                rows_cons[lane] = btoks
                                scol = qs_c if is_q else ks_c
                                tqn = vop(lambda e, sc=scol, s=QKT[oc], PG=PG:
                                          e.scalar_tensor_tensor(G0[:], s[:], sc[:], PG[:],
                                                                 ALU.mult, ALU.mult),
                                          waits=btoks + [t_qs, t_ks, raw_done[oc]])
                                s2_pg[lane] = tqn
                                CA = XAQ if is_q else XAK
                                CS2 = XSQ if is_q else XSK
                                tsh = vop(lambda e: e.stream_shuffle(rowbuf[:], G0[:],
                                                                     SHUF_SWAP))
                                tm1 = vop(lambda e, CA=CA: e.tensor_mul(G1[:], G0[:], CA),
                                          waits=[tok_xc])
                                tm2 = vop(lambda e, CS2=CS2:
                                          e.tensor_mul(rowbuf[:], rowbuf[:], CS2),
                                          waits=[tok_xc])
                                t4 = vop(lambda e, d=QKT[oc]: e.tensor_add(d[:], G1[:],
                                                                           rowbuf[:]),
                                         waits=[tm1, tm2])
                                qk_outs.append(t4)
                            grpf = [s2_pg[0], s2_pg[1]]

                            tap(f"QKT_l{l}", QKT, qk_outs)
                            tap(f"VL_l{l}", VL, vl_done)
                            for nmx in ("XCaq", "XCsq", "XCak", "XCsk"):
                                ar.free(nmx)
                            ar.free("SQL"); ar.free("SQL2"); ar.free("QKS2")
                            for i in range(4):
                                ar.free(f"NRM{i}")
                            ar.free("ROWS0"); ar.free("ROWS1")
                            for c in range(4):
                                ar.alloc(f"OTL{c}", 4 * 1024)
                            ar.alloc("EXTL", 8 * 1024)
                            ar.alloc("EXTL2", 8 * 1024)
                            ar.alloc("PTL", 4 * 1024)
                            ar.alloc("LWO", 8 * 1024)
                            for c in range(NF):
                                ar.alloc(f"XFl{c}", 8 * 1024)
                            XFl = [ar.view(f"XFl{c}", 0, 128, 0, T, F32)
                                   for c in range(NF)]
                            xsrc_buf, xsrc_tok = xf_src
                            xg = [p.now("T"), p.now("V"), p.now("S")]
                            tok_xlr = [dma("sync", XFl[c], xsrc_buf[c],
                                           waits=[xsrc_tok[c]] + xg,
                                           stream=f"xlr{l}{c}")
                                       for c in range(NF)]
                            OTL = [ar.view(f"OTL{c}", 0, 128, 0, T, BF16) for c in range(4)]
                            QXL = [ar.view("EXTL", 0, 65, 0, T, BF16),
                                   ar.view("EXTL2", 0, 65, 0, T, BF16)]
                            KXL = [ar.view("EXTL", 0, 65, T, T, BF16),
                                   ar.view("EXTL2", 0, 65, T, T, BF16)]
                            PTL = [[ar.view("PTL", 0, 128, i * 512, 512, BF16)
                                    for i in range(2)],
                                   [ar.view("PTL", 0, 128, (2 + i) * 512, 512, BF16)
                                    for i in range(2)]]
                            LWO = ar.view("LWO", 0, 128, 0, 4 * D, BF16)
                            tok_wol = dma("gpsimd",
                                          LWO.rearrange("p (c m) -> p c m", c=4),
                                          LWd[f"l{l}_wo"].rearrange("(c p) m -> p c m", p=128),
                                          waits=[p.now("T"), p.now("V"), p.now("S")],
                                          stream="lwo")
                            tok_kxo = [vop(lambda e: e.memset(KXL[0][64:65, :], 1.0),
                                           waits=[p.now("S")]),
                                       vop(lambda e: e.memset(KXL[1][64:65, :], 1.0),
                                           waits=[p.now("S")])]

                            otl_done = []
                            lstate = [dict(prev_norm=None, prev_bcopy=None, prev_exp=None,
                                           pt_av=[None, None]),
                                      dict(prev_norm=None, prev_bcopy=None, prev_exp=None,
                                           pt_av=[None, None])]

                            def lgen_head(h, lane):
                                st = lstate[lane]
                                PG = PGA if lane == 0 else PGB
                                ch, sub = h // 2, (h % 2) * 64
                                QXl, KXl = QXL[lane], KXL[lane]
                                mf, mb = m_f[:, lane:lane + 1], m_bf[:, lane:lane + 1]
                                tqx = vop(lambda e: e.tensor_copy(QXl[0:64, :],
                                                                  QKT[ch][sub:sub + 64, :]),
                                          waits=[qk_outs[ch], st["prev_norm"],
                                                 st["prev_exp"]])
                                tkx = vop(lambda e: e.tensor_copy(KXl[0:64, :],
                                                                  QKT[4 + ch][sub:sub + 64, :]),
                                          waits=[qk_outs[4 + ch], tok_kxo[lane]])
                                for w in range(NW):
                                    kcol0 = (w - 1) * 512 if w > 0 else 0
                                    kw = 1024 if w > 0 else 512
                                    nkt = kw // 128
                                    scal_mrow = st.get("scal_mrow")
                                    prev_max = st.get("win_free")
                                    for qt_i in range(4):
                                        q0 = w * 512 + qt_i * 128
                                        ktoks = []
                                        for kc in range(kw // 512):
                                            tmm = mm(PG[:, kc * 512:(kc + 1) * 512],
                                                     QKT[ch][sub:sub + 64, q0:q0 + 128],
                                                     QKT[4 + ch][sub:sub + 64,
                                                                 kcol0 + kc * 512:kcol0 + (kc + 1) * 512],
                                                     True, True,
                                                     waits=[tqx, tkx, prev_max])
                                            ktoks.append(tmm)
                                        tmax = vop(lambda e, kw=kw, PG=PG, mf=mf:
                                                   e.reduce_max(mf, PG[:, 0:kw:MAXSTRIDE],
                                                                axis=AX.X), waits=ktoks)
                                        tneg = vop(lambda e, mf=mf, mb=mb:
                                                   e.tensor_scalar_mul(mb, mf, -1.0))
                                        yield
                                        tmr = mm(PG[0:1, 1536 + qt_i * 128:1536 + (qt_i + 1) * 128],
                                                 mb, ident_b[:], True, True,
                                                 waits=[tneg, tok_c["ident_bf"], scal_mrow,
                                                        st["pt_av"][0] if (w == 0 and qt_i == 0)
                                                        else None,
                                                        st["pt_av"][1] if (w == 0 and qt_i == 0)
                                                        else None])
                                        scal_mrow = sop(lambda e, q0=q0, qt_i=qt_i, PG=PG,
                                                        QXl=QXl:
                                                        e.copy(QXl[64:65, q0:q0 + 128],
                                                               PG[0:1, 1536 + qt_i * 128:1536 + (qt_i + 1) * 128]),
                                                        waits=[tmr, tqx])
                                        prev_max = [tmax, scal_mrow]
                                        yield
                                    st["scal_mrow"] = scal_mrow
                                    avtoks = []
                                    rot_free = [prev_max, prev_max]
                                    own0 = (kw - 512) // 128
                                    order = [(own0 + i2, i2 * 128, 512, "own") for i2 in range(4)]
                                    order += [(i2, 0, (i2 + 1) * 128, "prev")
                                              for i2 in range(nkt - 4)]
                                    for idx, (kt, qlo, qhi, kind) in enumerate(order):
                                        width = qhi - qlo
                                        rb = idx % 2
                                        ptb = PTL[lane][rb]
                                        tmm = mm(PG[:, rb * 512:rb * 512 + width],
                                                 KXl[:, kcol0 + kt * 128:kcol0 + (kt + 1) * 128],
                                                 QXl[:, w * 512 + qlo:w * 512 + qhi],
                                                 True, True,
                                                 waits=[tkx, scal_mrow, rot_free[rb]])
                                        texp = sop(lambda e, rb=rb, ptb=ptb, qlo=qlo,
                                                   width=width, PG=PG:
                                                   e.activation(ptb[:, qlo:qlo + width],
                                                                PG[:, rb * 512:rb * 512 + width],
                                                                AF.Exp),
                                                   waits=[tmm, st["pt_av"][rb]])
                                        rot_free[rb] = texp
                                        if kind == "own":
                                            tmask = vop(lambda e, ptb=ptb, qlo=qlo:
                                                        e.tensor_mul(ptb[:, qlo:qlo + 128],
                                                                     ptb[:, qlo:qlo + 128],
                                                                     tri_u[:]),
                                                        waits=[texp, tok_c["tri_u"]])
                                        else:
                                            tmask = vop(lambda e, ptb=ptb, qhi=qhi:
                                                        e.tensor_mul(ptb[:, qhi - 128:qhi],
                                                                     ptb[:, qhi - 128:qhi],
                                                                     tri_l[:]),
                                                        waits=[texp, tok_c["tri_l"]])
                                        yield
                                        gkt = kcol0 // 128 + kt
                                        tav = mm(PG[0:65, 1024 + qlo:1024 + qhi],
                                                 VL[gkt][:, h * 65:(h + 1) * 65],
                                                 ptb[:, qlo:qhi],
                                                 idx == 0, idx == len(order) - 1,
                                                 waits=[tmask, vl_done[gkt],
                                                        st["prev_norm"] if idx == 0 else None])
                                        st["pt_av"][rb] = tav
                                        avtoks.append(tav)
                                        yield
                                    sop(lambda e, PG=PG, lane=lane:
                                        e.activation(ROW_L[lane][0:1, 0:512],
                                                     PG[64:65, 1024:1536], AF.Ln),
                                        waits=avtoks)
                                    trc = sop(lambda e, lane=lane:
                                              e.activation(ROW_L[lane][0:1, 0:512],
                                                           ROW_L[lane][0:1, 0:512],
                                                           AF.Exp, scale=-1.0))
                                    yield
                                    yield
                                    tb = mm(PG[0:64, 0:512], ones_r[0:1, 0:64],
                                            ROW_L[lane][0:1, 0:512], True, True,
                                            waits=[trc, rot_free[0]])
                                    tbc = sop(lambda e, PG=PG, lane=lane:
                                              e.copy(GB64[lane][:, 0:512], PG[0:64, 0:512]),
                                              waits=[tb])
                                    tnm = vop(lambda e, PG=PG, lane=lane, w=w:
                                              e.tensor_mul(OTL[ch][sub:sub + 64,
                                                               w * 512:(w + 1) * 512],
                                                           PG[0:64, 1024:1536],
                                                           GB64[lane][:, 0:512]),
                                              waits=[tbc] + avtoks)
                                    st["prev_bcopy"] = tbc
                                    st["prev_norm"] = tnm
                                    st["prev_exp"] = rot_free[1]
                                    st["win_free"] = [rot_free[0], rot_free[1], tbc, tnm]
                                    otl_done.append(tnm)
                                    yield

                            lact = {0: lgen_head(0, 0), 1: lgen_head(1, 1)}
                            lnxt = [2, 3]
                            while lact:
                                for lane in (0, 1):
                                    g = lact.get(lane)
                                    if g is None:
                                        continue
                                    try:
                                        next(g)
                                    except StopIteration:
                                        if lnxt[lane] < LH:
                                            lact[lane] = lgen_head(lnxt[lane], lane)
                                            lnxt[lane] += 2
                                        else:
                                            del lact[lane]
                            prev_norm = [lstate[0]["prev_norm"], lstate[1]["prev_norm"]]
                            prev_bcopy = [lstate[0]["prev_bcopy"], lstate[1]["prev_bcopy"]]

                            tap(f"OTL_l{l}", OTL, otl_done)
                            attn_res = []
                            grpw = [lstate[0]["win_free"], lstate[1]["win_free"]]
                            for oc in range(NF):
                                g = oc % 2
                                PG = PGA if g == 0 else PGB
                                tmm = None
                                for fc in range(4):
                                    for tc in range(4):
                                        tmm = mm(PG[:, tc * 512:(tc + 1) * 512],
                                                 LWO[:, fc * D + oc * 128:fc * D + (oc + 1) * 128],
                                                 OTL[fc][:, tc * 512:(tc + 1) * 512],
                                                 fc == 0, fc == 3,
                                                 waits=[tok_wol, grpw[g], otl_done[-1],
                                                        prev_norm])
                                tadd = vop(lambda e, PG=PG, c=oc:
                                           e.tensor_add(XFl[c][:], XFl[c][:], PG[:]),
                                           waits=[tmm, tok_xlr[oc]])
                                grpw[g] = tadd
                                attn_res.append(tadd)

                            for c in range(NF):
                                ar.free(f"QKT{c}")
                            for i in range(NT):
                                ar.free(f"vL{i}")
                            ar.free("EXTL"); ar.free("EXTL2")
                            ar.free("PTL")
                            for c in range(NF):
                                ar.alloc(f"HB{c}", 4 * 1024)
                            for c in range(NF):
                                ar.alloc(f"ZB2c{c}", 4 * 1024)
                            ar.alloc("SQ2", 4 * 1024)
                            for c in range(4):
                                ar.free(f"OTL{c}")
                            ar.free("LWO")
                            HB = [ar.view(f"HB{c}", 0, 128, 0, T, BF16) for c in range(NF)]
                            ZB2 = [ar.view(f"ZB2c{c}", 0, 128, 0, T, BF16) for c in range(NF)]
                            SQ2 = ar.view("SQ2", 0, 128, 0, T, BF16)

                            tok_sq2 = None
                            tok_s_l = None
                            for fc in range(NF):
                                tz = sop(lambda e, c=fc: e.copy(ZB2[c][:], XFl[c][:]),
                                         waits=[attn_res[fc]])
                                tsq = sop(lambda e, c=fc: e.activation(SQ2, ZB2[c][:], AF.Square),
                                          waits=[tz, tok_sq2])
                                half = 0 if fc < 4 else 1
                                srow = 0 if half == 0 else 32
                                for tc in range(4):
                                    tok_s_l = mm(PGA[srow:srow + 1, tc * 512:(tc + 1) * 512],
                                                 ones_cb[:],
                                                 ZB2[fc][:, tc * 512:(tc + 1) * 512],
                                                 fc % 4 == 0, fc % 4 == 3,
                                                 waits=[tz] + ([grpw[0]] if fc == 0 else []),
                                                 tpos=None if half == 0 else (0, 32))
                                    tok_sq2 = mm(PGB[srow:srow + 1, tc * 512:(tc + 1) * 512],
                                                 ones_cb[:],
                                                 SQ2[:, tc * 512:(tc + 1) * 512],
                                                 fc % 4 == 0, fc % 4 == 3,
                                                 waits=[tsq] + ([grpw[1]] if fc == 0 else []),
                                                 tpos=None if half == 0 else (0, 32))
                            tok_stat2 = [tok_sq2, tok_s_l]
                            r_s2 = sop(lambda e: e.copy(rowbuf[0:1, :], PGA[32:33, :]),
                                       waits=[tok_stat2])
                            v_mu = vop(lambda e: e.tensor_add(G0[0:1, 1:T], PGA[0:1, 1:T],
                                                              rowbuf[0:1, 0:T - 1]),
                                       waits=[r_s2, tok_stat2])
                            v_mu0 = vop(lambda e: e.tensor_copy(G0[0:1, 0:1], PGA[0:1, 0:1]))
                            s_mu = sop(lambda e: e.activation(G1[0:1, :], G0[0:1, :], AF.Copy,
                                                              scale=1.0 / D),
                                       waits=[v_mu, v_mu0])
                            r_q2 = sop(lambda e: e.copy(rowbuf[0:1, :], PGB[32:33, :]))
                            v_e2 = vop(lambda e: e.tensor_add(G0[0:1, 1:T], PGB[0:1, 1:T],
                                                              rowbuf[0:1, 0:T - 1]),
                                       waits=[r_q2, s_mu])
                            v_e20 = vop(lambda e: e.tensor_copy(G0[0:1, 0:1], PGB[0:1, 0:1]))
                            s_e2 = sop(lambda e: e.activation(rowbuf[0:1, :], G0[0:1, :], AF.Copy,
                                                              scale=1.0 / D),
                                       waits=[v_e2, v_e20])
                            s_m2 = sop(lambda e: e.activation(G0[0:1, :], G1[0:1, :], AF.Square))
                            v_var = vop(lambda e: e.tensor_sub(rowbuf[0:1, :], rowbuf[0:1, :],
                                                               G0[0:1, :]), waits=[s_m2, s_e2])
                            s_ln2 = sop(lambda e: e.activation(G0[0:1, :], rowbuf[0:1, :], AF.Ln,
                                                               bias=eps_ln[:]), waits=[v_var])
                            s_rstd2 = sop(lambda e: e.activation(rowbuf[0:1, :], G0[0:1, :],
                                                                 AF.Exp, scale=-0.5))
                            tmb2, trb2 = [], []
                            tmb2, trb2 = [], []
                            for tc in range(4):
                                tmb2.append(mm(PGA[:, tc * 512:(tc + 1) * 512], ones_r[0:1, 0:128],
                                               G1[0:1, tc * 512:(tc + 1) * 512], True, True,
                                               waits=[s_mu, s_e2, s_m2, r_s2, s_rstd2]))
                                trb2.append(mm(PGB[:, tc * 512:(tc + 1) * 512], ones_r[0:1, 0:128],
                                               rowbuf[0:1, tc * 512:(tc + 1) * 512], True, True,
                                               waits=[s_rstd2, r_q2]))
                            hb_done = []
                            t3p = [None, None]
                            for fc in range(NF):
                                if fc < 4:
                                    t1 = vop(lambda e, c=fc: e.tensor_sub(G0[:], XFl[c][:], PGA[:]),
                                             waits=tmb2 + [s_rstd2, s_ln2, attn_res[fc]])
                                else:
                                    vop(lambda e: e.tensor_scalar_mul(G0[:, 0:1], PGA[:, 0:1], -1.0),
                                        waits=tmb2 + [s_rstd2, s_ln2, attn_res[fc]])
                                    t1 = vop(lambda e, c=fc: e.tensor_sub(G0[:, 1:T],
                                                                          XFl[c][:, 0:T - 1],
                                                                          PGA[:, 1:T]))
                                gb = G1 if fc % 2 == 0 else rowbuf
                                t2 = vop(lambda e, gb=gb: e.tensor_mul(gb[:], G0[:], PGB[:]),
                                         waits=trb2 + [t3p[fc % 2]])
                                t3 = sop(lambda e, c=fc, gb=gb: e.activation(
                                    HB[c][:], gb[:], AF.Identity,
                                    scale=lng_c[:, c:c + 1], bias=lnb_c[:, c:c + 1]),
                                    waits=[t2, t_lng, t_lnb])
                                t3p[fc % 2] = t3
                                hb_done.append([t2, t3])
                            tap(f"HB_l{l}", HB, hb_done)
                            for c in range(NF):
                                ar.free(f"ZB2c{c}")
                            ar.free("SQ2")

                            ar.alloc("W1S", 4 * 1024)
                            ar.alloc("W2Sa", 6 * 1024)
                            ar.alloc("W2Sb", 6 * 1024)
                            W1S = [ar.view("W1S", 0, 128, i * D, D, BF16) for i in range(2)]
                            NB0 = NHC // 2
                            wstate = {"cons": [[t3, t3], [t3, t3]]}
                            tok_b1c = dma("gpsimd", b1c_s[:], LWd[f"l{l}_b1c"],
                                          waits=[p.now("V"), p.now("S")], stream="b1c")
                            tok_b2c = dma("gpsimd", b2c_s[:], LWd[f"l{l}_b2c"],
                                          waits=[p.now("V")], stream="b2c")
                            w2state = [[p.now("T"), p.now("S")]] * 2
                            mlp_res = []
                            mlp_add = []
                            pgl = [hb_done[-1], hb_done[-1]]
                            for blk in range(2):
                                ccs = list(range(0, NB0)) if blk == 0 else \
                                      list(range(NB0, NHC))
                                nblk = len(ccs)
                                for cc in ccs:
                                    ar.alloc(f"H2c{cc}", 4 * 1024)
                                H2 = {cc: ar.view(f"H2c{cc}", 0, 128, 0, T, BF16)
                                      for cc in ccs}
                                h2_done = {}
                                for cc in ccs:
                                    toks_pair = {}
                                    for ag in range(2):
                                        mcc = 2 * cc + ag
                                        sl = mcc % 2
                                        tok_w1 = dma("gpsimd",
                                                     W1S[sl].rearrange("p (c m) -> p c m", c=NF),
                                                     LWd[f"l{l}_w1"][:, mcc * 128:(mcc + 1) * 128]
                                                     .rearrange("(c p) m -> p c m", p=128),
                                                     waits=[wstate["cons"][sl][0]] +
                                                           (hb_done[-1] if mcc < 2 else []),
                                                     stream=f"w1s{sl}")
                                        PG = PGA if ag == 0 else PGB
                                        tmm = None
                                        for fc in range(NF):
                                            for tc in range(4):
                                                tmm = mm(PG[:, tc * 512:(tc + 1) * 512],
                                                         W1S[sl][:, fc * 128:(fc + 1) * 128],
                                                         HB[fc][:, tc * 512:(tc + 1) * 512],
                                                         fc == 0, fc == NF - 1,
                                                         waits=[hb_done[fc], tok_w1] +
                                                               ([pgl[ag]] if fc == 0
                                                                else []))
                                        wstate["cons"][sl] = [tmm, tmm]
                                        toks_pair[ag] = tmm
                                    tgel = sop(lambda e, cc=cc:
                                               e.activation(G0[:], PGB[:], AF.Gelu,
                                                            bias=b1c_s[:, 2 * cc + 1:
                                                                       2 * cc + 2]),
                                               waits=[toks_pair[1], tok_b1c])
                                    tmul = vop(lambda e, d=H2[cc], cc=cc:
                                               e.scalar_tensor_tensor(
                                                   d[:], PGA[:],
                                                   b1c_s[:, 2 * cc:2 * cc + 1], G0[:],
                                                   ALU.add, ALU.mult),
                                               waits=[tgel, toks_pair[0], tok_b1c])
                                    pgl = [tmul, tgel]
                                    h2_done[cc] = tmul
                                if blk == 1:
                                    for c in range(NF):
                                        ar.free(f"HB{c}")
                                    ar.free("W1S")
                                    if final:
                                        ar.alloc("X2Ra", 8 * 1024)
                                        ar.alloc("X2Rb", 8 * 1024)
                                        X2R = [ar.view("X2Ra", 0, 128, 0, T, F32),
                                               ar.view("X2Rb", 0, 128, 0, T, F32)]
                                        x2_buf, x2_tok = x2_src
                                        x2g = [p.now("T"), p.now("V"), p.now("S")]
                                        x2prev = [None, None]
                                for oq in range(4):
                                    sl = oq % 2
                                    tok_w2 = dma("gpsimd",
                                                 ar.view(f"W2S{'ab'[sl]}", 0, 128, 0,
                                                         nblk * 256, BF16)
                                                 .rearrange("p (c m) -> p c m", c=nblk),
                                                 LWd[f"l{l}_w2"][ccs[0] * 128:
                                                                 (ccs[0] + nblk) * 128,
                                                                 oq * 256:(oq + 1) * 256]
                                                 .rearrange("(c p) m -> p c m", p=128),
                                                 waits=[w2state[sl]], stream=f"w2s{sl}")
                                    W2Sv = ar.view(f"W2S{'ab'[sl]}", 0, 128, 0,
                                                   nblk * 256, BF16)
                                    tmm = {}
                                    for i_f, cc in enumerate(ccs):
                                        for och in range(2):
                                            PG = PGA if och == 0 else PGB
                                            for tc in range(4):
                                                tmm[och] = mm(
                                                    PG[:, tc * 512:(tc + 1) * 512],
                                                    W2Sv[:, i_f * 256 + och * 128:
                                                         i_f * 256 + (och + 1) * 128],
                                                    H2[cc][:, tc * 512:(tc + 1) * 512],
                                                    i_f == 0, i_f == nblk - 1,
                                                    waits=[h2_done[cc], tok_w2] +
                                                          ([pgl[och]] if i_f == 0
                                                           else []))
                                    w2state[sl] = [tmm[0], tmm[1]]
                                    for och in range(2):
                                        oc = oq * 2 + och
                                        PG = PGA if och == 0 else PGB
                                        if blk == 0:
                                            tadd = vop(lambda e, PG=PG, c=oc, oc=oc:
                                                       e.scalar_tensor_tensor(
                                                           XFl[c][:], PG[:],
                                                           b2c_s[:, oc:oc + 1],
                                                           XFl[c][:], ALU.add, ALU.add),
                                                       waits=[tmm[och], tok_b2c])
                                        else:
                                            tadd = vop(lambda e, PG=PG, c=oc:
                                                       e.tensor_add(XFl[c][:],
                                                                    XFl[c][:], PG[:]),
                                                       waits=[tmm[och]])
                                        pgl[och] = tadd
                                        if blk == 1:
                                            mlp_add.append(tadd)
                                            if final:
                                                tld = dma("sync", X2R[oc % 2], x2_buf[oc],
                                                          waits=[x2_tok[oc],
                                                                 x2prev[oc % 2]] + x2g,
                                                          stream=f"x2l{oc % 2}")
                                                tz2 = vop(lambda e, c=oc:
                                                          e.tensor_add(XFl[c][:], XFl[c][:],
                                                                       X2R[c % 2][:]),
                                                          waits=[tld])
                                                x2prev[oc % 2] = tz2
                                                pgl[och] = tadd
                                                mlp_res.append(tz2)
                                            else:
                                                mlp_res.append(
                                                    dma("sync", spill_xf[oc], XFl[oc][:],
                                                        waits=[tadd], stream=f"sxf{oc}"))
                                for cc in ccs:
                                    ar.free(f"H2c{cc}")
                            ar.free("W2Sa")
                            ar.free("W2Sb")
                            if final:
                                ar.free("X2Ra"); ar.free("X2Rb")
                            return mlp_res, [pgl[0], pgl[1]], XFl, mlp_add

                        mlp_res0, pgf, XFl0, mlp_add0 = emit_ffn_layer(
                            0, XR2, x2_done, [x2_done[-1], x2_done[-1]],
                            region_free=list(tok_x2sp),
                            xf_src=(spill_x2, tok_x2sp))
                        tap("XF0", "dram_xf", mlp_res0)

                        if LVL >= 5:
                            mlp_res1, pgf, XFl1, mlp_add1 = emit_ffn_layer(
                                1, XFl0, mlp_add0, pgf,
                                region_free=list(mlp_res0),
                                xf_src=(spill_xf, mlp_res0),
                                final=(LVL >= 6),
                                x2_src=(spill_x2, tok_x2sp))
                            tap("XF1", XFl1, mlp_res1)

                        if LVL >= 6:
                            for c in range(NF):
                                ar.alloc(f"ZB3c{c}", 4 * 1024)
                            ar.alloc("SQ3", 4 * 1024)
                            XZ = XFl1
                            ZB3 = [ar.view(f"ZB3c{c}", 0, 128, 0, T, BF16) for c in range(NF)]
                            SQ3 = ar.view("SQ3", 0, 128, 0, T, BF16)
                            z_done = mlp_res1
                            tguard = [p.now("T"), p.now("V")]

                            tok_sq3 = None
                            for fc in range(NF):
                                tz = sop(lambda e, c=fc: e.copy(ZB3[c][:], XZ[c][:]),
                                         waits=[z_done[fc],
                                                tguard if fc == 0 else None])
                                tsq = sop(lambda e, c=fc: e.activation(SQ3, ZB3[c][:], AF.Square),
                                          waits=[tz, tok_sq3])
                                for tc in range(4):
                                    mm(PGA[0:1, tc * 512:(tc + 1) * 512], ones_cb[:],
                                       ZB3[fc][:, tc * 512:(tc + 1) * 512],
                                       fc == 0, fc == NF - 1,
                                       waits=[tz] + ([pgf[0], p.now("S")]
                                                     if fc == 0 else []))
                                    tok_sq3 = mm(PGA[32:33, tc * 512:(tc + 1) * 512], ones_cb[:],
                                                 SQ3[:, tc * 512:(tc + 1) * 512],
                                                 fc == 0, fc == NF - 1, waits=[tsq],
                                                 tpos=(0, 32))
                            s_mu = sop(lambda e: e.activation(rowbuf[0:1, :], PGA[0:1, :], AF.Copy,
                                                              scale=1.0 / D), waits=[tok_sq3])
                            s_e2 = sop(lambda e: e.activation(G1[0:1, :], PGA[32:33, :], AF.Copy,
                                                              scale=1.0 / D))
                            s_m2 = sop(lambda e: e.activation(G0[0:1, :], rowbuf[0:1, :], AF.Square))
                            v_var = vop(lambda e: e.tensor_sub(G1[0:1, :], G1[0:1, :],
                                                               G0[0:1, :]), waits=[s_m2, s_e2])
                            s_ln = sop(lambda e: e.activation(G0[0:1, :], G1[0:1, :], AF.Ln,
                                                              bias=eps_ln[:]), waits=[v_var])
                            s_rstd = sop(lambda e: e.activation(G1[0:1, :], G0[0:1, :], AF.Exp,
                                                                scale=-0.5))
                            tmb = []
                            for tc in range(4):
                                tmb.append(mm(PGA[:, tc * 512:(tc + 1) * 512], ones_r[0:1, 0:128],
                                              rowbuf[0:1, tc * 512:(tc + 1) * 512], True, True,
                                              waits=[s_e2, s_mu]))
                            trb = []
                            for tc in range(4):
                                trb.append(mm(PGB[:, tc * 512:(tc + 1) * 512], ones_r[0:1, 0:128],
                                              G1[0:1, tc * 512:(tc + 1) * 512], True, True,
                                              waits=[s_rstd, pgf[1]]))
                            out_done = []
                            for fc in range(NF):
                                t1 = vop(lambda e, c=fc: e.tensor_sub(G0[:], XZ[c][:], PGA[:]),
                                         waits=tmb + [tok_sq3, s_rstd])
                                t2 = vop(lambda e, c=fc: e.tensor_mul(XZ[c][:], G0[:], PGB[:]),
                                         waits=trb)
                                out_done.append(t2)

                            otile = [G0, G1]
                            prev_ev = [out_done[-1], out_done[-1]]
                            st_tok = [None, None]
                            for i in range(NT):
                                hh = i % 2
                                ps = PGA[:, 0:1024] if hh == 0 else PGB[:, 0:1024]
                                tmm = None
                                for fc in range(NF):
                                    tmm = mm(ps[:, fc * 128:(fc + 1) * 128],
                                             XZ[fc][:, i * 128:(i + 1) * 128], ident_f[:],
                                             True, True,
                                             waits=[out_done[fc], tok_c["ident_f32"],
                                                    prev_ev[hh]])
                                tev = vop(lambda e, ps=ps, gb=otile[hh]:
                                          e.tensor_copy(gb[:, 0:1024], ps),
                                          waits=[tmm, st_tok[hh]])
                                p.stream(f"sto{hh}")
                                st_tok[hh] = p.emit("sync",
                                       lambda e, i=i, gb=otile[hh]:
                                       e.dma_start(out=OUT[i * 128:(i + 1) * 128, :],
                                                   in_=gb[:, 0:1024]),
                                       waits=[tev], inc=f"sto{hh}", n=16)
                                prev_ev[hh] = tev

                        p.barrier()


        p.emit("sync", None,
               waits=[("STX", p.counts["STX"])] +
                     [(s, p.counts[s]) for s in ("sto0", "sto1") if s in p.counts])
        if "out_dummy" not in dbgset and LVL < 6:
            zs = rowbuf[0:128, 0:D]
            tz = vop(lambda e: e.memset(zs, 0.0))
            for i in range(NT):
                p.emit("sync", lambda e, o=OUT[i * 128:(i + 1) * 128, :], i2=zs:
                       e.dma_start(out=o, in_=i2), waits=[tz], inc="STX", n=16)
            p.emit("sync", None, waits=[("STX", p.counts["STX"])])

        block = es.enter_context(nc.Block())
        p.install(block)

    return nc


def kernel(**inputs):
    consts = host_constants()
    w = prep_weights(inputs)
    nc = build_nc(upto="all")
    x = np.asarray(inputs["L0_attn_x"])
    in_maps = []
    for core in range(8):
        im = {"x": np.ascontiguousarray(x[core])}
        im.update({k: np.asarray(v) for k, v in w.items()})
        im.update(consts)
        in_maps.append(im)
    res = run_bass_kernel_spmd(nc, in_maps, core_ids=list(range(8)))
    out = np.stack([r["out"] for r in res.results], axis=0)
    return out.astype(np.float32)
